# version 04
# baseline (speedup 1.0000x reference)
import math
import numpy as np
import ml_dtypes
import concourse.bass as bass
import concourse.mybir as mybir
from concourse.bass_utils import run_bass_kernel_spmd

F32 = mybir.dt.float32
BF16 = mybir.dt.bfloat16
F16 = mybir.dt.float16
I32 = mybir.dt.int32
AF = mybir.ActivationFunctionType
ALU = mybir.AluOpType
AX = mybir.AxisListType

NCORES = 8
B = 4
S = 2048
D = 4096
PC = S // NCORES
TC = B * PC
NCH = D // 128
EPS = 1e-5
ISQ = 1.0 / math.sqrt(128.0)

bf16 = ml_dtypes.bfloat16


from contextlib import ExitStack


class _Scope:

    def __init__(self, nc):
        self.nc = nc
        self.st = ExitStack()

    def sb(self, name, shape, dt):
        return self.st.enter_context(self.nc.sbuf_tensor(name, shape, dt))

    def ps(self, name, shape, dt=F32):
        return self.st.enter_context(self.nc.psum_tensor(name, shape, dt))

    def close(self):
        self.st.close()


class VSem:

    __slots__ = ("h", "base", "emitted")

    def __init__(self, h):
        self.h = h
        self.base = 0
        self.emitted = 0


class VSemPool:
    def __init__(self, nc):
        self.nc = nc
        self.free = []
        self.inuse = []

    def get(self, name, swdge=False):
        if swdge:
            return VSem(self.nc.alloc_semaphore(name))
        if self.free:
            v = self.free.pop()
        else:
            v = VSem(self.nc.alloc_semaphore(name))
        self.inuse.append(v)
        return v

    def retire_phase(self):
        for v in self.inuse:
            v.base += v.emitted
            v.emitted = 0
        self.free.extend(self.inuse)
        self.inuse = []


def _vs(nc, name, swdge=False):
    if not hasattr(nc, "_vsem_pool"):
        nc._vsem_pool = VSemPool(nc)
    return nc._vsem_pool.get(name, swdge)


def _ti(S, n):
    S.emitted += n
    return (S.h, n)


def _wge(e, S, v):
    e.wait_ge(S.h, S.base + v)


def _gbar_setup(nc):
    if not hasattr(nc, "_gbar"):
        nc._gbar = [nc.alloc_semaphore(f"GBAR{i}") for i in range(5)]
        nc._gbar_n = 0
        nc._gbar_dum = nc.alloc_sbuf_tensor("gbar_dum", [1, 8], F32)
        nc._gbar_dmb = nc.alloc_sbuf_tensor("gbar_dmb", [1, 8], BF16)


def emit_barrier(nc, blk):
    if not hasattr(nc, "_gbar"):
        nc._gbar = nc.alloc_semaphore("GBAR")
        nc._gbar_n = 0
        nc._gbar_dum = nc.alloc_sbuf_tensor("gbar_dum", [1, 8], F32)
        nc._gbar_dmb = nc.alloc_sbuf_tensor("gbar_dmb", [1, 8], BF16)
    _gbar_setup(nc)
    first = nc._gbar_n == 0
    nc._gbar_n += 1
    n = nc._gbar_n
    GV, GA, GP, GT, GS_ = nc._gbar
    DUM = nc._gbar_dum

    def wait_all(e):
        for g_ in nc._gbar:
            e.wait_ge(g_, n)

    @blk.vector
    def _(e):
        if first:
            e.memset(nc._gbar_dmb[:], 1.0)
        e.memset(DUM[:, 0:2], 0.0).then_inc(GV, 1)
        wait_all(e)

    @blk.scalar
    def _(e):
        e.memzero(DUM[:, 2:4]).then_inc(GA, 1)
        wait_all(e)

    @blk.gpsimd
    def _(e):
        e.memset(DUM[:, 4:6], 0.0).then_inc(GP, 1)
        wait_all(e)

    with nc.psum_tensor(f"gbar_dps{n}", [1, 8], F32) as dps:
        @blk.tensor
        def _(e):
            e.wait_ge(GV, n)
            e.wait_ge(GA, n)
            e.wait_ge(GP, n)
            e.matmul(dps[0:1, 0:1], nc._gbar_dmb[0:1, 0:1],
                     nc._gbar_dmb[0:1, 0:1], start=True,
                     stop=True).then_inc(GT, 1)
            wait_all(e)

    @blk.sync
    def _(e):
        e.sem_inc(GS_, 1)
        wait_all(e)



def _ident(nc, blk, sem, sc, tag=""):
    iot = sc.sb(f"{tag}identiota", [128, 128], I32)
    idt = sc.sb(f"{tag}identity", [128, 128], BF16)

    @blk.gpsimd
    def _(g):
        g.iota(iot[:], pattern=[[1, 128]], base=0,
               channel_multiplier=-1).then_inc(*_ti(sem, 1))
        _wge(g, sem, 1)
        g.tensor_scalar(out=idt[:], in0=iot[:], scalar1=0, scalar2=None,
                        op0=ALU.is_equal).then_inc(*_ti(sem, 1))

    return idt


def _load_bias_pp(nc, blk, sem, hbm_vec, n_chunks, name):
    t = nc.alloc_sbuf_tensor(name, [128, n_chunks], F32)

    @blk.sync
    def _(e):
        for c in range(n_chunks):
            src = hbm_vec[c * 128:(c + 1) * 128]
            e.dma_start(out=t[:, c:c + 1], in_=src[:, None]).then_inc(sem, 16)

    return t


def emit_ln(nc, blk, x_hbm, xnT_hbm, pi_order, tag, ntok=TC):
    sc = _Scope(nc)
    LS = [_vs(nc, f"{tag}_LS{i}") for i in range(2)]
    VA = _vs(nc, f"{tag}_VA")
    SA = _vs(nc, f"{tag}_SA")
    TS = _vs(nc, f"{tag}_TS")
    ES = _vs(nc, f"{tag}_ES")
    SS = [_vs(nc, f"{tag}_SS{i}", swdge=True) for i in range(2)]
    IDS = _vs(nc, f"{tag}_IDS")
    VC = _vs(nc, f"{tag}_VC")
    AC = _vs(nc, f"{tag}_AC")

    idt = _ident(nc, blk, IDS, sc, tag)

    XT = [sc.sb(f"{tag}_xt{i}", [128, D], F32) for i in range(2)]
    SQ = sc.sb(f"{tag}_sq", [128, D], F32)
    XN = [sc.sb(f"{tag}_xn{i}", [128, D], BF16) for i in range(2)]
    SUM = sc.sb(f"{tag}_sum", [128, 1], F32)
    SSUM = sc.sb(f"{tag}_ssum", [128, 1], F32)
    M = sc.sb(f"{tag}_m", [128, 1], F32)
    MS = sc.sb(f"{tag}_ms", [128, 1], F32)
    VAR = sc.sb(f"{tag}_var", [128, 1], F32)
    VAR2 = sc.sb(f"{tag}_var2", [128, 1], F32)
    LV = sc.sb(f"{tag}_lv", [128, 1], F32)
    RSTD = sc.sb(f"{tag}_rstd", [128, 1], F32)
    EPST = sc.sb(f"{tag}_epst", [128, 1], F32)
    PT = [sc.ps(f"{tag}_pt{i}", [128, 1024], BF16) for i in range(4)]
    STG = [sc.sb(f"{tag}_stg{i}", [128, 1024], BF16) for i in range(2)]

    NT = ntok // 128
    NG = 4

    @blk.sync
    def _(e):
        for tt in range(NT):
            if tt >= 2:
                _wge(e, VA, 3 * (tt - 2) + 3)
            e.dma_start(out=XT[tt % 2][:],
                        in_=x_hbm[tt * 128:(tt + 1) * 128, :]).then_inc(*_ti(LS[tt % 2], 16))

    @blk.gpsimd
    def _(e):
        total = NT * NG * 16
        for tt in range(NT):
            for g in range(NG):
                k = tt * NG + g
                _wge(e, ES, k + 1)
                dst = xnT_hbm[g * 8:(g + 1) * 8, :,
                              tt * 128:(tt + 1) * 128].transpose([1, 0, 2])
                e.dma_start(
                    out=dst,
                    in_=STG[k % 2][:].rearrange("p (c t) -> p c t", c=8),
                ).then_inc(*_ti(SS[k % 2], 16))
        _wge(e, SS[0], total // 2)
        _wge(e, SS[1], total // 2)

    @blk.scalar
    def _(e):
        for tt in range(NT):
            _wge(e, LS[tt % 2], 16 * (tt // 2 + 1))
            e.activation(out=SQ[:], in_=XT[tt % 2][:], func=AF.Square,
                         accum_out=SSUM[:]).then_inc(*_ti(SA, 1))
            _wge(e, VA, 3 * tt + 2)
            e.activation(out=LV[:], in_=VAR2[:], func=AF.Ln,
                         bias=EPST[:]).then_inc(*_ti(AC, 1))
            _wge(e, AC, tt + 1)
            e.activation(out=RSTD[:], in_=LV[:], func=AF.Exp,
                         scale=-0.5).then_inc(*_ti(SA, 1))
            if tt > 0:
                for g in range(NG):
                    k = (tt - 1) * NG + g
                    _wge(e, TS, k + 1)
                    if k >= 2:
                        _wge(e, SS[k % 2], 16 * (k // 2))
                    e.copy(out=STG[k % 2][:], in_=PT[k % 4][:]).then_inc(*_ti(ES, 1))
        for g in range(NG):
            k = (NT - 1) * NG + g
            _wge(e, TS, k + 1)
            _wge(e, SS[k % 2], 16 * (k // 2))
            e.copy(out=STG[k % 2][:], in_=PT[k % 4][:]).then_inc(*_ti(ES, 1))

    @blk.vector
    def _(e):
        e.memset(EPST[:], float(EPS))
        for tt in range(NT):
            _wge(e, LS[tt % 2], 16 * (tt // 2 + 1))
            e.reduce_sum(out=SUM[:], in_=XT[tt % 2][:], axis=AX.X).then_inc(*_ti(VA, 1))
            _wge(e, VA, 3 * tt + 1)
            _wge(e, SA, 2 * tt + 1)
            e.tensor_scalar(out=M[:], in0=SUM[:], scalar1=1.0 / D, scalar2=None,
                            op0=ALU.mult)
            e.tensor_scalar(out=MS[:], in0=SSUM[:], scalar1=1.0 / D, scalar2=None,
                            op0=ALU.mult).then_inc(*_ti(VC, 1))
            _wge(e, VC, 3 * tt + 1)
            e.tensor_tensor(out=VAR[:], in0=M[:], in1=M[:],
                            op=ALU.mult).then_inc(*_ti(VC, 1))
            _wge(e, VC, 3 * tt + 2)
            e.tensor_sub(VAR2[:], MS[:], VAR[:]).then_inc(*_ti(VA, 1))
            e.sem_inc(*_ti(VC, 1))
            _wge(e, SA, 2 * tt + 2)
            if tt >= 2:
                _wge(e, TS, NG * (tt - 1))
            e.tensor_scalar(out=XN[tt % 2][:], in0=XT[tt % 2][:], scalar1=M[:],
                            scalar2=RSTD[:], op0=ALU.subtract,
                            op1=ALU.mult).then_inc(*_ti(VA, 1))

    @blk.tensor
    def _(e):
        _wge(e, IDS, 2)
        for tt in range(NT):
            _wge(e, VA, 3 * tt + 3)
            for g in range(NG):
                k = tt * NG + g
                if k >= 4:
                    _wge(e, ES, k - 3)
                for cc in range(8):
                    c = g * 8 + cc
                    if pi_order:
                        Tb, j = c // 8, c % 8
                        src = XN[tt % 2][:].rearrange("p (T m j) -> p T m j",
                                                      T=4, m=128, j=8)[:, Tb, :, j]
                    else:
                        src = XN[tt % 2][:, c * 128:(c + 1) * 128]
                    ins = e.transpose(PT[k % 4][:, cc * 128:(cc + 1) * 128],
                                      src, idt[:])
                ins.then_inc(*_ti(TS, 1))
    sc.close()



def build_ln(pi_order, tag="ln"):
    nc = bass.Bass("TRN2", target_bir_lowering=False, debug=False)
    x = nc.dram_tensor("x", [TC, D], F32, kind="ExternalInput").ap()
    xnT = nc.dram_tensor("xnT", [NCH, 128, TC], BF16, kind="ExternalOutput").ap()
    with nc.Block() as blk:
        emit_ln(nc, blk, x, xnT, pi_order, tag)
    return nc


def emit_qkv0(nc, blk, xnT_hbm, w_hbm, b_hbm, q_hbm, k_hbm, vtm_hbm, tag,
              q_only=False):
    sc = _Scope(nc)
    XL = _vs(nc, f"{tag}_XL")
    BS = _vs(nc, f"{tag}_BS")
    IDS = _vs(nc, f"{tag}_IDS")
    LW = [_vs(nc, f"{tag}_LW{i}") for i in range(2)]
    CV = _vs(nc, f"{tag}_CV")
    MM = _vs(nc, f"{tag}_MM")
    MME = _vs(nc, f"{tag}_MME")
    EVB = _vs(nc, f"{tag}_EVB")
    SQK = [_vs(nc, f"{tag}_SQK{i}", swdge=True) for i in range(2)]
    VEV = _vs(nc, f"{tag}_VEV")
    TRV = _vs(nc, f"{tag}_TRV")
    EV2 = _vs(nc, f"{tag}_EV2")
    VS = [_vs(nc, f"{tag}_VS{i}", swdge=True) for i in range(2)]

    idt = _ident(nc, blk, IDS, sc, tag)

    XNT = sc.sb(f"{tag}_xnt", [128, NCH, TC], BF16)
    BIAS = sc.sb(f"{tag}_bias", [128, 96], F32)
    WF = [sc.sb(f"{tag}_wf{i}", [128, 512], F32) for i in range(2)]
    WB = [sc.sb(f"{tag}_wb{i}", [128, 512], BF16) for i in range(2)]
    STQ = [sc.sb(f"{tag}_stq{i}", [128, 512], BF16) for i in range(2)]
    VSTG = [sc.sb(f"{tag}_vstg{i}", [128, 512], BF16) for i in range(2)]
    VST2 = [sc.sb(f"{tag}_vst2{i}", [128, 4, 128], BF16)
            for i in range(2)]
    PS = [sc.ps(f"{tag}_ps{i}", [128, 512], F32) for i in range(8)]
    PTV = [PS[6][:].bitcast(BF16), PS[7][:].bitcast(BF16)]

    if q_only:
        groups = [list(range(g * 4, g * 4 + 4)) for g in range(8)]
    else:
        groups = [list(range(g * 4, g * 4 + 4)) for g in range(16)]
        vstart = [64 + 3 * g for g in range(10)] + [94]
        groups += [list(range(s, min(s + 3, 96))) for s in vstart]

    def is_v(goc):
        return goc >= 64

    nbank = [len(g) * 2 for g in groups]
    kctr = 0

    @blk.sync
    def _(e):
        e.dma_start(out=XNT[:], in_=xnT_hbm.transpose([1, 0, 2])).then_inc(*_ti(XL, 16))
        for c in range(96):
            e.dma_start(out=BIAS[:, c:c + 1],
                        in_=b_hbm[c * 128:(c + 1) * 128][:, None]).then_inc(*_ti(BS, 16))
        k = 0
        for gi, g in enumerate(groups):
            c0, c1 = g[0] * 128, (g[-1] + 1) * 128
            for ic in range(NCH):
                if k >= 2:
                    _wge(e, CV, k - 1)
                e.dma_start(out=WF[k % 2][:, 0:(c1 - c0)],
                            in_=w_hbm[ic * 128:(ic + 1) * 128, c0:c1]
                            ).then_inc(*_ti(LW[k % 2], 16))
                k += 1

    @blk.scalar
    def _(e):
        k = 0
        ev2 = 0
        for gi, g in enumerate(groups):
            w = len(g) * 128
            for ic in range(NCH):
                _wge(e, LW[k % 2], 16 * (k // 2 + 1))
                if k >= 2:
                    _wge(e, MM, k - 1)
                e.copy(out=WB[k % 2][:, 0:w], in_=WF[k % 2][:, 0:w]).then_inc(*_ti(CV, 1))
                k += 1
            if is_v(g[0]):
                for idx in range(len(g) * 2):
                    _wge(e, TRV, ev2 + 1)
                    vi = ev2
                    _wge(e, VS[vi % 2], 16 * (vi // 2))
                    e.copy(out=VST2[vi % 2][:],
                           in_=PTV[vi % 2][:, 0:512]).then_inc(*_ti(EV2, 1))
                    ev2 += 1

    @blk.tensor
    def _(e):
        _wge(e, XL, 16)
        _wge(e, IDS, 2)
        k = 0
        qk_done = 0
        v_done = 0
        trv = 0
        for gi, g in enumerate(groups):
            _wge(e, EVB, qk_done)
            _wge(e, VEV, v_done)
            for ic in range(NCH):
                _wge(e, CV, k + 1)
                for j, goc in enumerate(g):
                    for tt in range(2):
                        ins = e.matmul(
                            PS[j * 2 + tt][:],
                            WB[k % 2][:, j * 128:(j + 1) * 128],
                            XNT[:, ic, tt * 512:(tt + 1) * 512],
                            start=(ic == 0), stop=(ic == NCH - 1))
                ins.then_inc(*_ti(MM, 1))
                k += 1
            for goc in g:
                if is_v(goc):
                    v_done += 2
                else:
                    qk_done += 2
            if is_v(g[0]):
                for idx in range(len(g) * 2):
                    vi = trv
                    _wge(e, VEV, vi + 1)
                    if vi >= 2:
                        _wge(e, EV2, vi - 1)
                    for j4 in range(4):
                        ins = e.transpose(
                            PTV[vi % 2][:, j4 * 128:(j4 + 1) * 128],
                            VSTG[vi % 2][:, j4 * 128:(j4 + 1) * 128], idt[:])
                    ins.then_inc(*_ti(TRV, 1))
                    trv += 1

    @blk.vector
    def _(e):
        _wge(e, BS, 16 * 96)
        sq = 0
        vev = 0
        for gi, g in enumerate(groups):
            _wge(e, MM, NCH * (gi + 1))
            for j, goc in enumerate(g):
                for tt in range(2):
                    if not is_v(goc):
                        _wge(e, SQK[sq % 2], 16 * (sq // 2))
                        if goc < 32:
                            e.tensor_scalar(
                                out=STQ[sq % 2][:], in0=PS[j * 2 + tt][:],
                                scalar1=BIAS[:, goc:goc + 1], scalar2=float(ISQ),
                                op0=ALU.add, op1=ALU.mult).then_inc(*_ti(EVB, 1))
                        else:
                            e.tensor_scalar(
                                out=STQ[sq % 2][:], in0=PS[j * 2 + tt][:],
                                scalar1=BIAS[:, goc:goc + 1], scalar2=None,
                                op0=ALU.add).then_inc(*_ti(EVB, 1))
                        sq += 1
                    else:
                        _wge(e, TRV, max(0, vev - 1))
                        e.tensor_scalar(
                            out=VSTG[vev % 2][:], in0=PS[j * 2 + tt][:],
                            scalar1=BIAS[:, goc:goc + 1], scalar2=None,
                            op0=ALU.add).then_inc(*_ti(VEV, 1))
                        vev += 1

    @blk.gpsimd
    def _(e):
        sq = 0
        ev2 = 0
        for gi, g in enumerate(groups):
            for j, goc in enumerate(g):
                for tt in range(2):
                    if not is_v(goc):
                        _wge(e, EVB, sq + 1)
                        dst = (q_hbm if goc < 32 else k_hbm)
                        c = goc if goc < 32 else goc - 32
                        e.dma_start(
                            out=dst[c, :, tt * 512:(tt + 1) * 512],
                            in_=STQ[sq % 2][:]).then_inc(*_ti(SQK[sq % 2], 16))
                        sq += 1
                    else:
                        _wge(e, EV2, ev2 + 1)
                        c = goc - 64
                        dst = vtm_hbm[tt * 512:(tt + 1) * 512,
                                      c * 128:(c + 1) * 128]
                        e.dma_start(
                            out=dst.rearrange("(j p) d -> p j d", j=4),
                            in_=VST2[ev2 % 2][:]).then_inc(*_ti(VS[ev2 % 2], 16))
                        ev2 += 1
        _wge(e, SQK[0], 16 * ((sq + 1) // 2))
        _wge(e, SQK[1], 16 * (sq // 2))
        _wge(e, VS[0], 16 * ((ev2 + 1) // 2))
        _wge(e, VS[1], 16 * (ev2 // 2))
    sc.close()



def build_qkv0(tag="qkv0"):
    nc = bass.Bass("TRN2", target_bir_lowering=False, debug=False)
    xnT = nc.dram_tensor("xnT", [NCH, 128, TC], BF16, kind="ExternalInput").ap()
    w = nc.dram_tensor("w", [D, 3 * D], F32, kind="ExternalInput").ap()
    b = nc.dram_tensor("b", [3 * D], F32, kind="ExternalInput").ap()
    q = nc.dram_tensor("q", [NCH, 128, TC], BF16, kind="ExternalOutput").ap()
    k = nc.dram_tensor("k", [NCH, 128, TC], BF16, kind="ExternalOutput").ap()
    vtm = nc.dram_tensor("vtm", [TC, D], BF16, kind="ExternalOutput").ap()
    with nc.Block() as blk:
        emit_qkv0(nc, blk, xnT, w, b, q, k, vtm, tag)
    return nc


def emit_ffn0(nc, blk, xnT_hbm, gw_hbm, ics_hbm, ocs_hbm, w_hbm, bias_hbm,
              g_hbm, tag):
    sc = _Scope(nc)
    XL = _vs(nc, f"{tag}_XL")
    GL = _vs(nc, f"{tag}_GL")
    BS = _vs(nc, f"{tag}_BS")
    IDS = _vs(nc, f"{tag}_IDS")
    CVS = _vs(nc, f"{tag}_CVS")
    RTR = _vs(nc, f"{tag}_RTR")
    VRT = _vs(nc, f"{tag}_VRT")
    ART = _vs(nc, f"{tag}_ART")
    USM = _vs(nc, f"{tag}_USM")
    UEV = _vs(nc, f"{tag}_UEV")
    LW2 = [_vs(nc, f"{tag}_LW2{i}") for i in range(2)]
    SGN = _vs(nc, f"{tag}_SGN")
    TRW = _vs(nc, f"{tag}_TRW")
    PWEV = _vs(nc, f"{tag}_PWEV")
    MM2 = _vs(nc, f"{tag}_MM2")
    POEV = _vs(nc, f"{tag}_POEV")
    YEV = _vs(nc, f"{tag}_YEV")
    SIL = _vs(nc, f"{tag}_SIL")
    ST2 = [_vs(nc, f"{tag}_ST2{i}", swdge=True) for i in range(2)]

    idt = _ident(nc, blk, IDS, sc, tag)
    iotf = sc.sb(f"{tag}_iotf", [128, 128], I32)
    idf = sc.sb(f"{tag}_idf", [128, 128], F32)

    @blk.gpsimd
    def _(g):
        g.iota(iotf[:], pattern=[[1, 128]], base=0,
               channel_multiplier=-1).then_inc(*_ti(IDS, 1))
        _wge(g, IDS, 3)
        g.tensor_scalar(out=idf[:], in0=iotf[:], scalar1=0, scalar2=None,
                        op0=ALU.is_equal).then_inc(*_ti(IDS, 1))

    OCSB = sc.sb(f"{tag}_ocsb", [8, D], BF16)
    GWT = sc.sb(f"{tag}_gwt", [128, NCH, 8], BF16)
    BIAS = sc.sb(f"{tag}_bias", [128, NCH], F32)
    RLS = sc.sb(f"{tag}_rls", [8, TC], F32)
    NMX = sc.sb(f"{tag}_nmx", [128, 1], F32)
    ZT = sc.sb(f"{tag}_zt", [128, 1], F32)
    RZ = sc.sb(f"{tag}_rz", [128, 1], F32)
    EXPT = sc.sb(f"{tag}_expt", [128, 8], F32)
    RTM = sc.sb(f"{tag}_rtm", [128, 8], BF16)
    RTT = sc.sb(f"{tag}_rtt", [8, TC], BF16)
    U = sc.sb(f"{tag}_u", [128, NCH, TC], BF16)

    NT = TC // 128

    with (
        nc.psum_tensor(f"{tag}_rl", [8, TC], F32) as RL,
        nc.psum_tensor(f"{tag}_pr", [128, NCH * 8], BF16) as PR,
        nc.psum_tensor(f"{tag}_psm", [128, 8], F32) as PSM,
        nc.psum_tensor(f"{tag}_prb", [8, TC], BF16) as PRB,
        nc.psum_tensor(f"{tag}_pu", [128, 512], F32) as PU,
        nc.sbuf_tensor(f"{tag}_xnt", [128, NCH, TC], BF16) as XNT_h,
        nc.sbuf_tensor(f"{tag}_scf", [8, D], F32) as SCF_h,
        nc.sbuf_tensor(f"{tag}_gwb", [8, D], BF16) as GWB_h,
        nc.sbuf_tensor(f"{tag}_icsb", [8, D], BF16) as ICSB_h,
    ):
        XNT, SCF, GWB, ICSB = XNT_h, SCF_h, GWB_h, ICSB_h

        @blk.sync
        def _(e):
            e.dma_start(out=XNT[:], in_=xnT_hbm.transpose([1, 0, 2])).then_inc(*_ti(XL, 16))
            e.dma_start(out=SCF[:], in_=gw_hbm).then_inc(*_ti(GL, 16))
            _wge(e, CVS, 1)
            e.dma_start(out=SCF[:], in_=ics_hbm).then_inc(*_ti(GL, 16))
            _wge(e, CVS, 2)
            e.dma_start(out=SCF[:], in_=ocs_hbm).then_inc(*_ti(GL, 16))
            for c in range(NCH):
                e.dma_start(out=BIAS[:, c:c + 1],
                            in_=bias_hbm[c * 128:(c + 1) * 128][:, None]
                            ).then_inc(*_ti(BS, 16))

        @blk.scalar
        def _(e):
            _wge(e, GL, 16)
            e.copy(out=GWB[:], in_=SCF[:]).then_inc(*_ti(CVS, 1))
            _wge(e, GL, 32)
            e.copy(out=ICSB[:], in_=SCF[:]).then_inc(*_ti(CVS, 1))
            _wge(e, GL, 48)
            e.copy(out=OCSB[:], in_=SCF[:]).then_inc(*_ti(CVS, 1))
            for tb in range(NT):
                _wge(e, VRT, 3 * tb + 1)
                e.activation(out=EXPT[:], in_=PSM[:], func=AF.Exp,
                             bias=NMX[:], accum_out=ZT[:]).then_inc(*_ti(ART, 1))

        @blk.tensor
        def _(e):
            _wge(e, IDS, 4)
            _wge(e, CVS, 1)
            for c in range(NCH):
                ins = e.transpose(PR[:, c * 8:(c + 1) * 8],
                                  GWB[:, c * 128:(c + 1) * 128], idt[0:8, 0:8])
            ins.then_inc(*_ti(RTR, 1))
            _wge(e, RTR, 2)
            _wge(e, XL, 16)
            for c in range(NCH):
                for hh in range(2):
                    ins = e.matmul(RL[:, hh * 512:(hh + 1) * 512],
                                   GWT[:, c, :],
                                   XNT[:, c, hh * 512:(hh + 1) * 512],
                                   start=(c == 0), stop=(c == NCH - 1))
            ins.then_inc(*_ti(RTR, 1))
            for tb in range(NT):
                _wge(e, RTR, 4 + 2 * tb)
                if tb >= 1:
                    _wge(e, ART, tb)
                e.transpose(PSM[:], RLS[:, tb * 128:(tb + 1) * 128],
                            idf[0:8, 0:8]).then_inc(*_ti(RTR, 1))
                _wge(e, VRT, 3 * tb + 3)
                e.transpose(PRB[:, tb * 128:(tb + 1) * 128], RTM[:],
                            idt[:]).then_inc(*_ti(RTR, 1))
            _wge(e, VRT, 3 * NT + 1)
            for oc in range(NCH):
                for tt in range(2):
                    k = 2 * oc + tt
                    if k >= 1:
                        _wge(e, UEV, k)
                    e.matmul(PU[:], ICSB[:, oc * 128:(oc + 1) * 128],
                             RTT[:, tt * 512:(tt + 1) * 512],
                             start=True, stop=True).then_inc(*_ti(USM, 1))

        @blk.vector
        def _(e):
            _wge(e, RTR, 1)
            e.tensor_copy(out=GWT[:], in_=PR[:].rearrange(
                "p (c e) -> p c e", e=8)).then_inc(*_ti(RTR, 1))
            _wge(e, RTR, 3)
            e.tensor_copy(out=RLS[:], in_=RL[:]).then_inc(*_ti(RTR, 1))
            for tb in range(NT):
                _wge(e, RTR, 5 + 2 * tb)
                if tb >= 1:
                    _wge(e, ART, tb)
                e.reduce_max(out=NMX[:], in_=PSM[:], axis=AX.X,
                             negate=True).then_inc(*_ti(VRT, 1))
                _wge(e, ART, tb + 1)
                e.reciprocal(out=RZ[:], in_=ZT[:]).then_inc(*_ti(VRT, 1))
                _wge(e, VRT, 3 * tb + 2)
                if tb >= 1:
                    _wge(e, RTR, 4 + 2 * tb)
                e.tensor_scalar(out=RTM[:], in0=EXPT[:], scalar1=RZ[:],
                                scalar2=None, op0=ALU.mult).then_inc(*_ti(VRT, 1))
            _wge(e, RTR, 4 + 2 * NT)
            e.tensor_copy(out=RTT[:], in_=PRB[:]).then_inc(*_ti(VRT, 1))
            for oc in range(NCH):
                for tt in range(2):
                    k = 2 * oc + tt
                    _wge(e, USM, k + 1)
                    e.tensor_mul(U[:, oc, tt * 512:(tt + 1) * 512],
                                 XNT[:, oc, tt * 512:(tt + 1) * 512],
                                 PU[:]).then_inc(*_ti(UEV, 1))

    WF2 = sc.sb(f"{tag}_wf2", [128, D], F32)
    SW = sc.sb(f"{tag}_sw", [128, D], BF16)
    TWB = [sc.sb(f"{tag}_twb{i}", [128, NCH, 128], BF16)
           for i in range(2)]
    OSS = [sc.sb(f"{tag}_oss{i}", [128, 512], F32) for i in range(2)]
    YT = [sc.sb(f"{tag}_yt{i}", [128, 512], F32) for i in range(2)]
    GST = [sc.sb(f"{tag}_gst{i}", [128, 512], BF16) for i in range(2)]

    with (
        nc.psum_tensor(f"{tag}_pw0", [128, 1024], BF16) as PW0,
        nc.psum_tensor(f"{tag}_pw1", [128, 1024], BF16) as PW1,
        nc.psum_tensor(f"{tag}_py0", [128, 512], F32) as PY0,
        nc.psum_tensor(f"{tag}_py1", [128, 512], F32) as PY1,
        nc.psum_tensor(f"{tag}_py2", [128, 512], F32) as PY2,
        nc.psum_tensor(f"{tag}_py3", [128, 512], F32) as PY3,
        nc.psum_tensor(f"{tag}_po0", [128, 512], F32) as PO0,
        nc.psum_tensor(f"{tag}_po1", [128, 512], F32) as PO1,
    ):
        PW = [PW0, PW1]
        PY = [PY0, PY1, PY2, PY3]
        PO = [PO0, PO1]

        @blk.sync
        def _(e):
            _wge(e, UEV, 64)
            for ob in range(NCH):
                if ob >= 1:
                    _wge(e, SGN, ob)
                e.dma_start(out=WF2[:],
                            in_=w_hbm[ob * 128:(ob + 1) * 128, :]
                            ).then_inc(*_ti(LW2[ob % 2], 16))

        @blk.scalar
        def _(e):
            _wge(e, BS, 16 * NCH)
            for ob in range(NCH):
                _wge(e, LW2[ob % 2], 16 * (ob // 2 + 1))
                if ob >= 1:
                    _wge(e, TRW, 4 * ob)
                e.sign(out=SW[:], in_=WF2[:]).then_inc(*_ti(SGN, 1))
                for tt in range(2):
                    if ob >= 1:
                        kk = 2 * (ob - 1) + tt
                        _wge(e, YEV, kk + 1)
                        _wge(e, ST2[kk % 2], 16 * (kk // 2))
                        e.activation(out=GST[kk % 2][:], in_=YT[kk % 2][:],
                                     func=AF.Silu,
                                     bias=BIAS[:, (ob - 1):ob]).then_inc(*_ti(SIL, 1))
            for tt in range(2):
                kk = 2 * (NCH - 1) + tt
                _wge(e, YEV, kk + 1)
                _wge(e, ST2[kk % 2], 16 * (kk // 2))
                e.activation(out=GST[kk % 2][:], in_=YT[kk % 2][:],
                             func=AF.Silu,
                             bias=BIAS[:, NCH - 1:NCH]).then_inc(*_ti(SIL, 1))

        @blk.tensor
        def _(e):
            for ob in range(NCH):
                _wge(e, SGN, ob + 1)
                for gg in range(4):
                    kg = 4 * ob + gg
                    if kg >= 2:
                        _wge(e, PWEV, kg - 1)
                    for cc in range(8):
                        hc = gg * 8 + cc
                        ins = e.transpose(
                            PW[kg % 2][:, cc * 128:(cc + 1) * 128],
                            SW[:, hc * 128:(hc + 1) * 128], idt[:])
                    ins.then_inc(*_ti(TRW, 1))
                for tt in range(2):
                    kk = 2 * ob + tt
                    if kk >= 2:
                        _wge(e, POEV, kk - 1)
                    e.matmul(PO[kk % 2][:], OCSB[:, ob * 128:(ob + 1) * 128],
                             RTT[:, tt * 512:(tt + 1) * 512],
                             start=True, stop=True).then_inc(*_ti(MM2, 1))
                _wge(e, PWEV, 4 * (ob + 1))
                for tt in range(2):
                    kk = 2 * ob + tt
                    if kk >= 4:
                        _wge(e, YEV, kk - 3)
                    for hc in range(NCH):
                        ins = e.matmul(PY[kk % 4][:],
                                       TWB[ob % 2][:, hc, :],
                                       U[:, hc, tt * 512:(tt + 1) * 512],
                                       start=(hc == 0), stop=(hc == NCH - 1))
                    ins.then_inc(*_ti(MM2, 1))

        @blk.vector
        def _(e):
            for ob in range(NCH):
                for gg in range(4):
                    kg = 4 * ob + gg
                    _wge(e, TRW, kg + 1)
                    if ob >= 2:
                        _wge(e, MM2, 4 * (ob - 1))
                    e.tensor_copy(
                        out=TWB[ob % 2][:, gg * 8:(gg + 1) * 8, :],
                        in_=PW[kg % 2][:].rearrange("p (c w) -> p c w", c=8)
                    ).then_inc(*_ti(PWEV, 1))
                for tt in range(2):
                    kk = 2 * ob + tt
                    _wge(e, MM2, 4 * ob + tt + 1)
                    e.tensor_copy(out=OSS[kk % 2][:],
                                  in_=PO[kk % 2][:]).then_inc(*_ti(POEV, 1))
                for tt in range(2):
                    kk = 2 * ob + tt
                    _wge(e, MM2, 4 * ob + 2 + tt + 1)
                    _wge(e, POEV, kk + 1)
                    if kk >= 2:
                        _wge(e, SIL, kk - 1)
                    e.tensor_mul(YT[kk % 2][:], OSS[kk % 2][:],
                                 PY[kk % 4][:]).then_inc(*_ti(YEV, 1))

        @blk.gpsimd
        def _(e):
            for ob in range(NCH):
                for tt in range(2):
                    kk = 2 * ob + tt
                    _wge(e, SIL, kk + 1)
                    e.dma_start(out=g_hbm[ob, :, tt * 512:(tt + 1) * 512],
                                in_=GST[kk % 2][:]).then_inc(*_ti(ST2[kk % 2], 16))
            _wge(e, ST2[0], 16 * NCH)
            _wge(e, ST2[1], 16 * NCH)
    sc.close()



def build_ffn0(tag="ffn0"):
    nc = bass.Bass("TRN2", target_bir_lowering=False, debug=False)
    xnT = nc.dram_tensor("xnT", [NCH, 128, TC], BF16, kind="ExternalInput").ap()
    gw = nc.dram_tensor("gw", [8, D], F32, kind="ExternalInput").ap()
    ics = nc.dram_tensor("ics", [8, D], F32, kind="ExternalInput").ap()
    ocs = nc.dram_tensor("ocs", [8, D], F32, kind="ExternalInput").ap()
    w = nc.dram_tensor("w", [D, D], F32, kind="ExternalInput").ap()
    bias = nc.dram_tensor("bias", [D], F32, kind="ExternalInput").ap()
    g = nc.dram_tensor("g", [NCH, 128, TC], BF16, kind="ExternalOutput").ap()
    with nc.Block() as blk:
        emit_ffn0(nc, blk, xnT, gw, ics, ocs, w, bias, g, tag)
    return nc


def emit_attn0(nc, blk, q_hbm, kf_hbm, vf_hbm, ctx_hbm, tag):
    sc = _Scope(nc)
    QL = _vs(nc, f"{tag}_QL")
    IDS = _vs(nc, f"{tag}_IDS")
    KL = [_vs(nc, f"{tag}_KL{i}") for i in range(2)]
    VL = [_vs(nc, f"{tag}_VL{i}") for i in range(2)]
    SCM = _vs(nc, f"{tag}_SCM")
    VN = _vs(nc, f"{tag}_VN")
    AE = _vs(nc, f"{tag}_AE")
    TRP = _vs(nc, f"{tag}_TRP")
    PEV = _vs(nc, f"{tag}_PEV")
    PVM = _vs(nc, f"{tag}_PVM")
    RC = _vs(nc, f"{tag}_RC")
    CEV = _vs(nc, f"{tag}_CEV")
    TCX = _vs(nc, f"{tag}_TCX")
    CTE = _vs(nc, f"{tag}_CTE")
    CST = [_vs(nc, f"{tag}_CST{i}", swdge=True) for i in range(2)]

    idt = _ident(nc, blk, IDS, sc, tag)

    QR = sc.sb(f"{tag}_qr", [128, NCH, TC], BF16)
    KT = [sc.sb(f"{tag}_kt{i}", [128, S], BF16) for i in range(2)]
    VT = [sc.sb(f"{tag}_vt{i}", [128, 16, 128], BF16)
          for i in range(2)]
    PB = [sc.sb(f"{tag}_pb{i}", [128, S], BF16) for i in range(2)]
    PTS = sc.sb(f"{tag}_pts", [128, 16, 128], BF16)
    NMX = sc.sb(f"{tag}_nmx", [128, 1], F32)
    Z = [sc.sb(f"{tag}_z{i}", [128, 1], F32) for i in range(2)]
    RZ = [sc.sb(f"{tag}_rz{i}", [128, 1], F32) for i in range(2)]
    CXS = [sc.sb(f"{tag}_cxs{i}", [128, 128], BF16) for i in range(2)]
    CTS = [sc.sb(f"{tag}_cts{i}", [128, 128], BF16) for i in range(2)]

    SC = sc.ps(f"{tag}_sc", [128, S], F32)
    PSB = [sc.ps(f"{tag}_psb{i}", [128, 1024], BF16)
           for i in range(2)]
    PCX = sc.ps(f"{tag}_pcx", [128, 128], F32)
    PCT = sc.ps(f"{tag}_pct", [128, 128], BF16)

    BH = [(b, h) for b in range(B) for h in range(32)]

    @blk.sync
    def _(e):
        e.dma_start(out=QR[:], in_=q_hbm.transpose([1, 0, 2])).then_inc(*_ti(QL, 16))
        for i, (b, h) in enumerate(BH):
            if i >= 2:
                _wge(e, SCM, 2 * (i - 1))
            e.dma_start(out=KT[i % 2][:], in_=kf_hbm[b, h]).then_inc(*_ti(KL[i % 2], 16))
            if i >= 2:
                _wge(e, PVM, 2 * (i - 1))
            e.dma_start(out=VT[i % 2][:],
                        in_=vf_hbm[b, h].rearrange("(kt p) d -> p kt d", p=128)
                        ).then_inc(*_ti(VL[i % 2], 16))

    @blk.tensor
    def _(e):
        _wge(e, QL, 16)
        _wge(e, IDS, 2)
        for i, (b, h) in enumerate(BH):
            _wge(e, KL[i % 2], 16 * (i // 2 + 1))
            _wge(e, VL[i % 2], 16 * (i // 2 + 1))
            for qt in range(2):
                u = 2 * i + qt
                if u >= 1:
                    _wge(e, AE, u)
                qsl = QR[:, h, b * PC + qt * 128: b * PC + (qt + 1) * 128]
                for j in range(4):
                    ins = e.matmul(SC[:, j * 512:(j + 1) * 512], qsl,
                                   KT[i % 2][:, j * 512:(j + 1) * 512],
                                   start=True, stop=True)
                ins.then_inc(*_ti(SCM, 1))
                _wge(e, AE, u + 1)
                _wge(e, PEV, 2 * u)
                for g in range(2):
                    for cc in range(8):
                        kt_i = g * 8 + cc
                        ins = e.transpose(
                            PSB[g][:, cc * 128:(cc + 1) * 128],
                            PB[u % 2][:, kt_i * 128:(kt_i + 1) * 128], idt[:])
                    ins.then_inc(*_ti(TRP, 1))
                _wge(e, PEV, 2 * u + 2)
                if u >= 1:
                    _wge(e, CEV, u)
                for kt_i in range(16):
                    ins = e.matmul(PCX[:], PTS[:, kt_i, :], VT[i % 2][:, kt_i, :],
                                   start=(kt_i == 0), stop=(kt_i == 15))
                ins.then_inc(*_ti(PVM, 1))
                _wge(e, CEV, u + 1)
                if u >= 1:
                    _wge(e, CTE, u)
                e.transpose(PCT[:], CXS[u % 2][:], idt[:]).then_inc(*_ti(TCX, 1))

    @blk.vector
    def _(e):
        for i, (b, h) in enumerate(BH):
            for qt in range(2):
                u = 2 * i + qt
                _wge(e, SCM, u + 1)
                e.reduce_max(out=NMX[:], in_=SC[:], axis=AX.X,
                             negate=True).then_inc(*_ti(VN, 1))
                _wge(e, AE, u + 1)
                e.reciprocal(out=RZ[u % 2][:], in_=Z[u % 2][:]).then_inc(*_ti(RC, 1))
                _wge(e, RC, u + 1)
                _wge(e, PVM, u + 1)
                if u >= 2:
                    _wge(e, TCX, u - 1)
                e.tensor_scalar(out=CXS[u % 2][:], in0=PCX[:],
                                scalar1=RZ[u % 2][:], scalar2=None,
                                op0=ALU.mult).then_inc(*_ti(CEV, 1))

    @blk.scalar
    def _(e):
        for i, (b, h) in enumerate(BH):
            for qt in range(2):
                u = 2 * i + qt
                _wge(e, VN, u + 1)
                if u >= 2:
                    _wge(e, TRP, 2 * (u - 1))
                    _wge(e, RC, u - 1)
                e.activation(out=PB[u % 2][:], in_=SC[:], func=AF.Exp,
                             bias=NMX[:], accum_out=Z[u % 2][:]).then_inc(*_ti(AE, 1))
                for g in range(2):
                    _wge(e, TRP, 2 * u + g + 1)
                    if u >= 1:
                        _wge(e, PVM, u)
                    e.copy(out=PTS[:, g * 8:(g + 1) * 8, :],
                           in_=PSB[g][:].rearrange("p (c w) -> p c w", c=8)
                           ).then_inc(*_ti(PEV, 1))
                _wge(e, TCX, u + 1)
                if u >= 2:
                    _wge(e, CST[u % 2], 16 * (u // 2))
                e.copy(out=CTS[u % 2][:], in_=PCT[:]).then_inc(*_ti(CTE, 1))

    @blk.gpsimd
    def _(e):
        for i, (b, h) in enumerate(BH):
            for qt in range(2):
                u = 2 * i + qt
                _wge(e, CTE, u + 1)
                e.dma_start(out=ctx_hbm[h, :, b * PC + qt * 128:
                                        b * PC + (qt + 1) * 128],
                            in_=CTS[u % 2][:]).then_inc(*_ti(CST[u % 2], 16))
        _wge(e, CST[0], 16 * 128)
        _wge(e, CST[1], 16 * 128)
    sc.close()



def build_attn0(tag="attn0"):
    nc = bass.Bass("TRN2", target_bir_lowering=False, debug=False)
    q = nc.dram_tensor("q", [NCH, 128, TC], BF16, kind="ExternalInput").ap()
    kf = nc.dram_tensor("kf", [B, 32, 128, S], BF16, kind="ExternalInput").ap()
    vf = nc.dram_tensor("vf", [B, 32, S, 128], BF16, kind="ExternalInput").ap()
    ctx = nc.dram_tensor("ctx", [NCH, 128, TC], BF16, kind="ExternalOutput").ap()
    with nc.Block() as blk:
        emit_attn0(nc, blk, q, kf, vf, ctx, tag)
    return nc


def emit_addln(nc, blk, tag, SG, SGR, sg_ready, x_hbm, x1_hbm, xnT_hbm, pi_order,
               idt, IDS, x1_dtype=F32, sg_all=None):
    sc = _Scope(nc)
    LXc = [_vs(nc, f"{tag}_LX{i}") for i in range(2)]
    TS1 = _vs(nc, f"{tag}_TS1")
    AD = _vs(nc, f"{tag}_AD")
    XS = _vs(nc, f"{tag}_XS")
    VA = _vs(nc, f"{tag}_VA")
    SA = _vs(nc, f"{tag}_SA")
    TS = _vs(nc, f"{tag}_TS")
    ES = _vs(nc, f"{tag}_ES")
    SSX = [_vs(nc, f"{tag}_SSX{i}", swdge=True) for i in range(2)]
    SSN = [_vs(nc, f"{tag}_SSN{i}", swdge=True) for i in range(2)]
    VC = _vs(nc, f"{tag}_VC")
    AC = _vs(nc, f"{tag}_AC")

    XTILE = [sc.sb(f"{tag}_xtile{i}", [128, D], F32)
             for i in range(2)]
    X1T = sc.sb(f"{tag}_x1t", [128, D], F32)
    X1STG = [sc.sb(f"{tag}_x1stg{i}", [128, D], x1_dtype)
             for i in range(2)]
    PX = [sc.ps(f"{tag}_px{i}", [128, 512], BF16) for i in range(2)]
    ln = xnT_hbm is not None
    if ln:
        SQ = sc.sb(f"{tag}_sq", [128, D], F32)
        XN = [sc.sb(f"{tag}_xn{i}", [128, D], BF16)
              for i in range(2)]
        SUM = sc.sb(f"{tag}_sum", [128, 1], F32)
        SSUM = sc.sb(f"{tag}_ssum", [128, 1], F32)
        M = sc.sb(f"{tag}_m", [128, 1], F32)
        MS = sc.sb(f"{tag}_ms", [128, 1], F32)
        VAR = sc.sb(f"{tag}_var", [128, 1], F32)
        VAR2 = sc.sb(f"{tag}_var2", [128, 1], F32)
        LV = sc.sb(f"{tag}_lv", [128, 1], F32)
        RSTD = sc.sb(f"{tag}_rstd", [128, 1], F32)
        EPST = sc.sb(f"{tag}_epst", [128, 1], F32)
        PT = [sc.ps(f"{tag}_pt{i}", [128, 1024], BF16)
              for i in range(4)]
        STG = [sc.sb(f"{tag}_stg{i}", [128, 1024], BF16)
               for i in range(2)]

    NT = TC // 128
    NG = 4

    @blk.sync
    def _(e):
        for tt in range(NT):
            if tt >= 2:
                _wge(e, AD, 8 * (tt - 1))
            e.dma_start(out=XTILE[tt % 2][:],
                        in_=x_hbm[tt * 128:(tt + 1) * 128, :]
                        ).then_inc(*_ti(LXc[tt % 2], 16))

    @blk.tensor
    def _(e):
        _wge(e, IDS, 2)
        if sg_all is not None:
            _wge(e, SGR, sg_all)
        for tt in range(NT):
            for gg in range(8):
                kq = 8 * tt + gg
                _wge(e, SGR, sg_ready(gg * 4 + 3))
                if kq >= 2:
                    _wge(e, AD, kq - 1)
                for cc in range(4):
                    oc = gg * 4 + cc
                    ins = e.transpose(PX[kq % 2][:, cc * 128:(cc + 1) * 128],
                                      SG[:, oc, tt * 128:(tt + 1) * 128], idt[:])
                ins.then_inc(*_ti(TS1, 1))
            if ln:
                _wge(e, VA, 2 * tt + 2)
                for g in range(NG):
                    k = tt * NG + g
                    if k >= 4:
                        _wge(e, ES, k - 3)
                    for cc in range(8):
                        c = g * 8 + cc
                        if pi_order:
                            Tb, j = c // 8, c % 8
                            src = XN[tt % 2][:].rearrange(
                                "p (T m j) -> p T m j",
                                T=4, m=128, j=8)[:, Tb, :, j]
                        else:
                            src = XN[tt % 2][:, c * 128:(c + 1) * 128]
                        ins = e.transpose(PT[k % 4][:, cc * 128:(cc + 1) * 128],
                                          src, idt[:])
                    ins.then_inc(*_ti(TS, 1))

    @blk.vector
    def _(e):
        if ln:
            e.memset(EPST[:], float(EPS))
        for tt in range(NT):
            _wge(e, LXc[tt % 2], 16 * (tt // 2 + 1))
            if tt >= 1:
                _wge(e, XS, tt)
            for gg in range(8):
                kq = 8 * tt + gg
                _wge(e, TS1, kq + 1)
                e.tensor_add(X1T[:, gg * 512:(gg + 1) * 512],
                             XTILE[tt % 2][:, gg * 512:(gg + 1) * 512],
                             PX[kq % 2][:]).then_inc(*_ti(AD, 1))
            if ln:
                _wge(e, SA, 2 * tt + 1)
                e.reduce_sum(out=SUM[:], in_=X1T[:], axis=AX.X).then_inc(*_ti(VC, 1))
                _wge(e, VC, 4 * tt + 1)
                e.tensor_scalar(out=M[:], in0=SUM[:], scalar1=1.0 / D,
                                scalar2=None, op0=ALU.mult)
                e.tensor_scalar(out=MS[:], in0=SSUM[:], scalar1=1.0 / D,
                                scalar2=None, op0=ALU.mult).then_inc(*_ti(VC, 1))
                _wge(e, VC, 4 * tt + 2)
                e.tensor_tensor(out=VAR[:], in0=M[:], in1=M[:],
                                op=ALU.mult).then_inc(*_ti(VC, 1))
                _wge(e, VC, 4 * tt + 3)
                e.tensor_sub(VAR2[:], MS[:], VAR[:]).then_inc(*_ti(VA, 1))
                e.sem_inc(*_ti(VC, 1))
                _wge(e, SA, 2 * tt + 2)
                if tt >= 2:
                    _wge(e, TS, NG * (tt - 1))
                e.tensor_scalar(out=XN[tt % 2][:], in0=X1T[:], scalar1=M[:],
                                scalar2=RSTD[:], op0=ALU.subtract,
                                op1=ALU.mult).then_inc(*_ti(VA, 1))

    @blk.scalar
    def _(e):
        for tt in range(NT):
            _wge(e, AD, 8 * (tt + 1))
            if ln:
                e.activation(out=SQ[:], in_=X1T[:], func=AF.Square,
                             accum_out=SSUM[:]).then_inc(*_ti(SA, 1))
                _wge(e, VA, 2 * tt + 1)
                e.activation(out=LV[:], in_=VAR2[:], func=AF.Ln,
                             bias=EPST[:]).then_inc(*_ti(AC, 1))
                _wge(e, AC, tt + 1)
                e.activation(out=RSTD[:], in_=LV[:], func=AF.Exp,
                             scale=-0.5).then_inc(*_ti(SA, 1))
            if tt >= 2:
                _wge(e, SSX[tt % 2], 16 * (tt // 2))
            if ln:
                _wge(e, VA, 2 * tt + 2)
            e.copy(out=X1STG[tt % 2][:], in_=X1T[:]).then_inc(*_ti(XS, 1))
            if ln and tt > 0:
                for g in range(NG):
                    k = (tt - 1) * NG + g
                    _wge(e, TS, k + 1)
                    if k >= 2:
                        _wge(e, SSN[k % 2], 16 * (k // 2))
                    e.copy(out=STG[k % 2][:], in_=PT[k % 4][:]).then_inc(*_ti(ES, 1))
        if ln:
            for g in range(NG):
                k = (NT - 1) * NG + g
                _wge(e, TS, k + 1)
                _wge(e, SSN[k % 2], 16 * (k // 2))
                e.copy(out=STG[k % 2][:], in_=PT[k % 4][:]).then_inc(*_ti(ES, 1))

    @blk.gpsimd
    def _(e):
        for tt in range(NT):
            _wge(e, XS, tt + 1)
            e.dma_start(out=x1_hbm[tt * 128:(tt + 1) * 128, :],
                        in_=X1STG[tt % 2][:]).then_inc(*_ti(SSX[tt % 2], 16))
            if ln:
                for g in range(NG):
                    k = tt * NG + g
                    _wge(e, ES, k + 1)
                    dst = xnT_hbm[g * 8:(g + 1) * 8, :,
                                  tt * 128:(tt + 1) * 128].transpose([1, 0, 2])
                    e.dma_start(
                        out=dst,
                        in_=STG[k % 2][:].rearrange("p (c t) -> p c t", c=8),
                    ).then_inc(*_ti(SSN[k % 2], 16))
        _wge(e, SSX[0], 16 * (NT // 2))
        _wge(e, SSX[1], 16 * (NT // 2))
        if ln:
            _wge(e, SSN[0], 16 * (NT * NG // 2))
            _wge(e, SSN[1], 16 * (NT * NG // 2))


    sc.close()



def emit_combine0(nc, blk, ctx_hbm, ow_hbm, ob_hbm, g_hbm, x_hbm, x1_hbm,
                  xnT_hbm, tag):
    sc = _Scope(nc)
    IDS = _vs(nc, f"{tag}_IDS")
    idt = _ident(nc, blk, IDS, sc, tag)

    CXL = _vs(nc, f"{tag}_CXL")
    OBL = _vs(nc, f"{tag}_OBL")
    GLD = [_vs(nc, f"{tag}_GLD{i}", swdge=True) for i in range(2)]
    LW = [_vs(nc, f"{tag}_LW{i}") for i in range(2)]
    CV = _vs(nc, f"{tag}_CV")
    MM = _vs(nc, f"{tag}_MM")
    SGR = _vs(nc, f"{tag}_SGR")

    SG = sc.sb(f"{tag}_sg", [128, NCH, TC], BF16)
    OB = sc.sb(f"{tag}_ob", [128, NCH], F32)
    GTL = [sc.sb(f"{tag}_gtl{i}", [128, TC], BF16)
           for i in range(2)]

    with (
        nc.sbuf_tensor(f"{tag}_ctx", [128, NCH, TC], BF16) as CTX,
        nc.sbuf_tensor(f"{tag}_wf0", [128, 512], F32) as WFa,
        nc.sbuf_tensor(f"{tag}_wf1", [128, 512], F32) as WFb,
        nc.sbuf_tensor(f"{tag}_wb0", [128, 512], BF16) as WBa,
        nc.sbuf_tensor(f"{tag}_wb1", [128, 512], BF16) as WBb,
        nc.psum_tensor(f"{tag}_ps0", [128, 512], F32) as Pa,
        nc.psum_tensor(f"{tag}_ps1", [128, 512], F32) as Pb,
        nc.psum_tensor(f"{tag}_ps2", [128, 512], F32) as Pc,
        nc.psum_tensor(f"{tag}_ps3", [128, 512], F32) as Pd,
        nc.psum_tensor(f"{tag}_ps4", [128, 512], F32) as Pe,
        nc.psum_tensor(f"{tag}_ps5", [128, 512], F32) as Pf,
    ):
        WF = [WFa, WFb]
        WB = [WBa, WBb]
        PS = [Pa, Pb, Pc, Pd, Pe, Pf]
        NGRP = NCH // 3 + (1 if NCH % 3 else 0)
        groups = [list(range(3 * i, min(3 * i + 3, NCH))) for i in range(NGRP)]

        @blk.sync
        def _(e):
            e.dma_start(out=CTX[:], in_=ctx_hbm.transpose([1, 0, 2])
                        ).then_inc(*_ti(CXL, 16))
            for c in range(NCH):
                e.dma_start(out=OB[:, c:c + 1],
                            in_=ob_hbm[c * 128:(c + 1) * 128][:, None]
                            ).then_inc(*_ti(OBL, 16))
            k = 0
            for gi, g in enumerate(groups):
                c0, c1 = g[0] * 128, (g[-1] + 1) * 128
                for ic in range(NCH):
                    if k >= 2:
                        _wge(e, CV, k - 1)
                    e.dma_start(out=WF[k % 2][:, 0:(c1 - c0)],
                                in_=ow_hbm[ic * 128:(ic + 1) * 128, c0:c1]
                                ).then_inc(*_ti(LW[k % 2], 16))
                    k += 1

        @blk.gpsimd
        def _(e):
            for oc in range(NCH):
                if oc >= 2:
                    _wge(e, SGR, 2 * (oc - 1))
                e.dma_start(out=GTL[oc % 2][:], in_=g_hbm[oc]
                            ).then_inc(*_ti(GLD[oc % 2], 16))

        @blk.scalar
        def _(e):
            k = 0
            for gi, g in enumerate(groups):
                w = len(g) * 128
                for ic in range(NCH):
                    _wge(e, LW[k % 2], 16 * (k // 2 + 1))
                    if k >= 2:
                        _wge(e, MM, k - 1)
                    e.copy(out=WB[k % 2][:, 0:w],
                           in_=WF[k % 2][:, 0:w]).then_inc(*_ti(CV, 1))
                    k += 1

        @blk.tensor
        def _(e):
            _wge(e, CXL, 16)
            k = 0
            for gi, g in enumerate(groups):
                _wge(e, SGR, 2 * (3 * gi))
                for ic in range(NCH):
                    _wge(e, CV, k + 1)
                    for j, oc in enumerate(g):
                        for tt in range(2):
                            ins = e.matmul(
                                PS[j * 2 + tt][:],
                                WB[k % 2][:, j * 128:(j + 1) * 128],
                                CTX[:, ic, tt * 512:(tt + 1) * 512],
                                start=(ic == 0), stop=(ic == NCH - 1))
                    ins.then_inc(*_ti(MM, 1))
                    k += 1

        @blk.vector
        def _(e):
            _wge(e, OBL, 16 * NCH)
            for gi, g in enumerate(groups):
                _wge(e, MM, NCH * (gi + 1))
                for j, oc in enumerate(g):
                    _wge(e, GLD[oc % 2], 16 * (oc // 2 + 1))
                    for tt in range(2):
                        e.scalar_tensor_tensor(
                            out=SG[:, oc, tt * 512:(tt + 1) * 512],
                            in0=PS[j * 2 + tt][:],
                            scalar=OB[:, oc:oc + 1],
                            in1=GTL[oc % 2][:, tt * 512:(tt + 1) * 512],
                            op0=ALU.add, op1=ALU.mult).then_inc(*_ti(SGR, 1))

    emit_addln(nc, blk, f"{tag}b", SG, SGR, lambda oc: 2 * (oc + 1),
               x_hbm, x1_hbm, xnT_hbm, True, idt, IDS, sg_all=2 * NCH)
    sc.close()



def build_combine0(tag="cmb0"):
    nc = bass.Bass("TRN2", target_bir_lowering=False, debug=False)
    ctx = nc.dram_tensor("ctx", [NCH, 128, TC], BF16, kind="ExternalInput").ap()
    ow = nc.dram_tensor("ow", [D, D], F32, kind="ExternalInput").ap()
    ob = nc.dram_tensor("ob", [D], F32, kind="ExternalInput").ap()
    g = nc.dram_tensor("g", [NCH, 128, TC], BF16, kind="ExternalInput").ap()
    x = nc.dram_tensor("x", [TC, D], F32, kind="ExternalInput").ap()
    x1 = nc.dram_tensor("x1", [TC, D], F32, kind="ExternalOutput").ap()
    xnT = nc.dram_tensor("xnT", [NCH, 128, TC], BF16, kind="ExternalOutput").ap()
    with nc.Block() as blk:
        emit_combine0(nc, blk, ctx, ow, ob, g, x, x1, xnT, tag)
    return nc


def emit_final(nc, blk, x1_hbm, s1_hbm, g1_hbm, out_hbm, tag):
    sc = _Scope(nc)
    IDS = _vs(nc, f"{tag}_IDS")
    idt = _ident(nc, blk, IDS, sc, tag)
    S1L = [_vs(nc, f"{tag}_S1L{i}") for i in range(2)]
    G1L = [_vs(nc, f"{tag}_G1L{i}") for i in range(2)]
    SGR = _vs(nc, f"{tag}_SGR")
    SG = sc.sb(f"{tag}_sg", [128, NCH, TC], BF16)
    SC1 = [sc.sb(f"{tag}_sc1{i}", [128, TC], BF16)
           for i in range(2)]
    GC1 = [sc.sb(f"{tag}_gc1{i}", [128, TC], F16)
           for i in range(2)]

    @blk.sync
    def _(e):
        for oc in range(NCH):
            if oc >= 2:
                _wge(e, SGR, oc - 1)
            e.dma_start(out=SC1[oc % 2][:], in_=s1_hbm[oc]
                        ).then_inc(*_ti(S1L[oc % 2], 16))
            e.dma_start(out=GC1[oc % 2][:], in_=g1_hbm[oc]
                        ).then_inc(*_ti(G1L[oc % 2], 16))

    @blk.vector
    def _(e):
        for oc in range(NCH):
            _wge(e, S1L[oc % 2], 16 * (oc // 2 + 1))
            _wge(e, G1L[oc % 2], 16 * (oc // 2 + 1))
            e.tensor_mul(SG[:, oc, :], SC1[oc % 2][:],
                         GC1[oc % 2][:]).then_inc(*_ti(SGR, 1))

    emit_addln(nc, blk, f"{tag}b", SG, SGR, lambda oc: oc + 1,
               x1_hbm, out_hbm, None, False, idt, IDS)
    sc.close()



def build_final(tag="fin"):
    nc = bass.Bass("TRN2", target_bir_lowering=False, debug=False)
    x1 = nc.dram_tensor("x1", [TC, D], F32, kind="ExternalInput").ap()
    s1 = nc.dram_tensor("s1", [NCH, 128, TC], BF16, kind="ExternalInput").ap()
    g1 = nc.dram_tensor("g1", [NCH, 128, TC], F16, kind="ExternalInput").ap()
    out = nc.dram_tensor("out", [TC, D], F32, kind="ExternalOutput").ap()
    with nc.Block() as blk:
        emit_final(nc, blk, x1, s1, g1, out, tag)
    return nc


def emit_densewT(nc, blk, tag, w_hbm, NOB, rhs_get, store_dst, permute,
                 gate=None, TW=None, QST=None, PWQ=None, PQ=None, idt=None,
                 IDS=None, n_in_blk=4, WF=None, WB=None):
    sc = _Scope(nc)
    LW = [_vs(nc, f"{tag}_LW{i}") for i in range(2)]
    CV = _vs(nc, f"{tag}_CV")
    TR = _vs(nc, f"{tag}_TR")
    TE = _vs(nc, f"{tag}_TE")
    MM = _vs(nc, f"{tag}_MM")
    PE = _vs(nc, f"{tag}_PE")
    ST = [_vs(nc, f"{tag}_ST{i}", swdge=True) for i in range(2)]

    if WF is None:
        WF = [sc.sb(f"{tag}_wf{i}", [128, 1024], F32)
              for i in range(2)]
        WB = [sc.sb(f"{tag}_wb{i}", [128, 1024], BF16)
              for i in range(2)]

    @blk.sync
    def _(e):
        if gate is not None:
            _wge(e, gate[0], gate[1])
        k = 0
        for ob in range(NOB):
            for Tb in range(n_in_blk):
                if k >= 2:
                    _wge(e, CV, k - 1)
                e.dma_start(out=WF[k % 2][:],
                            in_=w_hbm[ob * 128:(ob + 1) * 128,
                                      Tb * 1024:(Tb + 1) * 1024]
                            ).then_inc(*_ti(LW[k % 2], 16))
                k += 1

    @blk.scalar
    def _(e):
        k = 0
        for ob in range(NOB):
            for Tb in range(n_in_blk):
                _wge(e, LW[k % 2], 16 * (k // 2 + 1))
                if k >= 2:
                    _wge(e, TR, 8 * (k - 1))
                if permute:
                    e.copy(out=WB[k % 2][:].rearrange("p (j m) -> p j m", j=8),
                           in_=WF[k % 2][:].rearrange("p (m j) -> p j m", j=8)
                           ).then_inc(*_ti(CV, 1))
                else:
                    e.copy(out=WB[k % 2][:], in_=WF[k % 2][:]).then_inc(*_ti(CV, 1))
                k += 1

    @blk.tensor
    def _(e):
        _wge(e, IDS, 2)
        k = 0
        for ob in range(NOB):
            for Tb in range(n_in_blk):
                _wge(e, CV, k + 1)
                if k >= 2:
                    _wge(e, TE, 4 * (k - 1))
                for cc in range(8):
                    ins = e.transpose(
                        PWQ[k % 2][:, cc * 128:(cc + 1) * 128],
                        WB[k % 2][:, cc * 128:(cc + 1) * 128], idt[:])
                ins.then_inc(*_ti(TR, 8))
                k += 1
            _wge(e, TE, 4 * n_in_blk * (ob + 1))
            for tt in range(2):
                kk = 2 * ob + tt
                if kk >= 2:
                    _wge(e, PE, kk - 1)
                for hc in range(NCH):
                    ins = e.matmul(PQ[kk % 2][:],
                                   TW[ob % 2][:, hc, :],
                                   rhs_get(hc, tt),
                                   start=(hc == 0), stop=(hc == NCH - 1))
                ins.then_inc(*_ti(MM, 1))

    @blk.vector
    def _(e):
        for ob in range(NOB):
            for Tb in range(n_in_blk):
                k = ob * n_in_blk + Tb
                _wge(e, TR, 8 * (k + 1))
                if ob >= 1:
                    _wge(e, MM, 2 * ob)
                e.tensor_copy(
                    out=TW[ob % 2][:, Tb * 8:(Tb + 1) * 8, :],
                    in_=PWQ[k % 2][:].rearrange("p (c w) -> p c w", c=8)
                ).then_inc(*_ti(TE, 4))
            for tt in range(2):
                kk = 2 * ob + tt
                _wge(e, MM, kk + 1)
                _wge(e, ST[kk % 2], 16 * (kk // 2))
                e.tensor_copy(out=QST[kk % 2][:],
                              in_=PQ[kk % 2][:]).then_inc(*_ti(PE, 1))

    @blk.gpsimd
    def _(e):
        for ob in range(NOB):
            for tt in range(2):
                kk = 2 * ob + tt
                _wge(e, PE, kk + 1)
                e.dma_start(out=store_dst(ob, tt),
                            in_=QST[kk % 2][:]).then_inc(*_ti(ST[kk % 2], 16))
        _wge(e, ST[0], 16 * NOB)
        _wge(e, ST[1], 16 * NOB)

    return ST
    sc.close()



def emit_l1attn(nc, blk, xnT_hbm, wqkv_hbm, wproj_hbm, s1_hbm, tag):
    sc = _Scope(nc)
    IDS = _vs(nc, f"{tag}_IDS")
    idt = _ident(nc, blk, IDS, sc, tag)

    AO = sc.sb(f"{tag}_ao", [128, NCH, TC], BF16)
    TW = [sc.sb(f"{tag}_tw{i}", [128, NCH, 128], BF16)
          for i in range(2)]
    QST = [sc.sb(f"{tag}_qst{i}", [128, 512], BF16)
           for i in range(2)]
    WFs = [sc.sb(f"{tag}_wfs{i}", [128, 1024], F32)
           for i in range(2)]
    WBs = [sc.sb(f"{tag}_wbs{i}", [128, 1024], BF16)
           for i in range(2)]
    PWQ = [sc.ps(f"{tag}_pwq{i}", [128, 1024], BF16)
           for i in range(2)]
    PQ = [sc.ps(f"{tag}_pq{i}", [128, 512], F32) for i in range(2)]
    PSC = [sc.ps(f"{tag}_psc{i}", [128, 64, 16], F32)
           for i in range(2)]
    PW3 = PWQ[0][0:16, :]

    qkv1h = nc.dram_tensor(f"{tag}_qkv1h", [48, 128, TC], BF16,
                           kind="Internal").ap()
    w2h = nc.dram_tensor(f"{tag}_w2h", [16, 64, 128], BF16, kind="Internal").ap()

    pairs = [(s, t) for s in range(4) for t in range(s + 1)]
    col = {(s, t): s * (s + 1) // 2 + t for (s, t) in pairs}
    AVSCH = []
    for s in range(4):
        for t in range(s + 1):
            for hf in range(2):
                AVSCH.append((s, t, hf))
    for s in range(4):
        for hf in range(2):
            AVSCH.append((s, None, hf))
    MRD = []
    _av = 0
    for (s, t, hf) in AVSCH:
        if t is None or t == 0:
            _av += 1
            MRD.append(_av)
        else:
            _av += 1
            MRD.append(_av)
            _av += 1
    AV_TOTAL = _av

    with nc.sbuf_tensor(f"{tag}_xnt1", [128, NCH, TC], BF16) as XNT1:
        XL1 = _vs(nc, f"{tag}_XL1")

        @blk.sync
        def _(e):
            e.dma_start(out=XNT1[:], in_=xnT_hbm.transpose([1, 0, 2])
                        ).then_inc(*_ti(XL1, 16))

        @blk.tensor
        def _(e):
            _wge(e, XL1, 16)

        AST = emit_densewT(
            nc, blk, f"{tag}A", wqkv_hbm, 48,
            lambda hc, tt: XNT1[:, hc, tt * 512:(tt + 1) * 512],
            lambda ob, tt: qkv1h[ob, :, tt * 512:(tt + 1) * 512],
            True, gate=None, TW=TW, QST=QST, PWQ=PWQ, PQ=PQ, idt=idt, IDS=IDS,
            WF=WFs, WB=WBs)

    ESB = _vs(nc, f"{tag}_ESB")
    KVL = _vs(nc, f"{tag}_KVL")
    QSL = _vs(nc, f"{tag}_QSL")
    PRD = _vs(nc, f"{tag}_PRD")
    SMM = _vs(nc, f"{tag}_SMM")
    SFX = _vs(nc, f"{tag}_SFX")
    SFA = _vs(nc, f"{tag}_SFA")
    TW2 = _vs(nc, f"{tag}_TW2")
    W3E = _vs(nc, f"{tag}_W3E")
    W3S = _vs(nc, f"{tag}_W3S", swdge=True)
    WBL = [_vs(nc, f"{tag}_WBL{i}") for i in range(2)]
    AVV = _vs(nc, f"{tag}_AVV")
    SAD = _vs(nc, f"{tag}_SAD")

    with nc.sbuf_tensor(f"{tag}_sca0", [128, 64, 16], F32) as SCAa, \
         nc.sbuf_tensor(f"{tag}_sca1", [128, 64, 16], F32) as SCAb, \
         nc.sbuf_tensor(f"{tag}_qs", [128, 8, 4, 256], BF16) as QS, \
         nc.sbuf_tensor(f"{tag}_ktv", [128, 8, 4, 256], BF16) as KTV, \
         nc.sbuf_tensor(f"{tag}_pr0", [128, 8192], BF16) as PRa, \
         nc.sbuf_tensor(f"{tag}_pr1", [128, 8192], BF16) as PRb, \
         nc.sbuf_tensor(f"{tag}_est", [128, 31], BF16) as EST, \
         nc.sbuf_tensor(f"{tag}_w2", [128, 64, 16], BF16) as W2, \
         nc.sbuf_tensor(f"{tag}_nmx1", [128, 64], F32) as NMX1, \
         nc.sbuf_tensor(f"{tag}_scw", [128, 64, 4], F32) as SCW, \
         nc.sbuf_tensor(f"{tag}_z1", [128, 64], F32) as Z1, \
         nc.sbuf_tensor(f"{tag}_rz1", [128, 64], F32) as RZ1, \
         nc.sbuf_tensor(f"{tag}_w3s", [16, 8, 128], BF16) as W3STG, \
         nc.sbuf_tensor(f"{tag}_wbc0", [128, 4096], BF16) as WBCa, \
         nc.sbuf_tensor(f"{tag}_wbc1", [128, 4096], BF16) as WBCb, \
         nc.sbuf_tensor(f"{tag}_tmp", [128, 4096], BF16) as TMP:
        PR = [PRa, PRb]
        WBC = [WBCa, WBCb]
        SCACC = [SCAa, SCAb]
        SCA = SCACC[1]

        def kb(t):
            return KTV[:, :, t:t + 1, :].broadcast_to([128, 8, 4, 256])

        vb = kb

        def ao_view(s):
            return AO[:].rearrange("p (kv sl) t -> p kv sl t", kv=8)[
                :, :, :, s * 256:(s + 1) * 256]

        @blk.sync
        def _(e):
            _wge(e, AST[0], 16 * 48)
            _wge(e, AST[1], 16 * 48)
            for t in range(4):
                src = qkv1h.rearrange("(kv sl) d t -> kv sl d t", kv=8)[
                    :, 4, :, t * 256:(t + 1) * 256].transpose([1, 0, 2])
                e.dma_start(out=KTV[:, :, t, :], in_=src).then_inc(*_ti(KVL, 16))
            for s in range(4):
                if s >= 1:
                    _wge(e, PRD, s * (s + 1) // 2)
                for sl in range(4):
                    src = qkv1h.rearrange("(kv sl) d t -> kv sl d t", kv=8)[
                        :, sl, :, s * 256:(s + 1) * 256].transpose([1, 0, 2])
                    e.dma_start(out=QS[:, :, sl, :], in_=src).then_inc(*_ti(QSL, 16))
            _wge(e, SMM, 10)
            for t in range(4):
                src = qkv1h.rearrange("(kv sl) d t -> kv sl d t", kv=8)[
                    :, 5, :, t * 256:(t + 1) * 256].transpose([1, 0, 2])
                e.dma_start(out=KTV[:, :, t, :], in_=src).then_inc(*_ti(KVL, 16))
            _wge(e, W3S, 16 * 8)
            for li, (s, t, hf) in enumerate(AVSCH):
                st = col[(s, t)] if t is not None else 12 + s
                if li >= 2:
                    _wge(e, AVV, MRD[li - 2])
                src = w2h[st, hf * 32:(hf + 1) * 32].flatten()[None, :]
                src = src.partition_broadcast(128)[:, 0, :]
                e.dma_start(out=WBC[li % 2][:], in_=src).then_inc(*_ti(WBL[li % 2], 16))

        @blk.vector
        def _(e):
            e.memset(EST[:], 0.0).then_inc(*_ti(ESB, 1))
            _wge(e, ESB, 1)
            e.memset(EST[:, 15:16], float(ISQ)).then_inc(*_ti(ESB, 1))
            _wge(e, KVL, 16 * 4)
            def acc_round(pi):
                _wge(e, SMM, pi + 1)
                if pi == 0:
                    e.tensor_copy(out=SCACC[0][:],
                                  in_=PSC[0][:]).then_inc(*_ti(SAD, 1))
                else:
                    _wge(e, SAD, pi)
                    e.tensor_add(SCACC[pi % 2][:], SCACC[(pi - 1) % 2][:],
                                 PSC[pi % 2][:]).then_inc(*_ti(SAD, 1))

            for pi, (s, t) in enumerate(pairs):
                _wge(e, QSL, 16 * 4 * (s + 1))
                if pi >= 2:
                    _wge(e, SMM, pi - 1)
                e.tensor_mul(PR[pi % 2][:].rearrange(
                    "p (kv sl t) -> p kv sl t", kv=8, sl=4),
                    QS[:], kb(t)).then_inc(*_ti(PRD, 1))
                if pi >= 1:
                    acc_round(pi - 1)
            acc_round(9)
            _wge(e, SAD, 10)
            e.memset(W2[:], 0.0).then_inc(*_ti(SFX, 1))
            _wge(e, SFX, 1)
            e.memset(W2[:, :, 0:1], 1.0)
            e.memset(W2[:, :, 12:13], 1.0).then_inc(*_ti(SFX, 1))
            for si in range(1, 4):
                c0 = si * (si + 1) // 2
                base = 5 * (si - 1) + 2
                if si >= 2:
                    _wge(e, SFA, si - 1)
                e.reduce_max(out=NMX1[:], in_=SCA[:, :, c0:c0 + si + 1],
                             axis=AX.X, negate=True).then_inc(*_ti(SFX, 1))
                _wge(e, SFX, base + 1)
                e.tensor_add(SCW[:, :, 0:si + 1], SCA[:, :, c0:c0 + si + 1],
                             NMX1[:, :, None].broadcast_to([128, 64, si + 1])
                             ).then_inc(*_ti(SFX, 1))
                _wge(e, SFA, si)
                e.reduce_sum(out=Z1[:], in_=W2[:, :, c0:c0 + si + 1],
                             axis=AX.X).then_inc(*_ti(SFX, 1))
                _wge(e, SFX, base + 3)
                e.reciprocal(out=RZ1[:], in_=Z1[:]).then_inc(*_ti(SFX, 1))
                _wge(e, SFX, base + 4)
                e.tensor_copy(out=W2[:, :, 12 + si:13 + si],
                              in_=RZ1[:, :, None]).then_inc(*_ti(SFX, 1))
            _wge(e, KVL, 16 * 8)
            av = 0
            for li, (s, t, hf) in enumerate(AVSCH):
                kv0 = hf * 4
                _wge(e, WBL[li % 2], 16 * (li // 2 + 1))
                wv = WBC[li % 2][:].rearrange(
                    "p (kv sl t) -> p kv sl t", kv=4, sl=4)
                aov = ao_view(s)[:, kv0:kv0 + 4, :, :]
                if t is None:
                    _wge(e, AVV, av)
                    e.tensor_mul(aov, aov, wv).then_inc(*_ti(AVV, 1))
                    av += 1
                elif t == 0:
                    e.tensor_mul(aov, vb(t)[:, kv0:kv0 + 4, :, :],
                                 wv).then_inc(*_ti(AVV, 1))
                    av += 1
                else:
                    tv = TMP[:].rearrange("p (kv sl t) -> p kv sl t",
                                          kv=4, sl=4)
                    _wge(e, AVV, av)
                    e.tensor_mul(tv, vb(t)[:, kv0:kv0 + 4, :, :],
                                 wv).then_inc(*_ti(AVV, 1))
                    av += 1
                    _wge(e, AVV, av)
                    e.tensor_add(aov, aov, tv).then_inc(*_ti(AVV, 1))
                    av += 1

        @blk.tensor
        def _(e):
            _wge(e, ESB, 2)
            for pi, (s, t) in enumerate(pairs):
                _wge(e, PRD, pi + 1)
                if pi >= 2:
                    _wge(e, SAD, pi - 1)
                cl = col[(s, t)]
                for bb in range(64):
                    ins = e.matmul(PSC[pi % 2][:, bb, :],
                                   PR[pi % 2][:, bb * 128:(bb + 1) * 128],
                                   EST[:, 15 - cl:31 - cl],
                                   start=True, stop=True)
                ins.then_inc(*_ti(SMM, 1))
            _wge(e, SFX, 17)
            _wge(e, SFA, 3)
            for grp in range(8):
                if grp >= 1:
                    _wge(e, W3E, grp)
                for j in range(8):
                    bb = grp * 8 + j
                    ins = e.transpose(PW3[:, j * 128:(j + 1) * 128],
                                      W2[:, bb, :], idt[:])
                ins.then_inc(*_ti(TW2, 1))

        @blk.scalar
        def _(e):
            for si in range(1, 4):
                c0 = si * (si + 1) // 2
                _wge(e, SFX, 5 * (si - 1) + 4)
                e.activation(out=W2[:, :, c0:c0 + si + 1],
                             in_=SCW[:, :, 0:si + 1],
                             func=AF.Exp).then_inc(*_ti(SFA, 1))
            for grp in range(8):
                _wge(e, TW2, grp + 1)
                if grp >= 1:
                    _wge(e, W3S, 16 * grp)
                e.copy(out=W3STG[:],
                       in_=PW3[:].rearrange("p (c w) -> p c w", c=8)
                       ).then_inc(*_ti(W3E, 1))

        @blk.gpsimd
        def _(e):
            for grp in range(8):
                _wge(e, W3E, grp + 1)
                e.dma_start(out=w2h[:, grp * 8:(grp + 1) * 8, :],
                            in_=W3STG[:]).then_inc(*_ti(W3S, 16))

    emit_densewT(
        nc, blk, f"{tag}C", wproj_hbm, NCH,
        lambda hc, tt: AO[:, hc, tt * 512:(tt + 1) * 512],
        lambda ob, tt: s1_hbm[ob, :, tt * 512:(tt + 1) * 512],
        False, gate=(AVV, AV_TOTAL), TW=TW, QST=QST, PWQ=PWQ, PQ=PQ, idt=idt,
        IDS=IDS, WF=WFs, WB=WBs)
    sc.close()



def build_l1attn(tag="l1a"):
    nc = bass.Bass("TRN2", target_bir_lowering=False, debug=False)
    xnT = nc.dram_tensor("xnT", [NCH, 128, TC], BF16, kind="ExternalInput").ap()
    wqkv = nc.dram_tensor("wqkv", [6144, D], F32, kind="ExternalInput").ap()
    wproj = nc.dram_tensor("wproj", [D, D], F32, kind="ExternalInput").ap()
    s1 = nc.dram_tensor("s1", [NCH, 128, TC], BF16, kind="ExternalOutput").ap()
    with nc.Block() as blk:
        emit_l1attn(nc, blk, xnT, wqkv, wproj, s1, tag)
    return nc


INTER = 11008
NTP = 11


def _tp_cols(tp):
    n0 = tp * 1024
    w = min(1024, INTER - n0)
    return n0, w, w // 8


def emit_ffn1(nc, blk, xnT_hbm, gq, gs, gz, uq, us, uz, dq, ds, dz, g1_hbm, tag):
    sc = _Scope(nc)
    IDS = _vs(nc, f"{tag}_IDS")
    idt = _ident(nc, blk, IDS, sc, tag)
    interh = nc.dram_tensor(f"{tag}_interh", [88, 128, TC], BF16,
                            kind="Internal").ap()

    G16 = sc.sb(f"{tag}_g16", [128, 8], BF16)
    GI = sc.sb(f"{tag}_gi", [128, 8], I32)
    GI2 = sc.sb(f"{tag}_gi2", [128, 8], I32)
    GI3 = sc.sb(f"{tag}_gi3", [128, 8], I32)

    GS16 = _vs(nc, f"{tag}_GS16")

    @blk.gpsimd
    def _(g):
        g.iota(GI[:], pattern=[[-16, 8]], base=0,
               channel_multiplier=1).then_inc(*_ti(IDS, 1))

    @blk.vector
    def _(v):
        _wge(v, IDS, 3)
        v.tensor_scalar(out=GI2[:], in0=GI[:], scalar1=0, scalar2=None,
                        op0=ALU.is_ge).then_inc(*_ti(GS16, 1))
        _wge(v, GS16, 1)
        v.tensor_scalar(out=GI3[:], in0=GI[:], scalar1=16, scalar2=None,
                        op0=ALU.is_lt).then_inc(*_ti(GS16, 1))
        _wge(v, GS16, 2)
        v.tensor_tensor(out=G16[:], in0=GI2[:], in1=GI3[:],
                        op=ALU.mult).then_inc(*_ti(GS16, 1))

    XLF = _vs(nc, f"{tag}_XLF")
    ZQS = _vs(nc, f"{tag}_ZQS")
    SXM = _vs(nc, f"{tag}_SXM")
    SXE = _vs(nc, f"{tag}_SXE")
    SXD = _vs(nc, f"{tag}_SXD", swdge=True)
    SCLD = _vs(nc, f"{tag}_SCLD")
    QWL = [_vs(nc, f"{tag}_QWL{i}") for i in range(2)]
    SBL = [_vs(nc, f"{tag}_SBL{i}") for i in range(2)]
    ZPP = _vs(nc, f"{tag}_ZPP")
    UPK = _vs(nc, f"{tag}_UPK")
    SCL = _vs(nc, f"{tag}_SCL")
    MM1 = _vs(nc, f"{tag}_MM1")
    GEVC = _vs(nc, f"{tag}_GEVC")
    UEVC = _vs(nc, f"{tag}_UEVC")
    IST = [_vs(nc, f"{tag}_IST{i}", swdge=True) for i in range(2)]

    from contextlib import ExitStack
    _st = ExitStack()
    with _st:
        XNT1 = _st.enter_context(
            nc.sbuf_tensor(f"{tag}_xnt1", [128, NCH, TC], BF16))
        UB = _st.enter_context(
            nc.sbuf_tensor(f"{tag}_ub", [128, NCH, 1024], BF16))
        SGT = _st.enter_context(nc.sbuf_tensor(f"{tag}_sgt", [128, 8, TC], BF16))
        SXA = _st.enter_context(nc.sbuf_tensor(f"{tag}_sxa", [32, TC], BF16))
        ESX = _st.enter_context(nc.sbuf_tensor(f"{tag}_esx", [8, TC], BF16))
        ZQG = _st.enter_context(nc.sbuf_tensor(f"{tag}_zqg", [32, 1376], I32))
        ZQU = _st.enter_context(nc.sbuf_tensor(f"{tag}_zqu", [32, 1376], I32))
        QWa = _st.enter_context(nc.sbuf_tensor(f"{tag}_qw0", [128, 1024], I32))
        QWb = _st.enter_context(nc.sbuf_tensor(f"{tag}_qw1", [128, 1024], I32))
        SBa = _st.enter_context(nc.sbuf_tensor(f"{tag}_sbt0", [128, 1024], F16))
        SBb = _st.enter_context(nc.sbuf_tensor(f"{tag}_sbt1", [128, 1024], F16))
        UI = _st.enter_context(nc.sbuf_tensor(f"{tag}_ui", [128, 1024], I32))
        SCS = _st.enter_context(nc.sbuf_tensor(f"{tag}_scs", [32, 1024], F16))
        ZI = _st.enter_context(nc.sbuf_tensor(f"{tag}_zi", [32, 1024], I32))
        ZF = _st.enter_context(nc.sbuf_tensor(f"{tag}_zf", [32, 1024], F32))
        ZPN = _st.enter_context(nc.sbuf_tensor(f"{tag}_zpn", [32, 1024], BF16))
        ISa = _st.enter_context(nc.sbuf_tensor(f"{tag}_inst0", [128, 512], BF16))
        ISb = _st.enter_context(nc.sbuf_tensor(f"{tag}_inst1", [128, 512], BF16))
        P1a = _st.enter_context(nc.psum_tensor(f"{tag}_p1a", [128, 512], F32))
        P1b = _st.enter_context(nc.psum_tensor(f"{tag}_p1b", [128, 512], F32))
        P1c = _st.enter_context(nc.psum_tensor(f"{tag}_p1c", [128, 512], F32))
        P1d = _st.enter_context(nc.psum_tensor(f"{tag}_p1d", [128, 512], F32))
        PSX = _st.enter_context(nc.psum_tensor(f"{tag}_psx", [8, TC], F32))
        QW = [QWa, QWb]
        SBT = [SBa, SBb]
        INST = [ISa, ISb]
        P1 = [[P1a, P1b], [P1c, P1d]]
        ZQ = [ZQG, ZQU]
        mats = [(gq, gs, gz), (uq, us, uz)]

        @blk.sync
        def _(e):
            e.dma_start(out=XNT1[:], in_=xnT_hbm.transpose([1, 0, 2])
                        ).then_inc(*_ti(XLF, 16))
            e.dma_start(out=ZQG[:], in_=gz).then_inc(*_ti(ZQS, 16))
            e.dma_start(out=ZQU[:], in_=uz).then_inc(*_ti(ZQS, 16))
            k = 0
            for tp in range(NTP):
                n0, w, mr = _tp_cols(tp)
                for mi in range(2):
                    i = 2 * tp + mi
                    mq, ms, mz = mats[mi]
                    if i >= 1:
                        _wge(e, ZPP, 3 * i)
                    e.dma_start(out=SCS[:, 0:w],
                                in_=ms[:, n0:n0 + w]).then_inc(*_ti(SCLD, 16))
                    for T in range(4):
                        if k >= 2:
                            _wge(e, UPK, 8 * (k - 1))
                            _wge(e, SCL, 8 * (k - 1))
                        e.dma_start(out=QW[k % 2][:, 0:w],
                                    in_=mq[T * 128:(T + 1) * 128, n0:n0 + w]
                                    ).then_inc(*_ti(QWL[k % 2], 16))
                        src = ms[8 * T:8 * T + 8, n0:n0 + w]
                        src = src[:, None, :].partition_broadcast(16)
                        src = src.transpose([1, 0, 2, 3])[:, :, 0, :]
                        e.dma_start(out=SBT[k % 2][:, 0:w],
                                    in_=src).then_inc(*_ti(SBL[k % 2], 16))
                        k += 1

        @blk.vector
        def _(e):
            for T in range(4):
                _wge(e, SXM, T + 1)
                if T >= 1:
                    _wge(e, SXD, 16 * T)
                e.tensor_copy(out=ESX[:], in_=PSX[:]).then_inc(*_ti(SXE, 1))
            k = 0
            for tp in range(NTP):
                n0, w, mr = _tp_cols(tp)
                for mi in range(2):
                    i = 2 * tp + mi
                    _wge(e, ZQS, 32)
                    _wge(e, SCLD, 16 * (i + 1))
                    if i >= 1:
                        _wge(e, MM1, 16 * i)
                    for jn in range(8):
                        ins = e.tensor_scalar(
                            out=ZI[:, jn:w:8],
                            in0=ZQ[mi][:, n0 // 8:(n0 + w) // 8],
                            scalar1=4 * jn, scalar2=15,
                            op0=ALU.logical_shift_right, op1=ALU.bitwise_and)
                    ins.then_inc(*_ti(ZPP, 1))
                    _wge(e, ZPP, 3 * i + 1)
                    e.tensor_scalar(out=ZF[:, 0:w], in0=ZI[:, 0:w], scalar1=1,
                                    scalar2=-1.0, op0=ALU.add,
                                    op1=ALU.mult).then_inc(*_ti(ZPP, 1))
                    _wge(e, ZPP, 3 * i + 2)
                    e.tensor_mul(ZPN[:, 0:w], ZF[:, 0:w],
                                 SCS[:, 0:w]).then_inc(*_ti(ZPP, 1))
                    for T in range(4):
                        _wge(e, QWL[k % 2], 16 * (k // 2 + 1))
                        _wge(e, SBL[k % 2], 16 * (k // 2 + 1))
                        if T == 0:
                            _wge(e, MM1, 16 * i)
                        for j in range(8):
                            kc = 8 * T + j
                            if k + j > 0:
                                _wge(e, SCL, 8 * k + j)
                            e.tensor_scalar(
                                out=UI[:, 0:w], in0=QW[k % 2][:, 0:w],
                                scalar1=4 * j, scalar2=15,
                                op0=ALU.logical_shift_right,
                                op1=ALU.bitwise_and).then_inc(*_ti(UPK, 1))
                            _wge(e, UPK, 8 * k + j + 1)
                            e.tensor_mul(UB[:, kc, 0:w], UI[:, 0:w],
                                         SBT[k % 2][:, 0:w]).then_inc(*_ti(SCL, 1))
                        k += 1
                    if mi == 1:
                        for jp in range(8):
                            st = 8 * tp + jp
                            for tt in range(2):
                                _wge(e, MM1, 16 * i + 2 * jp + tt + 1)
                                _wge(e, GEVC, 16 * tp + 2 * jp + 2)
                                sti = 2 * st + tt
                                _wge(e, IST[sti % 2], 16 * (sti // 2))
                                e.tensor_mul(
                                    INST[sti % 2][0:mr, :],
                                    P1[jp % 2][tt][0:mr, :],
                                    SGT[0:mr, jp, tt * 512:(tt + 1) * 512]
                                ).then_inc(*_ti(UEVC, 1))

        @blk.scalar
        def _(e):
            for tp in range(NTP):
                n0, w, mr = _tp_cols(tp)
                i = 2 * tp
                for jp in range(8):
                    for tt in range(2):
                        _wge(e, MM1, 16 * i + 2 * jp + tt + 1)
                        e.activation(
                            out=SGT[0:mr, jp, tt * 512:(tt + 1) * 512],
                            in_=P1[jp % 2][tt][0:mr, :],
                            func=AF.Silu).then_inc(*_ti(GEVC, 1))

        @blk.tensor
        def _(e):
            _wge(e, GS16, 3)
            _wge(e, XLF, 16)
            for T in range(4):
                if T >= 1:
                    _wge(e, SXE, T)
                for tt in range(2):
                    for j in range(8):
                        kc = 8 * T + j
                        ins = e.matmul(PSX[:, tt * 512:(tt + 1) * 512],
                                       G16[:],
                                       XNT1[:, kc, tt * 512:(tt + 1) * 512],
                                       start=(j == 0), stop=(j == 7))
                ins.then_inc(*_ti(SXM, 1))
            for tp in range(NTP):
                n0, w, mr = _tp_cols(tp)
                for mi in range(2):
                    i = 2 * tp + mi
                    _wge(e, SCL, 32 * (i + 1))
                    _wge(e, ZPP, 3 * i + 3)
                    if i == 0:
                        _wge(e, SXD, 16 * 4)
                    for jp in range(8):
                        for tt in range(2):
                            if jp >= 2:
                                pi_, pjp = i, jp - 2
                            else:
                                pi_, pjp = i - 1, jp + 6
                            if pi_ >= 0:
                                psem = GEVC if pi_ % 2 == 0 else UEVC
                                _wge(e, psem, 16 * (pi_ // 2) + 2 * pjp + tt + 1)
                            for kc in range(NCH):
                                e.matmul(
                                    P1[jp % 2][tt][0:mr, :],
                                    UB[:, kc, jp:w:8],
                                    XNT1[:, kc, tt * 512:(tt + 1) * 512],
                                    start=(kc == 0), stop=False)
                            e.matmul(
                                P1[jp % 2][tt][0:mr, :],
                                ZPN[:, jp:w:8],
                                SXA[:, tt * 512:(tt + 1) * 512],
                                start=False, stop=True).then_inc(*_ti(MM1, 1))

        @blk.gpsimd
        def _(e):
            for T in range(4):
                _wge(e, SXE, T + 1)
                e.dma_start(out=SXA[8 * T:8 * T + 8, :],
                            in_=ESX[:]).then_inc(*_ti(SXD, 16))
            nst = 0
            for tp in range(NTP):
                n0, w, mr = _tp_cols(tp)
                for jp in range(8):
                    st = 8 * tp + jp
                    for tt in range(2):
                        sti = 2 * st + tt
                        _wge(e, UEVC, 16 * tp + 2 * jp + tt + 1)
                        e.dma_start(
                            out=interh[st, 0:mr, tt * 512:(tt + 1) * 512],
                            in_=INST[sti % 2][0:mr, :]
                        ).then_inc(*_ti(IST[sti % 2], 16))
                        nst += 1
            _wge(e, IST[0], 16 * (nst // 2))
            _wge(e, IST[1], 16 * (nst // 2))

    ILD = _vs(nc, f"{tag}_ILD")
    QDL = _vs(nc, f"{tag}_QDL")
    SDL = _vs(nc, f"{tag}_SDL")
    ZPD2 = _vs(nc, f"{tag}_ZPD2")
    UPK2 = _vs(nc, f"{tag}_UPK2")
    SCL2 = _vs(nc, f"{tag}_SCL2")
    MM2 = _vs(nc, f"{tag}_MM2")
    ADC = _vs(nc, f"{tag}_ADC")
    SX2M = _vs(nc, f"{tag}_SX2M")
    SX2E = _vs(nc, f"{tag}_SX2E")
    SX2D = _vs(nc, f"{tag}_SX2D", swdge=True)
    FEV = _vs(nc, f"{tag}_FEV")
    FSL = _vs(nc, f"{tag}_FSL")
    GST2 = [_vs(nc, f"{tag}_GST2{i}", swdge=True) for i in range(2)]

    ACC = sc.sb(f"{tag}_acc", [128, NCH, TC], BF16)
    ITL = sc.sb(f"{tag}_itl", [128, 8, TC], BF16)
    QWD = sc.sb(f"{tag}_qwd", [128, D], I32)
    SD = sc.sb(f"{tag}_sd", [128, D], F16)
    UID = sc.sb(f"{tag}_uid", [128, D], I32)
    SXD2 = sc.sb(f"{tag}_sxd2", [96, TC], BF16)
    ESX2 = sc.sb(f"{tag}_esx2", [8, TC], BF16)
    ZQD = sc.sb(f"{tag}_zqd", [86, 512], I32)
    ZPND = sc.sb(f"{tag}_zpnd", [86, D], BF16)
    GS2 = [sc.sb(f"{tag}_gs2{i}", [128, 512], F16)
           for i in range(2)]
    GS3 = [sc.sb(f"{tag}_gs3{i}", [128, 512], F16)
           for i in range(2)]

    with nc.sbuf_tensor(f"{tag}_zid", [86, D], I32) as ZID, \
         nc.sbuf_tensor(f"{tag}_zfd", [86, D], F32) as ZFD, \
         nc.sbuf_tensor(f"{tag}_scd", [86, D], F16) as SCD:
        @blk.sync
        def _(e):
            _wge(e, IST[0], 16 * 88)
            _wge(e, IST[1], 16 * 88)
            e.dma_start(out=ZQD[:], in_=dz).then_inc(*_ti(ZPD2, 16))
            e.dma_start(out=SCD[:], in_=ds).then_inc(*_ti(ZPD2, 16))

        @blk.vector
        def _(e):
            _wge(e, ZPD2, 32)
            for jn in range(8):
                ins = e.tensor_scalar(
                    out=ZID[:, jn:D:8], in0=ZQD[:],
                    scalar1=4 * jn, scalar2=15,
                    op0=ALU.logical_shift_right, op1=ALU.bitwise_and)
            ins.then_inc(*_ti(ZPD2, 1))
            _wge(e, ZPD2, 33)
            e.tensor_scalar(out=ZFD[:], in0=ZID[:], scalar1=1, scalar2=-1.0,
                            op0=ALU.add, op1=ALU.mult).then_inc(*_ti(ZPD2, 1))
            _wge(e, ZPD2, 34)
            e.tensor_mul(ZPND[:], ZFD[:], SCD[:]).then_inc(*_ti(ZPD2, 1))

    UBD = sc.sb(f"{tag}_ubd", [128, 8, D], BF16)
    PD = [sc.ps(f"{tag}_pd{i}", [128, 512], F32)
          for i in range(4)]
    PSX2 = sc.ps(f"{tag}_psx2", [8, TC], F32)

    @blk.sync
    def _(e):
        _wge(e, ZPD2, 35)
        for tpp in range(NTP):
            mr2 = 128 if tpp < 10 else 96
            glo = 8 * tpp
            if tpp >= 1:
                _wge(e, MM2, 64 * tpp)
                _wge(e, SX2M, tpp)
            e.dma_start(out=ITL[0:mr2, :, :],
                        in_=interh[glo:glo + 8, 0:mr2, :].transpose([1, 0, 2])
                        ).then_inc(*_ti(ILD, 16))
            e.dma_start(out=QWD[0:mr2, :],
                        in_=dq[128 * tpp:128 * tpp + mr2, :]).then_inc(*_ti(QDL, 16))
            ng = 8 if tpp < 10 else 6
            src = ds[8 * tpp:8 * tpp + ng, :]
            src = src[:, None, :].partition_broadcast(16)
            src = src.transpose([1, 0, 2, 3])[:, :, 0, :]
            e.dma_start(out=SD[0:16 * ng, :],
                        in_=src).then_inc(*_ti(SDL, 16))

    @blk.vector
    def _(e):
        for tpp in range(NTP):
            mr2 = 128 if tpp < 10 else 96
            _wge(e, QDL, 16 * (tpp + 1))
            _wge(e, SDL, 16 * (tpp + 1))
            for j in range(8):
                if 8 * tpp + j > 0:
                    _wge(e, SCL2, 8 * tpp + j)
                e.tensor_scalar(out=UID[0:mr2, :], in0=QWD[0:mr2, :],
                                scalar1=4 * j, scalar2=15,
                                op0=ALU.logical_shift_right,
                                op1=ALU.bitwise_and).then_inc(*_ti(UPK2, 1))
                _wge(e, UPK2, 8 * tpp + j + 1)
                e.tensor_mul(UBD[0:mr2, j, :], UID[0:mr2, :],
                             SD[0:mr2, :]).then_inc(*_ti(SCL2, 1))
            _wge(e, SX2M, tpp + 1)
            if tpp >= 1:
                _wge(e, SX2D, 16 * tpp)
            e.tensor_copy(out=ESX2[:], in_=PSX2[:]).then_inc(*_ti(SX2E, 1))
            for oc in range(NCH):
                for tt in range(2):
                    idx = 64 * tpp + 2 * oc + tt
                    _wge(e, MM2, idx + 1)
                    if tpp == 0:
                        e.tensor_copy(out=ACC[:, oc, tt * 512:(tt + 1) * 512],
                                      in_=PD[idx % 4][:]).then_inc(*_ti(ADC, 1))
                    else:
                        e.tensor_add(ACC[:, oc, tt * 512:(tt + 1) * 512],
                                     ACC[:, oc, tt * 512:(tt + 1) * 512],
                                     PD[idx % 4][:]).then_inc(*_ti(ADC, 1))
        for oc in range(NCH):
            for tt in range(2):
                idx = 64 * NTP + 2 * oc + tt
                _wge(e, MM2, idx + 1)
                e.tensor_add(ACC[:, oc, tt * 512:(tt + 1) * 512],
                             ACC[:, oc, tt * 512:(tt + 1) * 512],
                             PD[idx % 4][:]).then_inc(*_ti(ADC, 1))

    @blk.tensor
    def _(e):
        for tpp in range(NTP):
            mr2 = 128 if tpp < 10 else 96
            ng = 8 if tpp < 10 else 6
            _wge(e, ILD, 16 * (tpp + 1))
            if tpp >= 1:
                _wge(e, SX2E, tpp)
            for tt in range(2):
                for j in range(8):
                    ins = e.matmul(PSX2[0:ng, tt * 512:(tt + 1) * 512],
                                   G16[0:mr2, 0:ng],
                                   ITL[0:mr2, j, tt * 512:(tt + 1) * 512],
                                   start=(j == 0), stop=(j == 7))
            ins.then_inc(*_ti(SX2M, 1))
            _wge(e, SCL2, 8 * (tpp + 1))
            for oc in range(NCH):
                for tt in range(2):
                    idx = 64 * tpp + 2 * oc + tt
                    if idx >= 4:
                        _wge(e, ADC, idx - 3)
                    for j in range(8):
                        ins = e.matmul(PD[idx % 4][:],
                                       UBD[0:mr2, j, oc * 128:(oc + 1) * 128],
                                       ITL[0:mr2, j, tt * 512:(tt + 1) * 512],
                                       start=(j == 0), stop=(j == 7))
                    ins.then_inc(*_ti(MM2, 1))
        _wge(e, SX2D, 16 * NTP)
        _wge(e, ZPD2, 35)
        for oc in range(NCH):
            for tt in range(2):
                idx = 64 * NTP + 2 * oc + tt
                _wge(e, ADC, idx - 3)
                e.matmul(PD[idx % 4][:],
                         ZPND[:, oc * 128:(oc + 1) * 128],
                         SXD2[0:86, tt * 512:(tt + 1) * 512],
                         start=True, stop=True).then_inc(*_ti(MM2, 1))

    @blk.gpsimd
    def _(e):
        for tpp in range(NTP):
            _wge(e, SX2E, tpp + 1)
            ng = 8 if tpp < 10 else 6
            e.dma_start(out=SXD2[8 * tpp:8 * tpp + ng, :],
                        in_=ESX2[0:ng, :]).then_inc(*_ti(SX2D, 16))
        nst = 0
        for oc in range(NCH):
            for tt in range(2):
                kk = 2 * oc + tt
                _wge(e, FEV, kk + 1)
                e.dma_start(out=g1_hbm[oc, :, tt * 512:(tt + 1) * 512],
                            in_=GS3[kk % 2][:]).then_inc(*_ti(GST2[kk % 2], 16))
                nst += 1
        _wge(e, GST2[0], 16 * (nst // 2))
        _wge(e, GST2[1], 16 * (nst // 2))

    @blk.scalar
    def _(e):
        for oc in range(NCH):
            for tt in range(2):
                kk = 2 * oc + tt
                _wge(e, ADC, 64 * (NTP + 1))
                if kk >= 2:
                    _wge(e, FEV, kk - 1)
                e.copy(out=GS2[kk % 2][:],
                       in_=ACC[:, oc, tt * 512:(tt + 1) * 512]).then_inc(*_ti(FSL, 1))
                _wge(e, FSL, kk + 1)
                if kk >= 2:
                    _wge(e, GST2[kk % 2], 16 * (kk // 2))
                e.activation(out=GS3[kk % 2][:], in_=GS2[kk % 2][:],
                             func=AF.Silu).then_inc(*_ti(FEV, 1))
    sc.close()



def build_ffn1(tag="f1"):
    nc = bass.Bass("TRN2", target_bir_lowering=False, debug=False)
    xnT = nc.dram_tensor("xnT", [NCH, 128, TC], BF16, kind="ExternalInput").ap()
    gq = nc.dram_tensor("gq", [512, INTER], I32, kind="ExternalInput").ap()
    gs = nc.dram_tensor("gs", [32, INTER], F16, kind="ExternalInput").ap()
    gz = nc.dram_tensor("gz", [32, 1376], I32, kind="ExternalInput").ap()
    uq = nc.dram_tensor("uq", [512, INTER], I32, kind="ExternalInput").ap()
    us = nc.dram_tensor("us", [32, INTER], F16, kind="ExternalInput").ap()
    uz = nc.dram_tensor("uz", [32, 1376], I32, kind="ExternalInput").ap()
    dq = nc.dram_tensor("dq", [1376, D], I32, kind="ExternalInput").ap()
    ds = nc.dram_tensor("ds", [86, D], F16, kind="ExternalInput").ap()
    dz = nc.dram_tensor("dz", [86, 512], I32, kind="ExternalInput").ap()
    g1 = nc.dram_tensor("g1", [NCH, 128, TC], F16, kind="ExternalOutput").ap()
    with nc.Block() as blk:
        emit_ffn1(nc, blk, xnT, gq, gs, gz, uq, us, uz, dq, ds, dz, g1, tag)
    return nc


def emit_qkvkv(nc, blk, xnTf_hbm, w_hbm, b_hbm, kf_hbm, vf_hbm, tag):
    sc = _Scope(nc)
    IDS = _vs(nc, f"{tag}_IDS")
    idt = _ident(nc, blk, IDS, sc, tag)

    BS_ = _vs(nc, f"{tag}_BS")
    LW = [_vs(nc, f"{tag}_LW{i}") for i in range(2)]
    CVW = _vs(nc, f"{tag}_CVW")
    XLD = [_vs(nc, f"{tag}_XLD{i}") for i in range(2)]
    MM = _vs(nc, f"{tag}_MM")
    EVB = _vs(nc, f"{tag}_EVB")
    SQK = [_vs(nc, f"{tag}_SQK{i}", swdge=True) for i in range(2)]
    VEV = _vs(nc, f"{tag}_VEV")
    TRV = _vs(nc, f"{tag}_TRV")
    EV2 = _vs(nc, f"{tag}_EV2")
    VS = [_vs(nc, f"{tag}_VS{i}", swdge=True) for i in range(2)]

    BIAS = sc.sb(f"{tag}_bias", [128, 64], F32)
    WF = [sc.sb(f"{tag}_wf{i}", [128, 512], F32) for i in range(2)]
    WGB = [sc.sb(f"{tag}_wgb{i}", [128, NCH, 512], BF16) for i in range(2)]
    XNF = [sc.sb(f"{tag}_xnf{i}", [128, NCH, 512], BF16) for i in range(2)]
    STQ = [sc.sb(f"{tag}_stq{i}", [128, 512], BF16) for i in range(2)]
    VSTG = [sc.sb(f"{tag}_vstg{i}", [128, 512], BF16) for i in range(2)]
    VST2 = [sc.sb(f"{tag}_vst2{i}", [128, 4, 128], BF16) for i in range(2)]
    PS = [sc.ps(f"{tag}_ps{i}", [128, 512], F32) for i in range(8)]
    PTV = [PS[6][:].bitcast(BF16), PS[7][:].bitcast(BF16)]

    groups = [list(range(32 + g * 4, 32 + g * 4 + 4)) for g in range(8)]
    vstart = [64 + 3 * g for g in range(10)] + [94]
    groups += [list(range(st, min(st + 3, 96))) for st in vstart]

    def is_v(goc):
        return goc >= 64

    NTT = (B * S) // 512

    iters = [(gi, tq) for gi in range(len(groups)) for tq in range(NTT)]
    kev_after = []
    vev_after = []
    ke, ve = 0, 0
    for gi, tq in iters:
        if is_v(groups[gi][0]):
            ve += len(groups[gi])
        else:
            ke += len(groups[gi])
        kev_after.append(ke)
        vev_after.append(ve)

    @blk.sync
    def _(e):
        for c in range(64):
            e.dma_start(out=BIAS[:, c:c + 1],
                        in_=b_hbm[(32 + c) * 128:(33 + c) * 128][:, None]
                        ).then_inc(*_ti(BS_, 16))
        kwl = 0
        for gi, g in enumerate(groups):
            c0, c1 = g[0] * 128, (g[-1] + 1) * 128
            for ic in range(NCH):
                if kwl >= 2:
                    _wge(e, CVW, kwl - 1)
                e.dma_start(out=WF[kwl % 2][:, 0:(c1 - c0)],
                            in_=w_hbm[ic * 128:(ic + 1) * 128, c0:c1]
                            ).then_inc(*_ti(LW[kwl % 2], 16))
                kwl += 1

    @blk.gpsimd
    def _(e):
        NQ = len(iters)

        def load(q):
            gi, tq = iters[q]
            if q >= 2:
                _wge(e, MM, NCH * (q - 1))
            e.dma_start(out=XNF[q % 2][:],
                        in_=xnTf_hbm[:, :, tq * 512:(tq + 1) * 512]
                        .transpose([1, 0, 2])).then_inc(*_ti(XLD[q % 2], 16))

        load(0)
        load(1)
        sq = 0
        ev2 = 0
        for q, (gi, tq) in enumerate(iters):
            g = groups[gi]
            t0 = tq * 512
            b = t0 // S
            pos = t0 % S
            for j, goc in enumerate(g):
                if not is_v(goc):
                    h = goc - 32
                    _wge(e, EVB, sq + 1)
                    e.dma_start(out=kf_hbm[b, h, :, pos:pos + 512],
                                in_=STQ[sq % 2][:]
                                ).then_inc(*_ti(SQK[sq % 2], 16))
                    sq += 1
                else:
                    h = goc - 64
                    _wge(e, EV2, ev2 + 1)
                    dst = vf_hbm[b, h, pos:pos + 512, :]
                    e.dma_start(out=dst.rearrange("(j p) d -> p j d", j=4),
                                in_=VST2[ev2 % 2][:]
                                ).then_inc(*_ti(VS[ev2 % 2], 16))
                    ev2 += 1
            if q + 2 < NQ:
                load(q + 2)
        _wge(e, SQK[0], 16 * ((sq + 1) // 2))
        _wge(e, SQK[1], 16 * (sq // 2))
        _wge(e, VS[0], 16 * ((ev2 + 1) // 2))
        _wge(e, VS[1], 16 * (ev2 // 2))

    @blk.scalar
    def _(e):
        kwl = 0
        for gi, g in enumerate(groups):
            w = len(g) * 128
            for ic in range(NCH):
                _wge(e, LW[kwl % 2], 16 * (kwl // 2 + 1))
                if gi >= 2:
                    _wge(e, MM, NCH * NTT * (gi - 1))
                e.copy(out=WGB[gi % 2][:, ic, 0:w],
                       in_=WF[kwl % 2][:, 0:w]).then_inc(*_ti(CVW, 1))
                kwl += 1

    @blk.tensor
    def _(e):
        _wge(e, IDS, 2)
        trv = 0
        for q, (gi, tq) in enumerate(iters):
            g = groups[gi]
            if tq == 0:
                _wge(e, CVW, NCH * (gi + 1))
            _wge(e, XLD[q % 2], 16 * (q // 2 + 1))
            if q >= 2:
                _wge(e, EVB, kev_after[q - 2])
                _wge(e, VEV, vev_after[q - 2])
            for ic in range(NCH):
                for j, goc in enumerate(g):
                    ins = e.matmul(
                        PS[j * 2 + q % 2][:],
                        WGB[gi % 2][:, ic, j * 128:(j + 1) * 128],
                        XNF[q % 2][:, ic, :],
                        start=(ic == 0), stop=(ic == NCH - 1))
                ins.then_inc(*_ti(MM, 1))
            if is_v(g[0]):
                for idx in range(len(g)):
                    vi = trv
                    _wge(e, VEV, vi + 1)
                    if vi >= 2:
                        _wge(e, EV2, vi - 1)
                    for j4 in range(4):
                        ins = e.transpose(
                            PTV[vi % 2][:, j4 * 128:(j4 + 1) * 128],
                            VSTG[vi % 2][:, j4 * 128:(j4 + 1) * 128],
                            idt[:])
                    ins.then_inc(*_ti(TRV, 1))
                    trv += 1

    @blk.vector
    def _(e):
        _wge(e, BS_, 16 * 64)
        sq = 0
        vev = 0
        ev2 = [0]

        def emit_evac2():
            vi = ev2[0]
            _wge(e, TRV, vi + 1)
            _wge(e, VS[vi % 2], 16 * (vi // 2))
            e.tensor_copy(out=VST2[vi % 2][:],
                          in_=PTV[vi % 2][:, 0:512]).then_inc(*_ti(EV2, 1))
            ev2[0] += 1

        for q, (gi, tq) in enumerate(iters):
            g = groups[gi]
            _wge(e, MM, NCH * (q + 1))
            for j, goc in enumerate(g):
                if not is_v(goc):
                    _wge(e, SQK[sq % 2], 16 * (sq // 2))
                    e.tensor_scalar(
                        out=STQ[sq % 2][:], in0=PS[j * 2 + q % 2][:],
                        scalar1=BIAS[:, goc - 32:goc - 31], scalar2=None,
                        op0=ALU.add).then_inc(*_ti(EVB, 1))
                    sq += 1
                else:
                    _wge(e, TRV, max(0, vev - 1))
                    e.tensor_scalar(
                        out=VSTG[vev % 2][:], in0=PS[j * 2 + q % 2][:],
                        scalar1=BIAS[:, goc - 32:goc - 31], scalar2=None,
                        op0=ALU.add).then_inc(*_ti(VEV, 1))
                    vev += 1
                    if vev >= 3:
                        emit_evac2()
        while ev2[0] < vev:
            emit_evac2()
    sc.close()


def build_fused(nph=10):
    nc = bass.Bass("TRN2", target_bir_lowering=False, debug=False)
    BS_ = B * S
    xfull = nc.dram_tensor("xfull", [BS_, D], F32, kind="ExternalInput").ap()
    xs = nc.dram_tensor("xs", [TC, D], F32, kind="ExternalInput").ap()
    qkvw = nc.dram_tensor("qkvw", [D, 3 * D], F32, kind="ExternalInput").ap()
    qkvb = nc.dram_tensor("qkvb", [3 * D], F32, kind="ExternalInput").ap()
    ow = nc.dram_tensor("ow", [D, D], F32, kind="ExternalInput").ap()
    ob = nc.dram_tensor("ob", [D], F32, kind="ExternalInput").ap()
    w0 = nc.dram_tensor("w0", [D, D], F32, kind="ExternalInput").ap()
    b0 = nc.dram_tensor("b0", [D], F32, kind="ExternalInput").ap()
    gw0 = nc.dram_tensor("gw0", [8, D], F32, kind="ExternalInput").ap()
    ics = nc.dram_tensor("ics", [8, D], F32, kind="ExternalInput").ap()
    ocs = nc.dram_tensor("ocs", [8, D], F32, kind="ExternalInput").ap()
    wqkv1 = nc.dram_tensor("wqkv1", [6144, D], F32, kind="ExternalInput").ap()
    wproj1 = nc.dram_tensor("wproj1", [D, D], F32, kind="ExternalInput").ap()
    gq = nc.dram_tensor("gq", [512, INTER], I32, kind="ExternalInput").ap()
    gs = nc.dram_tensor("gs", [32, INTER], F16, kind="ExternalInput").ap()
    gz = nc.dram_tensor("gz", [32, 1376], I32, kind="ExternalInput").ap()
    uq = nc.dram_tensor("uq", [512, INTER], I32, kind="ExternalInput").ap()
    us = nc.dram_tensor("us", [32, INTER], F16, kind="ExternalInput").ap()
    uz = nc.dram_tensor("uz", [32, 1376], I32, kind="ExternalInput").ap()
    dq = nc.dram_tensor("dq", [1376, D], I32, kind="ExternalInput").ap()
    ds = nc.dram_tensor("ds", [86, D], F16, kind="ExternalInput").ap()
    dz = nc.dram_tensor("dz", [86, 512], I32, kind="ExternalInput").ap()
    out = nc.dram_tensor("out", [TC, D], F32, kind="ExternalOutput").ap()

    xnTf = nc.dram_tensor("i_xnTf", [NCH, 128, BS_], BF16, kind="Internal").ap()
    xnT0 = nc.dram_tensor("i_xnT0", [NCH, 128, TC], BF16, kind="Internal").ap()
    qscr = nc.dram_tensor("i_q", [NCH, 128, TC], BF16, kind="Internal").ap()
    kf = nc.dram_tensor("i_kf", [B, 32, 128, S], BF16, kind="Internal").ap()
    vf = nc.dram_tensor("i_vf", [B, 32, S, 128], BF16, kind="Internal").ap()
    gscr = nc.dram_tensor("i_g", [NCH, 128, TC], BF16, kind="Internal").ap()
    ctxs = nc.dram_tensor("i_ctx", [NCH, 128, TC], BF16, kind="Internal").ap()
    x1s = nc.dram_tensor("i_x1", [TC, D], F32, kind="Internal").ap()
    xn1s = nc.dram_tensor("i_xn1", [NCH, 128, TC], BF16, kind="Internal").ap()
    s1s = nc.dram_tensor("i_s1", [NCH, 128, TC], BF16, kind="Internal").ap()
    g1s = nc.dram_tensor("i_g1", [NCH, 128, TC], F16, kind="Internal").ap()

    _gbar_setup(nc)
    with nc.Block() as blk:
        def phase(fn):
            fn()
            emit_barrier(nc, blk)
            nc._vsem_pool.retire_phase()

        if nph == -2:
            phases = [
                lambda: emit_ln(nc, blk, xs, xnT0, False, "p1"),
                lambda: emit_ln(nc, blk, xs, xnT0, False, "p2b"),
            ]
            for fn in phases:
                phase(fn)
            nph = 0
        if nph == -5:
            def tiny():
                TS5 = _vs(nc, "t5")
                TT5 = nc.alloc_sbuf_tensor("t5t", [128, 64], F32)
                @blk.vector
                def _(e):
                    e.memset(TT5[:], 1.0).then_inc(*_ti(TS5, 1))
                @blk.gpsimd
                def _(e):
                    _wge(e, TS5, 1)
                    e.dma_start(out=xnT0[0, :, 0:64].transpose([0, 1]),
                                in_=TT5[:].bitcast(BF16)[:, 0:64]
                                ).then_inc(*_ti(TS5, 16))
                    _wge(e, TS5, 17)
            phase(tiny)
            phase(lambda: emit_ln(nc, blk, xfull, xnTf, False, "p2",
                                  ntok=BS_))
            nph = 0
        if nph == -6:
            emit_ln(nc, blk, xs, xnT0, False, "p1")
            emit_barrier(nc, blk)
            emit_ln(nc, blk, xfull, xnTf, False, "p2", ntok=BS_)
            emit_barrier(nc, blk)
            nph = 0
        if nph == -7:
            phase(lambda: emit_ln(nc, blk, xfull, xnTf, False, "p2",
                                  ntok=BS_))
            phase(lambda: emit_qkvkv(nc, blk, xnTf, qkvw, qkvb, kf, vf, "p4"))
            nph = 0
        if nph == -8:
            phase(lambda: emit_qkvkv(nc, blk, xnTf, qkvw, qkvb, kf, vf, "p4"))
            nph = 0
        if nph == -9:
            phase(lambda: emit_final(nc, blk, x1s, s1s, g1s, out, "p10"))
            nph = 0
        if nph == -10:
            phase(lambda: emit_ffn1(nc, blk, xn1s, gq, gs, gz, uq, us, uz,
                                    dq, ds, dz, g1s, "p9"))
            phase(lambda: emit_final(nc, blk, x1s, s1s, g1s, out, "p10"))
            nph = 0
        if nph == -3:
            phase(lambda: emit_ln(nc, blk, xfull, xnTf, False, "p2",
                                  ntok=BS_))
            nph = 0
        phases = [
            lambda: emit_ln(nc, blk, xs, xnT0, False, "p1"),
            lambda: emit_ln(nc, blk, xfull, xnTf, False, "p2", ntok=BS_),
            lambda: emit_qkv0(nc, blk, xnT0, qkvw, qkvb, qscr, None, None,
                              "p3", q_only=True),
            lambda: emit_qkvkv(nc, blk, xnTf, qkvw, qkvb, kf, vf, "p4"),
            lambda: emit_ffn0(nc, blk, xnT0, gw0, ics, ocs, w0, b0, gscr,
                              "p5"),
            lambda: emit_attn0(nc, blk, qscr, kf, vf, ctxs, "p6"),
            lambda: emit_combine0(nc, blk, ctxs, ow, ob, gscr, xs, x1s,
                                  xn1s, "p7"),
            lambda: emit_l1attn(nc, blk, xn1s, wqkv1, wproj1, s1s, "p8"),
            lambda: emit_ffn1(nc, blk, xn1s, gq, gs, gz, uq, us, uz,
                              dq, ds, dz, g1s, "p9"),
            lambda: emit_final(nc, blk, x1s, s1s, g1s, out, "p10"),
        ]
        for fn in phases[:nph]:
            phase(fn)
        if nph < 10:
            ZS = nc.alloc_semaphore("zf_s")
            ZD = nc.alloc_semaphore("zf_d")
            ztile = nc.alloc_sbuf_tensor("zf_t", [128, D], F32)

            @blk.vector
            def _(e):
                e.memset(ztile[:], 0.0).then_inc(ZS, 1)

            @blk.gpsimd
            def _(e):
                e.wait_ge(ZS, 1)
                for tt in range(TC // 128):
                    e.dma_start(out=out[tt * 128:(tt + 1) * 128, :],
                                in_=ztile[:]).then_inc(ZD, 16)
                e.wait_ge(ZD, 16 * (TC // 128))
            nc.all_engine_barrier()
    return nc



_nc_cache = {}


def _get_nc(key, builder):
    if key not in _nc_cache:
        _nc_cache[key] = builder()
    return _nc_cache[key]


def _run(key, builder, in_maps):
    nc = _get_nc(key, builder)
    res = run_bass_kernel_spmd(nc, in_maps, core_ids=list(range(NCORES)))
    return res.results


def kernel(L0_attn_x, L0_attn_attn_qkvw, L0_attn_attn_qkvb, L0_attn_attn_ow,
           L0_attn_attn_ob, L0_ffn_weight, L0_ffn_bias, L0_ffn_gate_w,
           L0_ffn_in_channel_scale, L0_ffn_out_channel_scale,
           L1_attn_weight_qkv, L1_attn_weight_proj,
           L1_ffn_gate_qweight, L1_ffn_gate_scales, L1_ffn_gate_zeros,
           L1_ffn_up_qweight, L1_ffn_up_scales, L1_ffn_up_zeros,
           L1_ffn_down_qweight, L1_ffn_down_scales, L1_ffn_down_zeros):
    f32 = lambda a: np.ascontiguousarray(np.asarray(a), dtype=np.float32)
    i32 = lambda a: np.ascontiguousarray(np.asarray(a), dtype=np.int32)
    f16 = lambda a: np.ascontiguousarray(np.asarray(a), dtype=np.float16)

    x = f32(L0_attn_x)
    xfull = np.ascontiguousarray(x.reshape(B * S, D))
    xs = [np.ascontiguousarray(
        x[:, c * PC:(c + 1) * PC, :].reshape(TC, D)) for c in range(NCORES)]

    common = {
        "xfull": xfull,
        "qkvw": f32(L0_attn_attn_qkvw), "qkvb": f32(L0_attn_attn_qkvb),
        "ow": f32(L0_attn_attn_ow), "ob": f32(L0_attn_attn_ob),
        "w0": f32(L0_ffn_weight), "b0": f32(L0_ffn_bias),
        "gw0": f32(L0_ffn_gate_w), "ics": f32(L0_ffn_in_channel_scale),
        "ocs": f32(L0_ffn_out_channel_scale),
        "wqkv1": f32(L1_attn_weight_qkv), "wproj1": f32(L1_attn_weight_proj),
        "gq": i32(L1_ffn_gate_qweight), "gs": f16(L1_ffn_gate_scales),
        "gz": i32(L1_ffn_gate_zeros),
        "uq": i32(L1_ffn_up_qweight), "us": f16(L1_ffn_up_scales),
        "uz": i32(L1_ffn_up_zeros),
        "dq": i32(L1_ffn_down_qweight), "ds": f16(L1_ffn_down_scales),
        "dz": i32(L1_ffn_down_zeros),
    }
    in_maps = [dict(common, xs=xs[c]) for c in range(NCORES)]

    nc = _get_nc("fused", build_fused)
    res = run_bass_kernel_spmd(nc, in_maps, core_ids=list(range(NCORES)))
    out = np.empty((B, S, D), np.float32)
    for c in range(NCORES):
        out[:, c * PC:(c + 1) * PC, :] = np.asarray(
            res.results[c]["out"]).reshape(B, PC, D)
    return out


# version 05
# speedup vs baseline: 1.2774x; 1.2774x over previous
import math
import numpy as np
import ml_dtypes
import concourse.bass as bass
import concourse.mybir as mybir
from concourse.bass_utils import run_bass_kernel_spmd

F32 = mybir.dt.float32
BF16 = mybir.dt.bfloat16
F16 = mybir.dt.float16
I32 = mybir.dt.int32
AF = mybir.ActivationFunctionType
ALU = mybir.AluOpType
AX = mybir.AxisListType

NCORES = 8
B = 4
S = 2048
D = 4096
PC = S // NCORES
TC = B * PC
NCH = D // 128
EPS = 1e-5
ISQ = 1.0 / math.sqrt(128.0)

bf16 = ml_dtypes.bfloat16


from contextlib import ExitStack


class _Scope:

    def __init__(self, nc):
        self.nc = nc
        self.st = ExitStack()

    def sb(self, name, shape, dt):
        return self.st.enter_context(self.nc.sbuf_tensor(name, shape, dt))

    def ps(self, name, shape, dt=F32):
        return self.st.enter_context(self.nc.psum_tensor(name, shape, dt))

    def close(self):
        self.st.close()


class VSem:

    __slots__ = ("h", "base", "emitted")

    def __init__(self, h):
        self.h = h
        self.base = 0
        self.emitted = 0


class VSemPool:
    def __init__(self, nc):
        self.nc = nc
        self.free = []
        self.inuse = []

    def get(self, name, swdge=False):
        if swdge:
            return VSem(self.nc.alloc_semaphore(name))
        if self.free:
            v = self.free.pop()
        else:
            v = VSem(self.nc.alloc_semaphore(name))
        self.inuse.append(v)
        return v

    def retire_phase(self):
        for v in self.inuse:
            v.base += v.emitted
            v.emitted = 0
        self.free.extend(self.inuse)
        self.inuse = []


def _vs(nc, name, swdge=False):
    if not hasattr(nc, "_vsem_pool"):
        nc._vsem_pool = VSemPool(nc)
    return nc._vsem_pool.get(name, swdge)


def _ti(S, n):
    S.emitted += n
    return (S.h, n)


def _wge(e, S, v):
    e.wait_ge(S.h, S.base + v)


def _gbar_setup(nc):
    if not hasattr(nc, "_gbar"):
        nc._gbar = [nc.alloc_semaphore(f"GBAR{i}") for i in range(5)]
        nc._gbar_n = 0
        nc._gbar_dum = nc.alloc_sbuf_tensor("gbar_dum", [1, 8], F32)
        nc._gbar_dmb = nc.alloc_sbuf_tensor("gbar_dmb", [1, 8], BF16)


def emit_barrier(nc, blk):
    if not hasattr(nc, "_gbar"):
        nc._gbar = nc.alloc_semaphore("GBAR")
        nc._gbar_n = 0
        nc._gbar_dum = nc.alloc_sbuf_tensor("gbar_dum", [1, 8], F32)
        nc._gbar_dmb = nc.alloc_sbuf_tensor("gbar_dmb", [1, 8], BF16)
    _gbar_setup(nc)
    first = nc._gbar_n == 0
    nc._gbar_n += 1
    n = nc._gbar_n
    GV, GA, GP, GT, GS_ = nc._gbar
    DUM = nc._gbar_dum

    def wait_all(e):
        for g_ in nc._gbar:
            e.wait_ge(g_, n)

    @blk.vector
    def _(e):
        if first:
            e.memset(nc._gbar_dmb[:], 1.0)
        e.memset(DUM[:, 0:2], 0.0).then_inc(GV, 1)
        wait_all(e)

    @blk.scalar
    def _(e):
        e.memzero(DUM[:, 2:4]).then_inc(GA, 1)
        wait_all(e)

    @blk.gpsimd
    def _(e):
        e.memset(DUM[:, 4:6], 0.0).then_inc(GP, 1)
        wait_all(e)

    with nc.psum_tensor(f"gbar_dps{n}", [1, 8], F32) as dps:
        @blk.tensor
        def _(e):
            e.wait_ge(GV, n)
            e.wait_ge(GA, n)
            e.wait_ge(GP, n)
            e.matmul(dps[0:1, 0:1], nc._gbar_dmb[0:1, 0:1],
                     nc._gbar_dmb[0:1, 0:1], start=True,
                     stop=True).then_inc(GT, 1)
            wait_all(e)

    @blk.sync
    def _(e):
        e.sem_inc(GS_, 1)
        wait_all(e)



def _ident(nc, blk, sem, sc, tag=""):
    iot = sc.sb(f"{tag}identiota", [128, 128], I32)
    idt = sc.sb(f"{tag}identity", [128, 128], BF16)

    @blk.gpsimd
    def _(g):
        g.iota(iot[:], pattern=[[1, 128]], base=0,
               channel_multiplier=-1).then_inc(*_ti(sem, 1))
        _wge(g, sem, 1)
        g.tensor_scalar(out=idt[:], in0=iot[:], scalar1=0, scalar2=None,
                        op0=ALU.is_equal).then_inc(*_ti(sem, 1))

    return idt


def _load_bias_pp(nc, blk, sem, hbm_vec, n_chunks, name):
    t = nc.alloc_sbuf_tensor(name, [128, n_chunks], F32)

    @blk.sync
    def _(e):
        for c in range(n_chunks):
            src = hbm_vec[c * 128:(c + 1) * 128]
            e.dma_start(out=t[:, c:c + 1], in_=src[:, None]).then_inc(sem, 16)

    return t


def emit_ln(nc, blk, x_hbm, xnT_hbm, pi_order, tag, ntok=TC):
    sc = _Scope(nc)
    LS = [_vs(nc, f"{tag}_LS{i}") for i in range(2)]
    VA = _vs(nc, f"{tag}_VA")
    SA = _vs(nc, f"{tag}_SA")
    TS = _vs(nc, f"{tag}_TS")
    ES = _vs(nc, f"{tag}_ES")
    SS = [_vs(nc, f"{tag}_SS{i}", swdge=True) for i in range(2)]
    IDS = _vs(nc, f"{tag}_IDS")
    VC = _vs(nc, f"{tag}_VC")
    AC = _vs(nc, f"{tag}_AC")

    idt = _ident(nc, blk, IDS, sc, tag)

    XT = [sc.sb(f"{tag}_xt{i}", [128, D], F32) for i in range(2)]
    SQ = sc.sb(f"{tag}_sq", [128, D], F32)
    XN = [sc.sb(f"{tag}_xn{i}", [128, D], BF16) for i in range(2)]
    SUM = sc.sb(f"{tag}_sum", [128, 1], F32)
    SSUM = sc.sb(f"{tag}_ssum", [128, 1], F32)
    M = sc.sb(f"{tag}_m", [128, 1], F32)
    MS = sc.sb(f"{tag}_ms", [128, 1], F32)
    VAR = sc.sb(f"{tag}_var", [128, 1], F32)
    VAR2 = sc.sb(f"{tag}_var2", [128, 1], F32)
    LV = sc.sb(f"{tag}_lv", [128, 1], F32)
    RSTD = sc.sb(f"{tag}_rstd", [128, 1], F32)
    EPST = sc.sb(f"{tag}_epst", [128, 1], F32)
    PT = [sc.ps(f"{tag}_pt{i}", [128, 1024], BF16) for i in range(4)]
    STG = [sc.sb(f"{tag}_stg{i}", [128, 1024], BF16) for i in range(2)]

    NT = ntok // 128
    NG = 4

    @blk.sync
    def _(e):
        for tt in range(NT):
            if tt >= 2:
                _wge(e, VA, 3 * (tt - 2) + 3)
            e.dma_start(out=XT[tt % 2][:],
                        in_=x_hbm[tt * 128:(tt + 1) * 128, :]).then_inc(*_ti(LS[tt % 2], 16))

    @blk.gpsimd
    def _(e):
        total = NT * NG * 16
        for tt in range(NT):
            for g in range(NG):
                k = tt * NG + g
                _wge(e, ES, k + 1)
                dst = xnT_hbm[g * 8:(g + 1) * 8, :,
                              tt * 128:(tt + 1) * 128].transpose([1, 0, 2])
                e.dma_start(
                    out=dst,
                    in_=STG[k % 2][:].rearrange("p (c t) -> p c t", c=8),
                ).then_inc(*_ti(SS[k % 2], 16))
        _wge(e, SS[0], total // 2)
        _wge(e, SS[1], total // 2)

    @blk.scalar
    def _(e):
        for tt in range(NT):
            _wge(e, LS[tt % 2], 16 * (tt // 2 + 1))
            e.activation(out=SQ[:], in_=XT[tt % 2][:], func=AF.Square,
                         accum_out=SSUM[:]).then_inc(*_ti(SA, 1))
            _wge(e, VA, 3 * tt + 2)
            e.activation(out=LV[:], in_=VAR2[:], func=AF.Ln,
                         bias=EPST[:]).then_inc(*_ti(AC, 1))
            _wge(e, AC, tt + 1)
            e.activation(out=RSTD[:], in_=LV[:], func=AF.Exp,
                         scale=-0.5).then_inc(*_ti(SA, 1))
            if tt > 0:
                for g in range(NG):
                    k = (tt - 1) * NG + g
                    _wge(e, TS, k + 1)
                    if k >= 2:
                        _wge(e, SS[k % 2], 16 * (k // 2))
                    e.copy(out=STG[k % 2][:], in_=PT[k % 4][:]).then_inc(*_ti(ES, 1))
        for g in range(NG):
            k = (NT - 1) * NG + g
            _wge(e, TS, k + 1)
            _wge(e, SS[k % 2], 16 * (k // 2))
            e.copy(out=STG[k % 2][:], in_=PT[k % 4][:]).then_inc(*_ti(ES, 1))

    @blk.vector
    def _(e):
        e.memset(EPST[:], float(EPS))
        for tt in range(NT):
            _wge(e, LS[tt % 2], 16 * (tt // 2 + 1))
            e.reduce_sum(out=SUM[:], in_=XT[tt % 2][:], axis=AX.X).then_inc(*_ti(VA, 1))
            _wge(e, VA, 3 * tt + 1)
            _wge(e, SA, 2 * tt + 1)
            e.tensor_scalar(out=M[:], in0=SUM[:], scalar1=1.0 / D, scalar2=None,
                            op0=ALU.mult)
            e.tensor_scalar(out=MS[:], in0=SSUM[:], scalar1=1.0 / D, scalar2=None,
                            op0=ALU.mult).then_inc(*_ti(VC, 1))
            _wge(e, VC, 3 * tt + 1)
            e.tensor_tensor(out=VAR[:], in0=M[:], in1=M[:],
                            op=ALU.mult).then_inc(*_ti(VC, 1))
            _wge(e, VC, 3 * tt + 2)
            e.tensor_sub(VAR2[:], MS[:], VAR[:]).then_inc(*_ti(VA, 1))
            e.sem_inc(*_ti(VC, 1))
            _wge(e, SA, 2 * tt + 2)
            if tt >= 2:
                _wge(e, TS, NG * (tt - 1))
            e.tensor_scalar(out=XN[tt % 2][:], in0=XT[tt % 2][:], scalar1=M[:],
                            scalar2=RSTD[:], op0=ALU.subtract,
                            op1=ALU.mult).then_inc(*_ti(VA, 1))

    @blk.tensor
    def _(e):
        _wge(e, IDS, 2)
        for tt in range(NT):
            _wge(e, VA, 3 * tt + 3)
            for g in range(NG):
                k = tt * NG + g
                if k >= 4:
                    _wge(e, ES, k - 3)
                for cc in range(8):
                    c = g * 8 + cc
                    if pi_order:
                        Tb, j = c // 8, c % 8
                        src = XN[tt % 2][:].rearrange("p (T m j) -> p T m j",
                                                      T=4, m=128, j=8)[:, Tb, :, j]
                    else:
                        src = XN[tt % 2][:, c * 128:(c + 1) * 128]
                    ins = e.transpose(PT[k % 4][:, cc * 128:(cc + 1) * 128],
                                      src, idt[:])
                ins.then_inc(*_ti(TS, 1))
    sc.close()



def build_ln(pi_order, tag="ln"):
    nc = bass.Bass("TRN2", target_bir_lowering=False, debug=False)
    x = nc.dram_tensor("x", [TC, D], F32, kind="ExternalInput").ap()
    xnT = nc.dram_tensor("xnT", [NCH, 128, TC], BF16, kind="ExternalOutput").ap()
    with nc.Block() as blk:
        emit_ln(nc, blk, x, xnT, pi_order, tag)
    return nc


def emit_qkv0(nc, blk, xnT_hbm, w_hbm, b_hbm, q_hbm, k_hbm, vtm_hbm, tag,
              q_only=False):
    sc = _Scope(nc)
    XL = _vs(nc, f"{tag}_XL")
    BS = _vs(nc, f"{tag}_BS")
    IDS = _vs(nc, f"{tag}_IDS")
    LW = [_vs(nc, f"{tag}_LW{i}") for i in range(2)]
    CV = _vs(nc, f"{tag}_CV")
    MM = _vs(nc, f"{tag}_MM")
    MME = _vs(nc, f"{tag}_MME")
    EVB = _vs(nc, f"{tag}_EVB")
    SQK = [_vs(nc, f"{tag}_SQK{i}", swdge=True) for i in range(2)]
    VEV = _vs(nc, f"{tag}_VEV")
    TRV = _vs(nc, f"{tag}_TRV")
    EV2 = _vs(nc, f"{tag}_EV2")
    VS = [_vs(nc, f"{tag}_VS{i}", swdge=True) for i in range(2)]

    idt = _ident(nc, blk, IDS, sc, tag)

    XNT = sc.sb(f"{tag}_xnt", [128, NCH, TC], BF16)
    BIAS = sc.sb(f"{tag}_bias", [128, 96], F32)
    WF = [sc.sb(f"{tag}_wf{i}", [128, 512], F32) for i in range(2)]
    WB = [sc.sb(f"{tag}_wb{i}", [128, 512], BF16) for i in range(2)]
    STQ = [sc.sb(f"{tag}_stq{i}", [128, 512], BF16) for i in range(2)]
    VSTG = [sc.sb(f"{tag}_vstg{i}", [128, 512], BF16) for i in range(2)]
    VST2 = [sc.sb(f"{tag}_vst2{i}", [128, 4, 128], BF16)
            for i in range(2)]
    PS = [sc.ps(f"{tag}_ps{i}", [128, 512], F32) for i in range(8)]
    PTV = [PS[6][:].bitcast(BF16), PS[7][:].bitcast(BF16)]

    if q_only:
        groups = [list(range(g * 4, g * 4 + 4)) for g in range(8)]
    else:
        groups = [list(range(g * 4, g * 4 + 4)) for g in range(16)]
        vstart = [64 + 3 * g for g in range(10)] + [94]
        groups += [list(range(s, min(s + 3, 96))) for s in vstart]

    def is_v(goc):
        return goc >= 64

    nbank = [len(g) * 2 for g in groups]
    kctr = 0

    @blk.sync
    def _(e):
        e.dma_start(out=XNT[:], in_=xnT_hbm.transpose([1, 0, 2])).then_inc(*_ti(XL, 16))
        for c in range(96):
            e.dma_start(out=BIAS[:, c:c + 1],
                        in_=b_hbm[c * 128:(c + 1) * 128][:, None]).then_inc(*_ti(BS, 16))
        k = 0
        for gi, g in enumerate(groups):
            c0, c1 = g[0] * 128, (g[-1] + 1) * 128
            for ic in range(NCH):
                if k >= 2:
                    _wge(e, CV, k - 1)
                e.dma_start(out=WF[k % 2][:, 0:(c1 - c0)],
                            in_=w_hbm[ic * 128:(ic + 1) * 128, c0:c1]
                            ).then_inc(*_ti(LW[k % 2], 16))
                k += 1

    @blk.scalar
    def _(e):
        k = 0
        ev2 = 0
        for gi, g in enumerate(groups):
            w = len(g) * 128
            for ic in range(NCH):
                _wge(e, LW[k % 2], 16 * (k // 2 + 1))
                if k >= 2:
                    _wge(e, MM, k - 1)
                e.copy(out=WB[k % 2][:, 0:w], in_=WF[k % 2][:, 0:w]).then_inc(*_ti(CV, 1))
                k += 1
            if is_v(g[0]):
                for idx in range(len(g) * 2):
                    _wge(e, TRV, ev2 + 1)
                    vi = ev2
                    _wge(e, VS[vi % 2], 16 * (vi // 2))
                    e.copy(out=VST2[vi % 2][:],
                           in_=PTV[vi % 2][:, 0:512]).then_inc(*_ti(EV2, 1))
                    ev2 += 1

    @blk.tensor
    def _(e):
        _wge(e, XL, 16)
        _wge(e, IDS, 2)
        k = 0
        qk_done = 0
        v_done = 0
        trv = 0
        for gi, g in enumerate(groups):
            _wge(e, EVB, qk_done)
            _wge(e, VEV, v_done)
            for ic in range(NCH):
                _wge(e, CV, k + 1)
                for j, goc in enumerate(g):
                    for tt in range(2):
                        ins = e.matmul(
                            PS[j * 2 + tt][:],
                            WB[k % 2][:, j * 128:(j + 1) * 128],
                            XNT[:, ic, tt * 512:(tt + 1) * 512],
                            start=(ic == 0), stop=(ic == NCH - 1))
                ins.then_inc(*_ti(MM, 1))
                k += 1
            for goc in g:
                if is_v(goc):
                    v_done += 2
                else:
                    qk_done += 2
            if is_v(g[0]):
                for idx in range(len(g) * 2):
                    vi = trv
                    _wge(e, VEV, vi + 1)
                    if vi >= 2:
                        _wge(e, EV2, vi - 1)
                    for j4 in range(4):
                        ins = e.transpose(
                            PTV[vi % 2][:, j4 * 128:(j4 + 1) * 128],
                            VSTG[vi % 2][:, j4 * 128:(j4 + 1) * 128], idt[:])
                    ins.then_inc(*_ti(TRV, 1))
                    trv += 1

    @blk.vector
    def _(e):
        _wge(e, BS, 16 * 96)
        sq = 0
        vev = 0
        for gi, g in enumerate(groups):
            _wge(e, MM, NCH * (gi + 1))
            for j, goc in enumerate(g):
                for tt in range(2):
                    if not is_v(goc):
                        _wge(e, SQK[sq % 2], 16 * (sq // 2))
                        if goc < 32:
                            e.tensor_scalar(
                                out=STQ[sq % 2][:], in0=PS[j * 2 + tt][:],
                                scalar1=BIAS[:, goc:goc + 1], scalar2=float(ISQ),
                                op0=ALU.add, op1=ALU.mult).then_inc(*_ti(EVB, 1))
                        else:
                            e.tensor_scalar(
                                out=STQ[sq % 2][:], in0=PS[j * 2 + tt][:],
                                scalar1=BIAS[:, goc:goc + 1], scalar2=None,
                                op0=ALU.add).then_inc(*_ti(EVB, 1))
                        sq += 1
                    else:
                        _wge(e, TRV, max(0, vev - 1))
                        e.tensor_scalar(
                            out=VSTG[vev % 2][:], in0=PS[j * 2 + tt][:],
                            scalar1=BIAS[:, goc:goc + 1], scalar2=None,
                            op0=ALU.add).then_inc(*_ti(VEV, 1))
                        vev += 1

    @blk.gpsimd
    def _(e):
        sq = 0
        ev2 = 0
        for gi, g in enumerate(groups):
            for j, goc in enumerate(g):
                for tt in range(2):
                    if not is_v(goc):
                        _wge(e, EVB, sq + 1)
                        dst = (q_hbm if goc < 32 else k_hbm)
                        c = goc if goc < 32 else goc - 32
                        e.dma_start(
                            out=dst[c, :, tt * 512:(tt + 1) * 512],
                            in_=STQ[sq % 2][:]).then_inc(*_ti(SQK[sq % 2], 16))
                        sq += 1
                    else:
                        _wge(e, EV2, ev2 + 1)
                        c = goc - 64
                        dst = vtm_hbm[tt * 512:(tt + 1) * 512,
                                      c * 128:(c + 1) * 128]
                        e.dma_start(
                            out=dst.rearrange("(j p) d -> p j d", j=4),
                            in_=VST2[ev2 % 2][:]).then_inc(*_ti(VS[ev2 % 2], 16))
                        ev2 += 1
        _wge(e, SQK[0], 16 * ((sq + 1) // 2))
        _wge(e, SQK[1], 16 * (sq // 2))
        _wge(e, VS[0], 16 * ((ev2 + 1) // 2))
        _wge(e, VS[1], 16 * (ev2 // 2))
    sc.close()



def build_qkv0(tag="qkv0"):
    nc = bass.Bass("TRN2", target_bir_lowering=False, debug=False)
    xnT = nc.dram_tensor("xnT", [NCH, 128, TC], BF16, kind="ExternalInput").ap()
    w = nc.dram_tensor("w", [D, 3 * D], F32, kind="ExternalInput").ap()
    b = nc.dram_tensor("b", [3 * D], F32, kind="ExternalInput").ap()
    q = nc.dram_tensor("q", [NCH, 128, TC], BF16, kind="ExternalOutput").ap()
    k = nc.dram_tensor("k", [NCH, 128, TC], BF16, kind="ExternalOutput").ap()
    vtm = nc.dram_tensor("vtm", [TC, D], BF16, kind="ExternalOutput").ap()
    with nc.Block() as blk:
        emit_qkv0(nc, blk, xnT, w, b, q, k, vtm, tag)
    return nc


def emit_ffn0(nc, blk, xnT_hbm, gw_hbm, ics_hbm, ocs_hbm, w_hbm, bias_hbm,
              g_hbm, tag):
    sc = _Scope(nc)
    XL = _vs(nc, f"{tag}_XL")
    GL = _vs(nc, f"{tag}_GL")
    BS = _vs(nc, f"{tag}_BS")
    IDS = _vs(nc, f"{tag}_IDS")
    CVS = _vs(nc, f"{tag}_CVS")
    RTR = _vs(nc, f"{tag}_RTR")
    VRT = _vs(nc, f"{tag}_VRT")
    ART = _vs(nc, f"{tag}_ART")
    USM = _vs(nc, f"{tag}_USM")
    UEV = _vs(nc, f"{tag}_UEV")
    LW2 = [_vs(nc, f"{tag}_LW2{i}") for i in range(2)]
    SGN = _vs(nc, f"{tag}_SGN")
    TRW = _vs(nc, f"{tag}_TRW")
    PWEV = _vs(nc, f"{tag}_PWEV")
    MM2 = _vs(nc, f"{tag}_MM2")
    POEV = _vs(nc, f"{tag}_POEV")
    YEV = _vs(nc, f"{tag}_YEV")
    SIL = _vs(nc, f"{tag}_SIL")
    ST2 = [_vs(nc, f"{tag}_ST2{i}", swdge=True) for i in range(2)]

    idt = _ident(nc, blk, IDS, sc, tag)
    iotf = sc.sb(f"{tag}_iotf", [128, 128], I32)
    idf = sc.sb(f"{tag}_idf", [128, 128], F32)

    @blk.gpsimd
    def _(g):
        g.iota(iotf[:], pattern=[[1, 128]], base=0,
               channel_multiplier=-1).then_inc(*_ti(IDS, 1))
        _wge(g, IDS, 3)
        g.tensor_scalar(out=idf[:], in0=iotf[:], scalar1=0, scalar2=None,
                        op0=ALU.is_equal).then_inc(*_ti(IDS, 1))

    OCSB = sc.sb(f"{tag}_ocsb", [8, D], BF16)
    GWT = sc.sb(f"{tag}_gwt", [128, NCH, 8], BF16)
    BIAS = sc.sb(f"{tag}_bias", [128, NCH], F32)
    RLS = sc.sb(f"{tag}_rls", [8, TC], F32)
    NMX = sc.sb(f"{tag}_nmx", [128, 1], F32)
    ZT = sc.sb(f"{tag}_zt", [128, 1], F32)
    RZ = sc.sb(f"{tag}_rz", [128, 1], F32)
    EXPT = sc.sb(f"{tag}_expt", [128, 8], F32)
    RTM = sc.sb(f"{tag}_rtm", [128, 8], BF16)
    RTT = sc.sb(f"{tag}_rtt", [8, TC], BF16)
    U = sc.sb(f"{tag}_u", [128, NCH, TC], BF16)

    NT = TC // 128

    with (
        nc.psum_tensor(f"{tag}_rl", [8, TC], F32) as RL,
        nc.psum_tensor(f"{tag}_pr", [128, NCH * 8], BF16) as PR,
        nc.psum_tensor(f"{tag}_psm", [128, 8], F32) as PSM,
        nc.psum_tensor(f"{tag}_prb", [8, TC], BF16) as PRB,
        nc.psum_tensor(f"{tag}_pu", [128, 512], F32) as PU,
        nc.sbuf_tensor(f"{tag}_xnt", [128, NCH, TC], BF16) as XNT_h,
        nc.sbuf_tensor(f"{tag}_scf", [8, D], F32) as SCF_h,
        nc.sbuf_tensor(f"{tag}_gwb", [8, D], BF16) as GWB_h,
        nc.sbuf_tensor(f"{tag}_icsb", [8, D], BF16) as ICSB_h,
    ):
        XNT, SCF, GWB, ICSB = XNT_h, SCF_h, GWB_h, ICSB_h

        @blk.sync
        def _(e):
            e.dma_start(out=XNT[:], in_=xnT_hbm.transpose([1, 0, 2])).then_inc(*_ti(XL, 16))
            e.dma_start(out=SCF[:], in_=gw_hbm).then_inc(*_ti(GL, 16))
            _wge(e, CVS, 1)
            e.dma_start(out=SCF[:], in_=ics_hbm).then_inc(*_ti(GL, 16))
            _wge(e, CVS, 2)
            e.dma_start(out=SCF[:], in_=ocs_hbm).then_inc(*_ti(GL, 16))
            for c in range(NCH):
                e.dma_start(out=BIAS[:, c:c + 1],
                            in_=bias_hbm[c * 128:(c + 1) * 128][:, None]
                            ).then_inc(*_ti(BS, 16))

        @blk.scalar
        def _(e):
            _wge(e, GL, 16)
            e.copy(out=GWB[:], in_=SCF[:]).then_inc(*_ti(CVS, 1))
            _wge(e, GL, 32)
            e.copy(out=ICSB[:], in_=SCF[:]).then_inc(*_ti(CVS, 1))
            _wge(e, GL, 48)
            e.copy(out=OCSB[:], in_=SCF[:]).then_inc(*_ti(CVS, 1))
            for tb in range(NT):
                _wge(e, VRT, 3 * tb + 1)
                e.activation(out=EXPT[:], in_=PSM[:], func=AF.Exp,
                             bias=NMX[:], accum_out=ZT[:]).then_inc(*_ti(ART, 1))

        @blk.tensor
        def _(e):
            _wge(e, IDS, 4)
            _wge(e, CVS, 1)
            for c in range(NCH):
                ins = e.transpose(PR[:, c * 8:(c + 1) * 8],
                                  GWB[:, c * 128:(c + 1) * 128], idt[0:8, 0:8])
            ins.then_inc(*_ti(RTR, 1))
            _wge(e, RTR, 2)
            _wge(e, XL, 16)
            for c in range(NCH):
                for hh in range(2):
                    ins = e.matmul(RL[:, hh * 512:(hh + 1) * 512],
                                   GWT[:, c, :],
                                   XNT[:, c, hh * 512:(hh + 1) * 512],
                                   start=(c == 0), stop=(c == NCH - 1))
            ins.then_inc(*_ti(RTR, 1))
            for tb in range(NT):
                _wge(e, RTR, 4 + 2 * tb)
                if tb >= 1:
                    _wge(e, ART, tb)
                e.transpose(PSM[:], RLS[:, tb * 128:(tb + 1) * 128],
                            idf[0:8, 0:8]).then_inc(*_ti(RTR, 1))
                _wge(e, VRT, 3 * tb + 3)
                e.transpose(PRB[:, tb * 128:(tb + 1) * 128], RTM[:],
                            idt[:]).then_inc(*_ti(RTR, 1))
            _wge(e, VRT, 3 * NT + 1)
            for oc in range(NCH):
                for tt in range(2):
                    k = 2 * oc + tt
                    if k >= 1:
                        _wge(e, UEV, k)
                    e.matmul(PU[:], ICSB[:, oc * 128:(oc + 1) * 128],
                             RTT[:, tt * 512:(tt + 1) * 512],
                             start=True, stop=True).then_inc(*_ti(USM, 1))

        @blk.vector
        def _(e):
            _wge(e, RTR, 1)
            e.tensor_copy(out=GWT[:], in_=PR[:].rearrange(
                "p (c e) -> p c e", e=8)).then_inc(*_ti(RTR, 1))
            _wge(e, RTR, 3)
            e.tensor_copy(out=RLS[:], in_=RL[:]).then_inc(*_ti(RTR, 1))
            for tb in range(NT):
                _wge(e, RTR, 5 + 2 * tb)
                if tb >= 1:
                    _wge(e, ART, tb)
                e.reduce_max(out=NMX[:], in_=PSM[:], axis=AX.X,
                             negate=True).then_inc(*_ti(VRT, 1))
                _wge(e, ART, tb + 1)
                e.reciprocal(out=RZ[:], in_=ZT[:]).then_inc(*_ti(VRT, 1))
                _wge(e, VRT, 3 * tb + 2)
                if tb >= 1:
                    _wge(e, RTR, 4 + 2 * tb)
                e.tensor_scalar(out=RTM[:], in0=EXPT[:], scalar1=RZ[:],
                                scalar2=None, op0=ALU.mult).then_inc(*_ti(VRT, 1))
            _wge(e, RTR, 4 + 2 * NT)
            e.tensor_copy(out=RTT[:], in_=PRB[:]).then_inc(*_ti(VRT, 1))
            for oc in range(NCH):
                for tt in range(2):
                    k = 2 * oc + tt
                    _wge(e, USM, k + 1)
                    e.tensor_mul(U[:, oc, tt * 512:(tt + 1) * 512],
                                 XNT[:, oc, tt * 512:(tt + 1) * 512],
                                 PU[:]).then_inc(*_ti(UEV, 1))

    WF2 = sc.sb(f"{tag}_wf2", [128, D], F32)
    SW = sc.sb(f"{tag}_sw", [128, D], BF16)
    TWB = [sc.sb(f"{tag}_twb{i}", [128, NCH, 128], BF16)
           for i in range(2)]
    OSS = [sc.sb(f"{tag}_oss{i}", [128, 512], F32) for i in range(2)]
    YT = [sc.sb(f"{tag}_yt{i}", [128, 512], F32) for i in range(2)]
    GST = [sc.sb(f"{tag}_gst{i}", [128, 512], BF16) for i in range(2)]

    with (
        nc.psum_tensor(f"{tag}_pw0", [128, 1024], BF16) as PW0,
        nc.psum_tensor(f"{tag}_pw1", [128, 1024], BF16) as PW1,
        nc.psum_tensor(f"{tag}_py0", [128, 512], F32) as PY0,
        nc.psum_tensor(f"{tag}_py1", [128, 512], F32) as PY1,
        nc.psum_tensor(f"{tag}_py2", [128, 512], F32) as PY2,
        nc.psum_tensor(f"{tag}_py3", [128, 512], F32) as PY3,
        nc.psum_tensor(f"{tag}_po0", [128, 512], F32) as PO0,
        nc.psum_tensor(f"{tag}_po1", [128, 512], F32) as PO1,
    ):
        PW = [PW0, PW1]
        PY = [PY0, PY1, PY2, PY3]
        PO = [PO0, PO1]

        @blk.sync
        def _(e):
            _wge(e, UEV, 64)
            for ob in range(NCH):
                if ob >= 1:
                    _wge(e, SGN, ob)
                e.dma_start(out=WF2[:],
                            in_=w_hbm[ob * 128:(ob + 1) * 128, :]
                            ).then_inc(*_ti(LW2[ob % 2], 16))

        @blk.scalar
        def _(e):
            _wge(e, BS, 16 * NCH)
            for ob in range(NCH):
                _wge(e, LW2[ob % 2], 16 * (ob // 2 + 1))
                if ob >= 1:
                    _wge(e, TRW, 4 * ob)
                e.sign(out=SW[:], in_=WF2[:]).then_inc(*_ti(SGN, 1))
                for tt in range(2):
                    if ob >= 1:
                        kk = 2 * (ob - 1) + tt
                        _wge(e, YEV, kk + 1)
                        _wge(e, ST2[kk % 2], 16 * (kk // 2))
                        e.activation(out=GST[kk % 2][:], in_=YT[kk % 2][:],
                                     func=AF.Silu,
                                     bias=BIAS[:, (ob - 1):ob]).then_inc(*_ti(SIL, 1))
            for tt in range(2):
                kk = 2 * (NCH - 1) + tt
                _wge(e, YEV, kk + 1)
                _wge(e, ST2[kk % 2], 16 * (kk // 2))
                e.activation(out=GST[kk % 2][:], in_=YT[kk % 2][:],
                             func=AF.Silu,
                             bias=BIAS[:, NCH - 1:NCH]).then_inc(*_ti(SIL, 1))

        @blk.tensor
        def _(e):
            for ob in range(NCH):
                _wge(e, SGN, ob + 1)
                for gg in range(4):
                    kg = 4 * ob + gg
                    if kg >= 2:
                        _wge(e, PWEV, kg - 1)
                    for cc in range(8):
                        hc = gg * 8 + cc
                        ins = e.transpose(
                            PW[kg % 2][:, cc * 128:(cc + 1) * 128],
                            SW[:, hc * 128:(hc + 1) * 128], idt[:])
                    ins.then_inc(*_ti(TRW, 1))
                for tt in range(2):
                    kk = 2 * ob + tt
                    if kk >= 2:
                        _wge(e, POEV, kk - 1)
                    e.matmul(PO[kk % 2][:], OCSB[:, ob * 128:(ob + 1) * 128],
                             RTT[:, tt * 512:(tt + 1) * 512],
                             start=True, stop=True).then_inc(*_ti(MM2, 1))
                _wge(e, PWEV, 4 * (ob + 1))
                for tt in range(2):
                    kk = 2 * ob + tt
                    if kk >= 4:
                        _wge(e, YEV, kk - 3)
                    for hc in range(NCH):
                        ins = e.matmul(PY[kk % 4][:],
                                       TWB[ob % 2][:, hc, :],
                                       U[:, hc, tt * 512:(tt + 1) * 512],
                                       start=(hc == 0), stop=(hc == NCH - 1))
                    ins.then_inc(*_ti(MM2, 1))

        @blk.vector
        def _(e):
            for ob in range(NCH):
                for gg in range(4):
                    kg = 4 * ob + gg
                    _wge(e, TRW, kg + 1)
                    if ob >= 2:
                        _wge(e, MM2, 4 * (ob - 1))
                    e.tensor_copy(
                        out=TWB[ob % 2][:, gg * 8:(gg + 1) * 8, :],
                        in_=PW[kg % 2][:].rearrange("p (c w) -> p c w", c=8)
                    ).then_inc(*_ti(PWEV, 1))
                for tt in range(2):
                    kk = 2 * ob + tt
                    _wge(e, MM2, 4 * ob + tt + 1)
                    e.tensor_copy(out=OSS[kk % 2][:],
                                  in_=PO[kk % 2][:]).then_inc(*_ti(POEV, 1))
                for tt in range(2):
                    kk = 2 * ob + tt
                    _wge(e, MM2, 4 * ob + 2 + tt + 1)
                    _wge(e, POEV, kk + 1)
                    if kk >= 2:
                        _wge(e, SIL, kk - 1)
                    e.tensor_mul(YT[kk % 2][:], OSS[kk % 2][:],
                                 PY[kk % 4][:]).then_inc(*_ti(YEV, 1))

        @blk.gpsimd
        def _(e):
            for ob in range(NCH):
                for tt in range(2):
                    kk = 2 * ob + tt
                    _wge(e, SIL, kk + 1)
                    e.dma_start(out=g_hbm[ob, :, tt * 512:(tt + 1) * 512],
                                in_=GST[kk % 2][:]).then_inc(*_ti(ST2[kk % 2], 16))
            _wge(e, ST2[0], 16 * NCH)
            _wge(e, ST2[1], 16 * NCH)
    sc.close()



def build_ffn0(tag="ffn0"):
    nc = bass.Bass("TRN2", target_bir_lowering=False, debug=False)
    xnT = nc.dram_tensor("xnT", [NCH, 128, TC], BF16, kind="ExternalInput").ap()
    gw = nc.dram_tensor("gw", [8, D], F32, kind="ExternalInput").ap()
    ics = nc.dram_tensor("ics", [8, D], F32, kind="ExternalInput").ap()
    ocs = nc.dram_tensor("ocs", [8, D], F32, kind="ExternalInput").ap()
    w = nc.dram_tensor("w", [D, D], F32, kind="ExternalInput").ap()
    bias = nc.dram_tensor("bias", [D], F32, kind="ExternalInput").ap()
    g = nc.dram_tensor("g", [NCH, 128, TC], BF16, kind="ExternalOutput").ap()
    with nc.Block() as blk:
        emit_ffn0(nc, blk, xnT, gw, ics, ocs, w, bias, g, tag)
    return nc


def emit_attn0(nc, blk, q_hbm, kf_hbm, vf_hbm, ctx_hbm, tag):
    sc = _Scope(nc)
    QL = _vs(nc, f"{tag}_QL")
    IDS = _vs(nc, f"{tag}_IDS")
    KL = [_vs(nc, f"{tag}_KL{i}") for i in range(2)]
    VL = [_vs(nc, f"{tag}_VL{i}") for i in range(2)]
    SCM = _vs(nc, f"{tag}_SCM")
    VN = _vs(nc, f"{tag}_VN")
    AE = _vs(nc, f"{tag}_AE")
    TRP = _vs(nc, f"{tag}_TRP")
    PEV = _vs(nc, f"{tag}_PEV")
    PVM = _vs(nc, f"{tag}_PVM")
    RC = _vs(nc, f"{tag}_RC")
    CEV = _vs(nc, f"{tag}_CEV")
    TCX = _vs(nc, f"{tag}_TCX")
    CTE = _vs(nc, f"{tag}_CTE")
    CST = [_vs(nc, f"{tag}_CST{i}", swdge=True) for i in range(2)]

    idt = _ident(nc, blk, IDS, sc, tag)

    QR = sc.sb(f"{tag}_qr", [128, NCH, TC], BF16)
    KT = [sc.sb(f"{tag}_kt{i}", [128, S], BF16) for i in range(2)]
    VT = [sc.sb(f"{tag}_vt{i}", [128, 16, 128], BF16)
          for i in range(2)]
    PB = [sc.sb(f"{tag}_pb{i}", [128, S], BF16) for i in range(2)]
    PTS = sc.sb(f"{tag}_pts", [128, 16, 128], BF16)
    NMX = sc.sb(f"{tag}_nmx", [128, 1], F32)
    Z = [sc.sb(f"{tag}_z{i}", [128, 1], F32) for i in range(2)]
    RZ = [sc.sb(f"{tag}_rz{i}", [128, 1], F32) for i in range(2)]
    CXS = [sc.sb(f"{tag}_cxs{i}", [128, 128], BF16) for i in range(2)]
    CTS = [sc.sb(f"{tag}_cts{i}", [128, 128], BF16) for i in range(2)]

    SC = sc.ps(f"{tag}_sc", [128, S], F32)
    PSB = [sc.ps(f"{tag}_psb{i}", [128, 1024], BF16)
           for i in range(2)]
    PCX = sc.ps(f"{tag}_pcx", [128, 128], F32)
    PCT = sc.ps(f"{tag}_pct", [128, 128], BF16)

    BH = [(b, h) for b in range(B) for h in range(32)]

    @blk.sync
    def _(e):
        e.dma_start(out=QR[:], in_=q_hbm.transpose([1, 0, 2])).then_inc(*_ti(QL, 16))
        for i, (b, h) in enumerate(BH):
            if i >= 2:
                _wge(e, SCM, 2 * (i - 1))
            e.dma_start(out=KT[i % 2][:], in_=kf_hbm[b, h]).then_inc(*_ti(KL[i % 2], 16))
            if i >= 2:
                _wge(e, PVM, 2 * (i - 1))
            e.dma_start(out=VT[i % 2][:],
                        in_=vf_hbm[b, h].rearrange("(kt p) d -> p kt d", p=128)
                        ).then_inc(*_ti(VL[i % 2], 16))

    @blk.tensor
    def _(e):
        _wge(e, QL, 16)
        _wge(e, IDS, 2)
        for i, (b, h) in enumerate(BH):
            _wge(e, KL[i % 2], 16 * (i // 2 + 1))
            _wge(e, VL[i % 2], 16 * (i // 2 + 1))
            for qt in range(2):
                u = 2 * i + qt
                if u >= 1:
                    _wge(e, AE, u)
                qsl = QR[:, h, b * PC + qt * 128: b * PC + (qt + 1) * 128]
                for j in range(4):
                    ins = e.matmul(SC[:, j * 512:(j + 1) * 512], qsl,
                                   KT[i % 2][:, j * 512:(j + 1) * 512],
                                   start=True, stop=True)
                ins.then_inc(*_ti(SCM, 1))
                _wge(e, AE, u + 1)
                _wge(e, PEV, 2 * u)
                for g in range(2):
                    for cc in range(8):
                        kt_i = g * 8 + cc
                        ins = e.transpose(
                            PSB[g][:, cc * 128:(cc + 1) * 128],
                            PB[u % 2][:, kt_i * 128:(kt_i + 1) * 128], idt[:])
                    ins.then_inc(*_ti(TRP, 1))
                _wge(e, PEV, 2 * u + 2)
                if u >= 1:
                    _wge(e, CEV, u)
                for kt_i in range(16):
                    ins = e.matmul(PCX[:], PTS[:, kt_i, :], VT[i % 2][:, kt_i, :],
                                   start=(kt_i == 0), stop=(kt_i == 15))
                ins.then_inc(*_ti(PVM, 1))
                _wge(e, CEV, u + 1)
                if u >= 1:
                    _wge(e, CTE, u)
                e.transpose(PCT[:], CXS[u % 2][:], idt[:]).then_inc(*_ti(TCX, 1))

    @blk.vector
    def _(e):
        for i, (b, h) in enumerate(BH):
            for qt in range(2):
                u = 2 * i + qt
                _wge(e, SCM, u + 1)
                e.reduce_max(out=NMX[:], in_=SC[:], axis=AX.X,
                             negate=True).then_inc(*_ti(VN, 1))
                _wge(e, AE, u + 1)
                e.reciprocal(out=RZ[u % 2][:], in_=Z[u % 2][:]).then_inc(*_ti(RC, 1))
                _wge(e, RC, u + 1)
                _wge(e, PVM, u + 1)
                if u >= 2:
                    _wge(e, TCX, u - 1)
                e.tensor_scalar(out=CXS[u % 2][:], in0=PCX[:],
                                scalar1=RZ[u % 2][:], scalar2=None,
                                op0=ALU.mult).then_inc(*_ti(CEV, 1))

    @blk.scalar
    def _(e):
        for i, (b, h) in enumerate(BH):
            for qt in range(2):
                u = 2 * i + qt
                _wge(e, VN, u + 1)
                if u >= 2:
                    _wge(e, TRP, 2 * (u - 1))
                    _wge(e, RC, u - 1)
                e.activation(out=PB[u % 2][:], in_=SC[:], func=AF.Exp,
                             bias=NMX[:], accum_out=Z[u % 2][:]).then_inc(*_ti(AE, 1))
                for g in range(2):
                    _wge(e, TRP, 2 * u + g + 1)
                    if u >= 1:
                        _wge(e, PVM, u)
                    e.copy(out=PTS[:, g * 8:(g + 1) * 8, :],
                           in_=PSB[g][:].rearrange("p (c w) -> p c w", c=8)
                           ).then_inc(*_ti(PEV, 1))
                _wge(e, TCX, u + 1)
                if u >= 2:
                    _wge(e, CST[u % 2], 16 * (u // 2))
                e.copy(out=CTS[u % 2][:], in_=PCT[:]).then_inc(*_ti(CTE, 1))

    @blk.gpsimd
    def _(e):
        for i, (b, h) in enumerate(BH):
            for qt in range(2):
                u = 2 * i + qt
                _wge(e, CTE, u + 1)
                e.dma_start(out=ctx_hbm[h, :, b * PC + qt * 128:
                                        b * PC + (qt + 1) * 128],
                            in_=CTS[u % 2][:]).then_inc(*_ti(CST[u % 2], 16))
        _wge(e, CST[0], 16 * 128)
        _wge(e, CST[1], 16 * 128)
    sc.close()



def build_attn0(tag="attn0"):
    nc = bass.Bass("TRN2", target_bir_lowering=False, debug=False)
    q = nc.dram_tensor("q", [NCH, 128, TC], BF16, kind="ExternalInput").ap()
    kf = nc.dram_tensor("kf", [B, 32, 128, S], BF16, kind="ExternalInput").ap()
    vf = nc.dram_tensor("vf", [B, 32, S, 128], BF16, kind="ExternalInput").ap()
    ctx = nc.dram_tensor("ctx", [NCH, 128, TC], BF16, kind="ExternalOutput").ap()
    with nc.Block() as blk:
        emit_attn0(nc, blk, q, kf, vf, ctx, tag)
    return nc


def emit_addln(nc, blk, tag, SG, SGR, sg_ready, x_hbm, x1_hbm, xnT_hbm, pi_order,
               idt, IDS, x1_dtype=F32, sg_all=None):
    sc = _Scope(nc)
    LXc = [_vs(nc, f"{tag}_LX{i}") for i in range(2)]
    TS1 = _vs(nc, f"{tag}_TS1")
    AD = _vs(nc, f"{tag}_AD")
    XS = _vs(nc, f"{tag}_XS")
    VA = _vs(nc, f"{tag}_VA")
    SA = _vs(nc, f"{tag}_SA")
    TS = _vs(nc, f"{tag}_TS")
    ES = _vs(nc, f"{tag}_ES")
    SSX = [_vs(nc, f"{tag}_SSX{i}", swdge=True) for i in range(2)]
    SSN = [_vs(nc, f"{tag}_SSN{i}", swdge=True) for i in range(2)]
    VC = _vs(nc, f"{tag}_VC")
    AC = _vs(nc, f"{tag}_AC")

    XTILE = [sc.sb(f"{tag}_xtile{i}", [128, D], F32)
             for i in range(2)]
    X1T = sc.sb(f"{tag}_x1t", [128, D], F32)
    X1STG = [sc.sb(f"{tag}_x1stg{i}", [128, D], x1_dtype)
             for i in range(2)]
    PX = [sc.ps(f"{tag}_px{i}", [128, 512], BF16) for i in range(2)]
    ln = xnT_hbm is not None
    if ln:
        SQ = sc.sb(f"{tag}_sq", [128, D], F32)
        XN = [sc.sb(f"{tag}_xn{i}", [128, D], BF16)
              for i in range(2)]
        SUM = sc.sb(f"{tag}_sum", [128, 1], F32)
        SSUM = sc.sb(f"{tag}_ssum", [128, 1], F32)
        M = sc.sb(f"{tag}_m", [128, 1], F32)
        MS = sc.sb(f"{tag}_ms", [128, 1], F32)
        VAR = sc.sb(f"{tag}_var", [128, 1], F32)
        VAR2 = sc.sb(f"{tag}_var2", [128, 1], F32)
        LV = sc.sb(f"{tag}_lv", [128, 1], F32)
        RSTD = sc.sb(f"{tag}_rstd", [128, 1], F32)
        EPST = sc.sb(f"{tag}_epst", [128, 1], F32)
        PT = [sc.ps(f"{tag}_pt{i}", [128, 1024], BF16)
              for i in range(4)]
        STG = [sc.sb(f"{tag}_stg{i}", [128, 1024], BF16)
               for i in range(2)]

    NT = TC // 128
    NG = 4

    @blk.sync
    def _(e):
        for tt in range(NT):
            if tt >= 2:
                _wge(e, AD, 8 * (tt - 1))
            e.dma_start(out=XTILE[tt % 2][:],
                        in_=x_hbm[tt * 128:(tt + 1) * 128, :]
                        ).then_inc(*_ti(LXc[tt % 2], 16))

    @blk.tensor
    def _(e):
        _wge(e, IDS, 2)
        if sg_all is not None:
            _wge(e, SGR, sg_all)
        for tt in range(NT):
            for gg in range(8):
                kq = 8 * tt + gg
                _wge(e, SGR, sg_ready(gg * 4 + 3))
                if kq >= 2:
                    _wge(e, AD, kq - 1)
                for cc in range(4):
                    oc = gg * 4 + cc
                    ins = e.transpose(PX[kq % 2][:, cc * 128:(cc + 1) * 128],
                                      SG[:, oc, tt * 128:(tt + 1) * 128], idt[:])
                ins.then_inc(*_ti(TS1, 1))
            if ln:
                _wge(e, VA, 2 * tt + 2)
                for g in range(NG):
                    k = tt * NG + g
                    if k >= 4:
                        _wge(e, ES, k - 3)
                    for cc in range(8):
                        c = g * 8 + cc
                        if pi_order:
                            Tb, j = c // 8, c % 8
                            src = XN[tt % 2][:].rearrange(
                                "p (T m j) -> p T m j",
                                T=4, m=128, j=8)[:, Tb, :, j]
                        else:
                            src = XN[tt % 2][:, c * 128:(c + 1) * 128]
                        ins = e.transpose(PT[k % 4][:, cc * 128:(cc + 1) * 128],
                                          src, idt[:])
                    ins.then_inc(*_ti(TS, 1))

    @blk.vector
    def _(e):
        if ln:
            e.memset(EPST[:], float(EPS))
        for tt in range(NT):
            _wge(e, LXc[tt % 2], 16 * (tt // 2 + 1))
            if tt >= 1:
                _wge(e, XS, tt)
            for gg in range(8):
                kq = 8 * tt + gg
                _wge(e, TS1, kq + 1)
                e.tensor_add(X1T[:, gg * 512:(gg + 1) * 512],
                             XTILE[tt % 2][:, gg * 512:(gg + 1) * 512],
                             PX[kq % 2][:]).then_inc(*_ti(AD, 1))
            if ln:
                _wge(e, SA, 2 * tt + 1)
                e.reduce_sum(out=SUM[:], in_=X1T[:], axis=AX.X).then_inc(*_ti(VC, 1))
                _wge(e, VC, 4 * tt + 1)
                e.tensor_scalar(out=M[:], in0=SUM[:], scalar1=1.0 / D,
                                scalar2=None, op0=ALU.mult)
                e.tensor_scalar(out=MS[:], in0=SSUM[:], scalar1=1.0 / D,
                                scalar2=None, op0=ALU.mult).then_inc(*_ti(VC, 1))
                _wge(e, VC, 4 * tt + 2)
                e.tensor_tensor(out=VAR[:], in0=M[:], in1=M[:],
                                op=ALU.mult).then_inc(*_ti(VC, 1))
                _wge(e, VC, 4 * tt + 3)
                e.tensor_sub(VAR2[:], MS[:], VAR[:]).then_inc(*_ti(VA, 1))
                e.sem_inc(*_ti(VC, 1))
                _wge(e, SA, 2 * tt + 2)
                if tt >= 2:
                    _wge(e, TS, NG * (tt - 1))
                e.tensor_scalar(out=XN[tt % 2][:], in0=X1T[:], scalar1=M[:],
                                scalar2=RSTD[:], op0=ALU.subtract,
                                op1=ALU.mult).then_inc(*_ti(VA, 1))

    @blk.scalar
    def _(e):
        for tt in range(NT):
            _wge(e, AD, 8 * (tt + 1))
            if ln:
                e.activation(out=SQ[:], in_=X1T[:], func=AF.Square,
                             accum_out=SSUM[:]).then_inc(*_ti(SA, 1))
                _wge(e, VA, 2 * tt + 1)
                e.activation(out=LV[:], in_=VAR2[:], func=AF.Ln,
                             bias=EPST[:]).then_inc(*_ti(AC, 1))
                _wge(e, AC, tt + 1)
                e.activation(out=RSTD[:], in_=LV[:], func=AF.Exp,
                             scale=-0.5).then_inc(*_ti(SA, 1))
            if tt >= 2:
                _wge(e, SSX[tt % 2], 16 * (tt // 2))
            if ln:
                _wge(e, VA, 2 * tt + 2)
            e.copy(out=X1STG[tt % 2][:], in_=X1T[:]).then_inc(*_ti(XS, 1))
            if ln and tt > 0:
                for g in range(NG):
                    k = (tt - 1) * NG + g
                    _wge(e, TS, k + 1)
                    if k >= 2:
                        _wge(e, SSN[k % 2], 16 * (k // 2))
                    e.copy(out=STG[k % 2][:], in_=PT[k % 4][:]).then_inc(*_ti(ES, 1))
        if ln:
            for g in range(NG):
                k = (NT - 1) * NG + g
                _wge(e, TS, k + 1)
                _wge(e, SSN[k % 2], 16 * (k // 2))
                e.copy(out=STG[k % 2][:], in_=PT[k % 4][:]).then_inc(*_ti(ES, 1))

    @blk.gpsimd
    def _(e):
        for tt in range(NT):
            _wge(e, XS, tt + 1)
            e.dma_start(out=x1_hbm[tt * 128:(tt + 1) * 128, :],
                        in_=X1STG[tt % 2][:]).then_inc(*_ti(SSX[tt % 2], 16))
            if ln:
                for g in range(NG):
                    k = tt * NG + g
                    _wge(e, ES, k + 1)
                    dst = xnT_hbm[g * 8:(g + 1) * 8, :,
                                  tt * 128:(tt + 1) * 128].transpose([1, 0, 2])
                    e.dma_start(
                        out=dst,
                        in_=STG[k % 2][:].rearrange("p (c t) -> p c t", c=8),
                    ).then_inc(*_ti(SSN[k % 2], 16))
        _wge(e, SSX[0], 16 * (NT // 2))
        _wge(e, SSX[1], 16 * (NT // 2))
        if ln:
            _wge(e, SSN[0], 16 * (NT * NG // 2))
            _wge(e, SSN[1], 16 * (NT * NG // 2))


    sc.close()



def emit_combine0(nc, blk, ctx_hbm, ow_hbm, ob_hbm, g_hbm, x_hbm, x1_hbm,
                  xnT_hbm, tag):
    sc = _Scope(nc)
    IDS = _vs(nc, f"{tag}_IDS")
    idt = _ident(nc, blk, IDS, sc, tag)

    CXL = _vs(nc, f"{tag}_CXL")
    OBL = _vs(nc, f"{tag}_OBL")
    GLD = [_vs(nc, f"{tag}_GLD{i}", swdge=True) for i in range(2)]
    LW = [_vs(nc, f"{tag}_LW{i}") for i in range(2)]
    CV = _vs(nc, f"{tag}_CV")
    MM = _vs(nc, f"{tag}_MM")
    SGR = _vs(nc, f"{tag}_SGR")

    SG = sc.sb(f"{tag}_sg", [128, NCH, TC], BF16)
    OB = sc.sb(f"{tag}_ob", [128, NCH], F32)
    GTL = [sc.sb(f"{tag}_gtl{i}", [128, TC], BF16)
           for i in range(2)]

    with (
        nc.sbuf_tensor(f"{tag}_ctx", [128, NCH, TC], BF16) as CTX,
        nc.sbuf_tensor(f"{tag}_wf0", [128, 512], F32) as WFa,
        nc.sbuf_tensor(f"{tag}_wf1", [128, 512], F32) as WFb,
        nc.sbuf_tensor(f"{tag}_wb0", [128, 512], BF16) as WBa,
        nc.sbuf_tensor(f"{tag}_wb1", [128, 512], BF16) as WBb,
        nc.psum_tensor(f"{tag}_ps0", [128, 512], F32) as Pa,
        nc.psum_tensor(f"{tag}_ps1", [128, 512], F32) as Pb,
        nc.psum_tensor(f"{tag}_ps2", [128, 512], F32) as Pc,
        nc.psum_tensor(f"{tag}_ps3", [128, 512], F32) as Pd,
        nc.psum_tensor(f"{tag}_ps4", [128, 512], F32) as Pe,
        nc.psum_tensor(f"{tag}_ps5", [128, 512], F32) as Pf,
    ):
        WF = [WFa, WFb]
        WB = [WBa, WBb]
        PS = [Pa, Pb, Pc, Pd, Pe, Pf]
        NGRP = NCH // 3 + (1 if NCH % 3 else 0)
        groups = [list(range(3 * i, min(3 * i + 3, NCH))) for i in range(NGRP)]

        @blk.sync
        def _(e):
            e.dma_start(out=CTX[:], in_=ctx_hbm.transpose([1, 0, 2])
                        ).then_inc(*_ti(CXL, 16))
            for c in range(NCH):
                e.dma_start(out=OB[:, c:c + 1],
                            in_=ob_hbm[c * 128:(c + 1) * 128][:, None]
                            ).then_inc(*_ti(OBL, 16))
            k = 0
            for gi, g in enumerate(groups):
                c0, c1 = g[0] * 128, (g[-1] + 1) * 128
                for ic in range(NCH):
                    if k >= 2:
                        _wge(e, CV, k - 1)
                    e.dma_start(out=WF[k % 2][:, 0:(c1 - c0)],
                                in_=ow_hbm[ic * 128:(ic + 1) * 128, c0:c1]
                                ).then_inc(*_ti(LW[k % 2], 16))
                    k += 1

        @blk.gpsimd
        def _(e):
            for oc in range(NCH):
                if oc >= 2:
                    _wge(e, SGR, 2 * (oc - 1))
                e.dma_start(out=GTL[oc % 2][:], in_=g_hbm[oc]
                            ).then_inc(*_ti(GLD[oc % 2], 16))

        @blk.scalar
        def _(e):
            k = 0
            for gi, g in enumerate(groups):
                w = len(g) * 128
                for ic in range(NCH):
                    _wge(e, LW[k % 2], 16 * (k // 2 + 1))
                    if k >= 2:
                        _wge(e, MM, k - 1)
                    e.copy(out=WB[k % 2][:, 0:w],
                           in_=WF[k % 2][:, 0:w]).then_inc(*_ti(CV, 1))
                    k += 1

        @blk.tensor
        def _(e):
            _wge(e, CXL, 16)
            k = 0
            for gi, g in enumerate(groups):
                _wge(e, SGR, 2 * (3 * gi))
                for ic in range(NCH):
                    _wge(e, CV, k + 1)
                    for j, oc in enumerate(g):
                        for tt in range(2):
                            ins = e.matmul(
                                PS[j * 2 + tt][:],
                                WB[k % 2][:, j * 128:(j + 1) * 128],
                                CTX[:, ic, tt * 512:(tt + 1) * 512],
                                start=(ic == 0), stop=(ic == NCH - 1))
                    ins.then_inc(*_ti(MM, 1))
                    k += 1

        @blk.vector
        def _(e):
            _wge(e, OBL, 16 * NCH)
            for gi, g in enumerate(groups):
                _wge(e, MM, NCH * (gi + 1))
                for j, oc in enumerate(g):
                    _wge(e, GLD[oc % 2], 16 * (oc // 2 + 1))
                    for tt in range(2):
                        e.scalar_tensor_tensor(
                            out=SG[:, oc, tt * 512:(tt + 1) * 512],
                            in0=PS[j * 2 + tt][:],
                            scalar=OB[:, oc:oc + 1],
                            in1=GTL[oc % 2][:, tt * 512:(tt + 1) * 512],
                            op0=ALU.add, op1=ALU.mult).then_inc(*_ti(SGR, 1))

    emit_addln(nc, blk, f"{tag}b", SG, SGR, lambda oc: 2 * (oc + 1),
               x_hbm, x1_hbm, xnT_hbm, True, idt, IDS, sg_all=2 * NCH)
    sc.close()



def build_combine0(tag="cmb0"):
    nc = bass.Bass("TRN2", target_bir_lowering=False, debug=False)
    ctx = nc.dram_tensor("ctx", [NCH, 128, TC], BF16, kind="ExternalInput").ap()
    ow = nc.dram_tensor("ow", [D, D], F32, kind="ExternalInput").ap()
    ob = nc.dram_tensor("ob", [D], F32, kind="ExternalInput").ap()
    g = nc.dram_tensor("g", [NCH, 128, TC], BF16, kind="ExternalInput").ap()
    x = nc.dram_tensor("x", [TC, D], F32, kind="ExternalInput").ap()
    x1 = nc.dram_tensor("x1", [TC, D], F32, kind="ExternalOutput").ap()
    xnT = nc.dram_tensor("xnT", [NCH, 128, TC], BF16, kind="ExternalOutput").ap()
    with nc.Block() as blk:
        emit_combine0(nc, blk, ctx, ow, ob, g, x, x1, xnT, tag)
    return nc


def emit_final(nc, blk, x1_hbm, s1_hbm, g1_hbm, out_hbm, tag):
    sc = _Scope(nc)
    IDS = _vs(nc, f"{tag}_IDS")
    idt = _ident(nc, blk, IDS, sc, tag)
    S1L = [_vs(nc, f"{tag}_S1L{i}") for i in range(2)]
    G1L = [_vs(nc, f"{tag}_G1L{i}") for i in range(2)]
    SGR = _vs(nc, f"{tag}_SGR")
    SG = sc.sb(f"{tag}_sg", [128, NCH, TC], BF16)
    SC1 = [sc.sb(f"{tag}_sc1{i}", [128, TC], BF16)
           for i in range(2)]
    GC1 = [sc.sb(f"{tag}_gc1{i}", [128, TC], F16)
           for i in range(2)]

    @blk.sync
    def _(e):
        for oc in range(NCH):
            if oc >= 2:
                _wge(e, SGR, oc - 1)
            e.dma_start(out=SC1[oc % 2][:], in_=s1_hbm[oc]
                        ).then_inc(*_ti(S1L[oc % 2], 16))
            e.dma_start(out=GC1[oc % 2][:], in_=g1_hbm[oc]
                        ).then_inc(*_ti(G1L[oc % 2], 16))

    @blk.vector
    def _(e):
        for oc in range(NCH):
            _wge(e, S1L[oc % 2], 16 * (oc // 2 + 1))
            _wge(e, G1L[oc % 2], 16 * (oc // 2 + 1))
            e.tensor_mul(SG[:, oc, :], SC1[oc % 2][:],
                         GC1[oc % 2][:]).then_inc(*_ti(SGR, 1))

    emit_addln(nc, blk, f"{tag}b", SG, SGR, lambda oc: oc + 1,
               x1_hbm, out_hbm, None, False, idt, IDS)
    sc.close()



def build_final(tag="fin"):
    nc = bass.Bass("TRN2", target_bir_lowering=False, debug=False)
    x1 = nc.dram_tensor("x1", [TC, D], F32, kind="ExternalInput").ap()
    s1 = nc.dram_tensor("s1", [NCH, 128, TC], BF16, kind="ExternalInput").ap()
    g1 = nc.dram_tensor("g1", [NCH, 128, TC], F16, kind="ExternalInput").ap()
    out = nc.dram_tensor("out", [TC, D], F32, kind="ExternalOutput").ap()
    with nc.Block() as blk:
        emit_final(nc, blk, x1, s1, g1, out, tag)
    return nc


def emit_densewT(nc, blk, tag, w_hbm, NOB, rhs_get, store_dst, permute,
                 gate=None, TW=None, QST=None, PWQ=None, PQ=None, idt=None,
                 IDS=None, n_in_blk=4, WF=None, WB=None):
    sc = _Scope(nc)
    LW = [_vs(nc, f"{tag}_LW{i}") for i in range(2)]
    CV = _vs(nc, f"{tag}_CV")
    TR = _vs(nc, f"{tag}_TR")
    TE = _vs(nc, f"{tag}_TE")
    MM = _vs(nc, f"{tag}_MM")
    PE = _vs(nc, f"{tag}_PE")
    ST = [_vs(nc, f"{tag}_ST{i}", swdge=True) for i in range(2)]

    if WF is None:
        WF = [sc.sb(f"{tag}_wf{i}", [128, 1024], F32)
              for i in range(2)]
        WB = [sc.sb(f"{tag}_wb{i}", [128, 1024], BF16)
              for i in range(2)]

    @blk.sync
    def _(e):
        if gate is not None:
            _wge(e, gate[0], gate[1])
        k = 0
        for ob in range(NOB):
            for Tb in range(n_in_blk):
                if k >= 2:
                    _wge(e, CV, k - 1)
                e.dma_start(out=WF[k % 2][:],
                            in_=w_hbm[ob * 128:(ob + 1) * 128,
                                      Tb * 1024:(Tb + 1) * 1024]
                            ).then_inc(*_ti(LW[k % 2], 16))
                k += 1

    @blk.scalar
    def _(e):
        k = 0
        for ob in range(NOB):
            for Tb in range(n_in_blk):
                _wge(e, LW[k % 2], 16 * (k // 2 + 1))
                if k >= 2:
                    _wge(e, TR, 8 * (k - 1))
                if permute:
                    e.copy(out=WB[k % 2][:].rearrange("p (j m) -> p j m", j=8),
                           in_=WF[k % 2][:].rearrange("p (m j) -> p j m", j=8)
                           ).then_inc(*_ti(CV, 1))
                else:
                    e.copy(out=WB[k % 2][:], in_=WF[k % 2][:]).then_inc(*_ti(CV, 1))
                k += 1

    @blk.tensor
    def _(e):
        _wge(e, IDS, 2)
        k = 0
        for ob in range(NOB):
            for Tb in range(n_in_blk):
                _wge(e, CV, k + 1)
                if k >= 2:
                    _wge(e, TE, 4 * (k - 1))
                for cc in range(8):
                    ins = e.transpose(
                        PWQ[k % 2][:, cc * 128:(cc + 1) * 128],
                        WB[k % 2][:, cc * 128:(cc + 1) * 128], idt[:])
                ins.then_inc(*_ti(TR, 8))
                k += 1
            _wge(e, TE, 4 * n_in_blk * (ob + 1))
            for tt in range(2):
                kk = 2 * ob + tt
                if kk >= 2:
                    _wge(e, PE, kk - 1)
                for hc in range(NCH):
                    ins = e.matmul(PQ[kk % 2][:],
                                   TW[ob % 2][:, hc, :],
                                   rhs_get(hc, tt),
                                   start=(hc == 0), stop=(hc == NCH - 1))
                ins.then_inc(*_ti(MM, 1))

    @blk.vector
    def _(e):
        for ob in range(NOB):
            for Tb in range(n_in_blk):
                k = ob * n_in_blk + Tb
                _wge(e, TR, 8 * (k + 1))
                if ob >= 1:
                    _wge(e, MM, 2 * ob)
                e.tensor_copy(
                    out=TW[ob % 2][:, Tb * 8:(Tb + 1) * 8, :],
                    in_=PWQ[k % 2][:].rearrange("p (c w) -> p c w", c=8)
                ).then_inc(*_ti(TE, 4))
            for tt in range(2):
                kk = 2 * ob + tt
                _wge(e, MM, kk + 1)
                _wge(e, ST[kk % 2], 16 * (kk // 2))
                e.tensor_copy(out=QST[kk % 2][:],
                              in_=PQ[kk % 2][:]).then_inc(*_ti(PE, 1))

    @blk.gpsimd
    def _(e):
        for ob in range(NOB):
            for tt in range(2):
                kk = 2 * ob + tt
                _wge(e, PE, kk + 1)
                e.dma_start(out=store_dst(ob, tt),
                            in_=QST[kk % 2][:]).then_inc(*_ti(ST[kk % 2], 16))
        _wge(e, ST[0], 16 * NOB)
        _wge(e, ST[1], 16 * NOB)

    return ST
    sc.close()



def emit_l1attn(nc, blk, xnT_hbm, wqkv_hbm, wproj_hbm, s1_hbm, tag):
    sc = _Scope(nc)
    IDS = _vs(nc, f"{tag}_IDS")
    idt = _ident(nc, blk, IDS, sc, tag)

    AO = sc.sb(f"{tag}_ao", [128, NCH, TC], BF16)
    TW = [sc.sb(f"{tag}_tw{i}", [128, NCH, 128], BF16)
          for i in range(2)]
    QST = [sc.sb(f"{tag}_qst{i}", [128, 512], BF16)
           for i in range(2)]
    WFs = [sc.sb(f"{tag}_wfs{i}", [128, 1024], F32)
           for i in range(2)]
    WBs = [sc.sb(f"{tag}_wbs{i}", [128, 1024], BF16)
           for i in range(2)]
    PWQ = [sc.ps(f"{tag}_pwq{i}", [128, 1024], BF16)
           for i in range(2)]
    PQ = [sc.ps(f"{tag}_pq{i}", [128, 512], F32) for i in range(2)]
    PSC = [sc.ps(f"{tag}_psc{i}", [128, 64, 16], F32)
           for i in range(2)]
    PW3 = PWQ[0][0:16, :]

    qkv1h = nc.dram_tensor(f"{tag}_qkv1h", [48, 128, TC], BF16,
                           kind="Internal").ap()
    w2h = nc.dram_tensor(f"{tag}_w2h", [16, 64, 128], BF16, kind="Internal").ap()

    pairs = [(s, t) for s in range(4) for t in range(s + 1)]
    col = {(s, t): s * (s + 1) // 2 + t for (s, t) in pairs}
    AVSCH = []
    for s in range(4):
        for t in range(s + 1):
            for hf in range(2):
                AVSCH.append((s, t, hf))
    for s in range(4):
        for hf in range(2):
            AVSCH.append((s, None, hf))
    MRD = []
    _av = 0
    for (s, t, hf) in AVSCH:
        if t is None or t == 0:
            _av += 1
            MRD.append(_av)
        else:
            _av += 1
            MRD.append(_av)
            _av += 1
    AV_TOTAL = _av

    with nc.sbuf_tensor(f"{tag}_xnt1", [128, NCH, TC], BF16) as XNT1:
        XL1 = _vs(nc, f"{tag}_XL1")

        @blk.sync
        def _(e):
            e.dma_start(out=XNT1[:], in_=xnT_hbm.transpose([1, 0, 2])
                        ).then_inc(*_ti(XL1, 16))

        @blk.tensor
        def _(e):
            _wge(e, XL1, 16)

        AST = emit_densewT(
            nc, blk, f"{tag}A", wqkv_hbm, 48,
            lambda hc, tt: XNT1[:, hc, tt * 512:(tt + 1) * 512],
            lambda ob, tt: qkv1h[ob, :, tt * 512:(tt + 1) * 512],
            True, gate=None, TW=TW, QST=QST, PWQ=PWQ, PQ=PQ, idt=idt, IDS=IDS,
            WF=WFs, WB=WBs)

    ESB = _vs(nc, f"{tag}_ESB")
    KVL = _vs(nc, f"{tag}_KVL")
    QSL = _vs(nc, f"{tag}_QSL")
    PRD = _vs(nc, f"{tag}_PRD")
    SMM = _vs(nc, f"{tag}_SMM")
    SFX = _vs(nc, f"{tag}_SFX")
    SFA = _vs(nc, f"{tag}_SFA")
    TW2 = _vs(nc, f"{tag}_TW2")
    W3E = _vs(nc, f"{tag}_W3E")
    W3S = _vs(nc, f"{tag}_W3S", swdge=True)
    WBL = [_vs(nc, f"{tag}_WBL{i}") for i in range(2)]
    AVV = _vs(nc, f"{tag}_AVV")
    SAD = _vs(nc, f"{tag}_SAD")

    with nc.sbuf_tensor(f"{tag}_sca0", [128, 64, 16], F32) as SCAa, \
         nc.sbuf_tensor(f"{tag}_sca1", [128, 64, 16], F32) as SCAb, \
         nc.sbuf_tensor(f"{tag}_qs", [128, 8, 4, 256], BF16) as QS, \
         nc.sbuf_tensor(f"{tag}_ktv", [128, 8, 4, 256], BF16) as KTV, \
         nc.sbuf_tensor(f"{tag}_pr0", [128, 8192], BF16) as PRa, \
         nc.sbuf_tensor(f"{tag}_pr1", [128, 8192], BF16) as PRb, \
         nc.sbuf_tensor(f"{tag}_est", [128, 31], BF16) as EST, \
         nc.sbuf_tensor(f"{tag}_w2", [128, 64, 16], BF16) as W2, \
         nc.sbuf_tensor(f"{tag}_nmx1", [128, 64], F32) as NMX1, \
         nc.sbuf_tensor(f"{tag}_scw", [128, 64, 4], F32) as SCW, \
         nc.sbuf_tensor(f"{tag}_z1", [128, 64], F32) as Z1, \
         nc.sbuf_tensor(f"{tag}_rz1", [128, 64], F32) as RZ1, \
         nc.sbuf_tensor(f"{tag}_w3s", [16, 8, 128], BF16) as W3STG, \
         nc.sbuf_tensor(f"{tag}_wbc0", [128, 4096], BF16) as WBCa, \
         nc.sbuf_tensor(f"{tag}_wbc1", [128, 4096], BF16) as WBCb, \
         nc.sbuf_tensor(f"{tag}_tmp", [128, 4096], BF16) as TMP:
        PR = [PRa, PRb]
        WBC = [WBCa, WBCb]
        SCACC = [SCAa, SCAb]
        SCA = SCACC[1]

        def kb(t):
            return KTV[:, :, t:t + 1, :].broadcast_to([128, 8, 4, 256])

        vb = kb

        def ao_view(s):
            return AO[:].rearrange("p (kv sl) t -> p kv sl t", kv=8)[
                :, :, :, s * 256:(s + 1) * 256]

        @blk.sync
        def _(e):
            _wge(e, AST[0], 16 * 48)
            _wge(e, AST[1], 16 * 48)
            for t in range(4):
                src = qkv1h.rearrange("(kv sl) d t -> kv sl d t", kv=8)[
                    :, 4, :, t * 256:(t + 1) * 256].transpose([1, 0, 2])
                e.dma_start(out=KTV[:, :, t, :], in_=src).then_inc(*_ti(KVL, 16))
            for s in range(4):
                if s >= 1:
                    _wge(e, PRD, s * (s + 1) // 2)
                for sl in range(4):
                    src = qkv1h.rearrange("(kv sl) d t -> kv sl d t", kv=8)[
                        :, sl, :, s * 256:(s + 1) * 256].transpose([1, 0, 2])
                    e.dma_start(out=QS[:, :, sl, :], in_=src).then_inc(*_ti(QSL, 16))
            _wge(e, SMM, 10)
            for t in range(4):
                src = qkv1h.rearrange("(kv sl) d t -> kv sl d t", kv=8)[
                    :, 5, :, t * 256:(t + 1) * 256].transpose([1, 0, 2])
                e.dma_start(out=KTV[:, :, t, :], in_=src).then_inc(*_ti(KVL, 16))
            _wge(e, W3S, 16 * 8)
            for li, (s, t, hf) in enumerate(AVSCH):
                st = col[(s, t)] if t is not None else 12 + s
                if li >= 2:
                    _wge(e, AVV, MRD[li - 2])
                src = w2h[st, hf * 32:(hf + 1) * 32].flatten()[None, :]
                src = src.partition_broadcast(128)[:, 0, :]
                e.dma_start(out=WBC[li % 2][:], in_=src).then_inc(*_ti(WBL[li % 2], 16))

        @blk.vector
        def _(e):
            e.memset(EST[:], 0.0).then_inc(*_ti(ESB, 1))
            _wge(e, ESB, 1)
            e.memset(EST[:, 15:16], float(ISQ)).then_inc(*_ti(ESB, 1))
            _wge(e, KVL, 16 * 4)
            def acc_round(pi):
                _wge(e, SMM, pi + 1)
                if pi == 0:
                    e.tensor_copy(out=SCACC[0][:],
                                  in_=PSC[0][:]).then_inc(*_ti(SAD, 1))
                else:
                    _wge(e, SAD, pi)
                    e.tensor_add(SCACC[pi % 2][:], SCACC[(pi - 1) % 2][:],
                                 PSC[pi % 2][:]).then_inc(*_ti(SAD, 1))

            for pi, (s, t) in enumerate(pairs):
                _wge(e, QSL, 16 * 4 * (s + 1))
                if pi >= 2:
                    _wge(e, SMM, pi - 1)
                e.tensor_mul(PR[pi % 2][:].rearrange(
                    "p (kv sl t) -> p kv sl t", kv=8, sl=4),
                    QS[:], kb(t)).then_inc(*_ti(PRD, 1))
                if pi >= 1:
                    acc_round(pi - 1)
            acc_round(9)
            _wge(e, SAD, 10)
            e.memset(W2[:], 0.0).then_inc(*_ti(SFX, 1))
            _wge(e, SFX, 1)
            e.memset(W2[:, :, 0:1], 1.0)
            e.memset(W2[:, :, 12:13], 1.0).then_inc(*_ti(SFX, 1))
            for si in range(1, 4):
                c0 = si * (si + 1) // 2
                base = 5 * (si - 1) + 2
                if si >= 2:
                    _wge(e, SFA, si - 1)
                e.reduce_max(out=NMX1[:], in_=SCA[:, :, c0:c0 + si + 1],
                             axis=AX.X, negate=True).then_inc(*_ti(SFX, 1))
                _wge(e, SFX, base + 1)
                e.tensor_add(SCW[:, :, 0:si + 1], SCA[:, :, c0:c0 + si + 1],
                             NMX1[:, :, None].broadcast_to([128, 64, si + 1])
                             ).then_inc(*_ti(SFX, 1))
                _wge(e, SFA, si)
                e.reduce_sum(out=Z1[:], in_=W2[:, :, c0:c0 + si + 1],
                             axis=AX.X).then_inc(*_ti(SFX, 1))
                _wge(e, SFX, base + 3)
                e.reciprocal(out=RZ1[:], in_=Z1[:]).then_inc(*_ti(SFX, 1))
                _wge(e, SFX, base + 4)
                e.tensor_copy(out=W2[:, :, 12 + si:13 + si],
                              in_=RZ1[:, :, None]).then_inc(*_ti(SFX, 1))
            _wge(e, KVL, 16 * 8)
            av = 0
            for li, (s, t, hf) in enumerate(AVSCH):
                kv0 = hf * 4
                _wge(e, WBL[li % 2], 16 * (li // 2 + 1))
                wv = WBC[li % 2][:].rearrange(
                    "p (kv sl t) -> p kv sl t", kv=4, sl=4)
                aov = ao_view(s)[:, kv0:kv0 + 4, :, :]
                if t is None:
                    _wge(e, AVV, av)
                    e.tensor_mul(aov, aov, wv).then_inc(*_ti(AVV, 1))
                    av += 1
                elif t == 0:
                    e.tensor_mul(aov, vb(t)[:, kv0:kv0 + 4, :, :],
                                 wv).then_inc(*_ti(AVV, 1))
                    av += 1
                else:
                    tv = TMP[:].rearrange("p (kv sl t) -> p kv sl t",
                                          kv=4, sl=4)
                    _wge(e, AVV, av)
                    e.tensor_mul(tv, vb(t)[:, kv0:kv0 + 4, :, :],
                                 wv).then_inc(*_ti(AVV, 1))
                    av += 1
                    _wge(e, AVV, av)
                    e.tensor_add(aov, aov, tv).then_inc(*_ti(AVV, 1))
                    av += 1

        @blk.tensor
        def _(e):
            _wge(e, ESB, 2)
            for pi, (s, t) in enumerate(pairs):
                _wge(e, PRD, pi + 1)
                if pi >= 2:
                    _wge(e, SAD, pi - 1)
                cl = col[(s, t)]
                for bb in range(64):
                    ins = e.matmul(PSC[pi % 2][:, bb, :],
                                   PR[pi % 2][:, bb * 128:(bb + 1) * 128],
                                   EST[:, 15 - cl:31 - cl],
                                   start=True, stop=True)
                ins.then_inc(*_ti(SMM, 1))
            _wge(e, SFX, 17)
            _wge(e, SFA, 3)
            for grp in range(8):
                if grp >= 1:
                    _wge(e, W3E, grp)
                for j in range(8):
                    bb = grp * 8 + j
                    ins = e.transpose(PW3[:, j * 128:(j + 1) * 128],
                                      W2[:, bb, :], idt[:])
                ins.then_inc(*_ti(TW2, 1))

        @blk.scalar
        def _(e):
            for si in range(1, 4):
                c0 = si * (si + 1) // 2
                _wge(e, SFX, 5 * (si - 1) + 4)
                e.activation(out=W2[:, :, c0:c0 + si + 1],
                             in_=SCW[:, :, 0:si + 1],
                             func=AF.Exp).then_inc(*_ti(SFA, 1))
            for grp in range(8):
                _wge(e, TW2, grp + 1)
                if grp >= 1:
                    _wge(e, W3S, 16 * grp)
                e.copy(out=W3STG[:],
                       in_=PW3[:].rearrange("p (c w) -> p c w", c=8)
                       ).then_inc(*_ti(W3E, 1))

        @blk.gpsimd
        def _(e):
            for grp in range(8):
                _wge(e, W3E, grp + 1)
                e.dma_start(out=w2h[:, grp * 8:(grp + 1) * 8, :],
                            in_=W3STG[:]).then_inc(*_ti(W3S, 16))

    emit_densewT(
        nc, blk, f"{tag}C", wproj_hbm, NCH,
        lambda hc, tt: AO[:, hc, tt * 512:(tt + 1) * 512],
        lambda ob, tt: s1_hbm[ob, :, tt * 512:(tt + 1) * 512],
        False, gate=(AVV, AV_TOTAL), TW=TW, QST=QST, PWQ=PWQ, PQ=PQ, idt=idt,
        IDS=IDS, WF=WFs, WB=WBs)
    sc.close()



def build_l1attn(tag="l1a"):
    nc = bass.Bass("TRN2", target_bir_lowering=False, debug=False)
    xnT = nc.dram_tensor("xnT", [NCH, 128, TC], BF16, kind="ExternalInput").ap()
    wqkv = nc.dram_tensor("wqkv", [6144, D], F32, kind="ExternalInput").ap()
    wproj = nc.dram_tensor("wproj", [D, D], F32, kind="ExternalInput").ap()
    s1 = nc.dram_tensor("s1", [NCH, 128, TC], BF16, kind="ExternalOutput").ap()
    with nc.Block() as blk:
        emit_l1attn(nc, blk, xnT, wqkv, wproj, s1, tag)
    return nc


INTER = 11008
NTP = 11


def _tp_cols(tp):
    n0 = tp * 1024
    w = min(1024, INTER - n0)
    return n0, w, w // 8


def emit_ffn1(nc, blk, xnT_hbm, gq, gs, gz, uq, us, uz, dq, ds, dz, g1_hbm, tag):
    sc = _Scope(nc)
    IDS = _vs(nc, f"{tag}_IDS")
    idt = _ident(nc, blk, IDS, sc, tag)
    interh = nc.dram_tensor(f"{tag}_interh", [88, 128, TC], BF16,
                            kind="Internal").ap()

    G16 = sc.sb(f"{tag}_g16", [128, 8], BF16)
    GI = sc.sb(f"{tag}_gi", [128, 8], I32)
    GI2 = sc.sb(f"{tag}_gi2", [128, 8], I32)
    GI3 = sc.sb(f"{tag}_gi3", [128, 8], I32)

    GS16 = _vs(nc, f"{tag}_GS16")

    @blk.gpsimd
    def _(g):
        g.iota(GI[:], pattern=[[-16, 8]], base=0,
               channel_multiplier=1).then_inc(*_ti(IDS, 1))

    @blk.vector
    def _(v):
        _wge(v, IDS, 3)
        v.tensor_scalar(out=GI2[:], in0=GI[:], scalar1=0, scalar2=None,
                        op0=ALU.is_ge).then_inc(*_ti(GS16, 1))
        _wge(v, GS16, 1)
        v.tensor_scalar(out=GI3[:], in0=GI[:], scalar1=16, scalar2=None,
                        op0=ALU.is_lt).then_inc(*_ti(GS16, 1))
        _wge(v, GS16, 2)
        v.tensor_tensor(out=G16[:], in0=GI2[:], in1=GI3[:],
                        op=ALU.mult).then_inc(*_ti(GS16, 1))

    XLF = _vs(nc, f"{tag}_XLF")
    ZQS = _vs(nc, f"{tag}_ZQS")
    SXM = _vs(nc, f"{tag}_SXM")
    SXE = _vs(nc, f"{tag}_SXE")
    SXD = _vs(nc, f"{tag}_SXD", swdge=True)
    SCLD = _vs(nc, f"{tag}_SCLD")
    QWL = [_vs(nc, f"{tag}_QWL{i}") for i in range(2)]
    SBL = [_vs(nc, f"{tag}_SBL{i}") for i in range(2)]
    ZPP = _vs(nc, f"{tag}_ZPP")
    UPK = _vs(nc, f"{tag}_UPK")
    SCL = _vs(nc, f"{tag}_SCL")
    MM1 = _vs(nc, f"{tag}_MM1")
    GEVC = _vs(nc, f"{tag}_GEVC")
    UEVC = _vs(nc, f"{tag}_UEVC")
    IST = [_vs(nc, f"{tag}_IST{i}", swdge=True) for i in range(2)]

    from contextlib import ExitStack
    _st = ExitStack()
    with _st:
        XNT1 = _st.enter_context(
            nc.sbuf_tensor(f"{tag}_xnt1", [128, NCH, TC], BF16))
        UB = _st.enter_context(
            nc.sbuf_tensor(f"{tag}_ub", [128, NCH, 1024], BF16))
        SGT = _st.enter_context(nc.sbuf_tensor(f"{tag}_sgt", [128, 8, TC], BF16))
        SXA = _st.enter_context(nc.sbuf_tensor(f"{tag}_sxa", [32, TC], BF16))
        ESX = _st.enter_context(nc.sbuf_tensor(f"{tag}_esx", [8, TC], BF16))
        ZQG = _st.enter_context(nc.sbuf_tensor(f"{tag}_zqg", [32, 1376], I32))
        ZQU = _st.enter_context(nc.sbuf_tensor(f"{tag}_zqu", [32, 1376], I32))
        QWa = _st.enter_context(nc.sbuf_tensor(f"{tag}_qw0", [128, 1024], I32))
        QWb = _st.enter_context(nc.sbuf_tensor(f"{tag}_qw1", [128, 1024], I32))
        SBa = _st.enter_context(nc.sbuf_tensor(f"{tag}_sbt0", [128, 1024], F16))
        SBb = _st.enter_context(nc.sbuf_tensor(f"{tag}_sbt1", [128, 1024], F16))
        UI = _st.enter_context(nc.sbuf_tensor(f"{tag}_ui", [128, 1024], I32))
        SCS = _st.enter_context(nc.sbuf_tensor(f"{tag}_scs", [32, 1024], F16))
        ZI = _st.enter_context(nc.sbuf_tensor(f"{tag}_zi", [32, 1024], I32))
        ZF = _st.enter_context(nc.sbuf_tensor(f"{tag}_zf", [32, 1024], F32))
        ZPN = _st.enter_context(nc.sbuf_tensor(f"{tag}_zpn", [32, 1024], BF16))
        ISa = _st.enter_context(nc.sbuf_tensor(f"{tag}_inst0", [128, 512], BF16))
        ISb = _st.enter_context(nc.sbuf_tensor(f"{tag}_inst1", [128, 512], BF16))
        P1a = _st.enter_context(nc.psum_tensor(f"{tag}_p1a", [128, 512], F32))
        P1b = _st.enter_context(nc.psum_tensor(f"{tag}_p1b", [128, 512], F32))
        P1c = _st.enter_context(nc.psum_tensor(f"{tag}_p1c", [128, 512], F32))
        P1d = _st.enter_context(nc.psum_tensor(f"{tag}_p1d", [128, 512], F32))
        PSX = _st.enter_context(nc.psum_tensor(f"{tag}_psx", [8, TC], F32))
        QW = [QWa, QWb]
        SBT = [SBa, SBb]
        INST = [ISa, ISb]
        P1 = [[P1a, P1b], [P1c, P1d]]
        ZQ = [ZQG, ZQU]
        mats = [(gq, gs, gz), (uq, us, uz)]

        @blk.sync
        def _(e):
            e.dma_start(out=XNT1[:], in_=xnT_hbm.transpose([1, 0, 2])
                        ).then_inc(*_ti(XLF, 16))
            e.dma_start(out=ZQG[:], in_=gz).then_inc(*_ti(ZQS, 16))
            e.dma_start(out=ZQU[:], in_=uz).then_inc(*_ti(ZQS, 16))
            k = 0
            for tp in range(NTP):
                n0, w, mr = _tp_cols(tp)
                for mi in range(2):
                    i = 2 * tp + mi
                    mq, ms, mz = mats[mi]
                    if i >= 1:
                        _wge(e, ZPP, 3 * i)
                    e.dma_start(out=SCS[:, 0:w],
                                in_=ms[:, n0:n0 + w]).then_inc(*_ti(SCLD, 16))
                    for T in range(4):
                        if k >= 2:
                            _wge(e, UPK, 8 * (k - 1))
                            _wge(e, SCL, 8 * (k - 1))
                        e.dma_start(out=QW[k % 2][:, 0:w],
                                    in_=mq[T * 128:(T + 1) * 128, n0:n0 + w]
                                    ).then_inc(*_ti(QWL[k % 2], 16))
                        src = ms[8 * T:8 * T + 8, n0:n0 + w]
                        src = src[:, None, :].partition_broadcast(16)
                        src = src.transpose([1, 0, 2, 3])[:, :, 0, :]
                        e.dma_start(out=SBT[k % 2][:, 0:w],
                                    in_=src).then_inc(*_ti(SBL[k % 2], 16))
                        k += 1

        @blk.vector
        def _(e):
            for T in range(4):
                _wge(e, SXM, T + 1)
                if T >= 1:
                    _wge(e, SXD, 16 * T)
                e.tensor_copy(out=ESX[:], in_=PSX[:]).then_inc(*_ti(SXE, 1))
            k = 0
            for tp in range(NTP):
                n0, w, mr = _tp_cols(tp)
                for mi in range(2):
                    i = 2 * tp + mi
                    _wge(e, ZQS, 32)
                    _wge(e, SCLD, 16 * (i + 1))
                    if i >= 1:
                        _wge(e, MM1, 16 * i)
                    for jn in range(8):
                        ins = e.tensor_scalar(
                            out=ZI[:, jn:w:8],
                            in0=ZQ[mi][:, n0 // 8:(n0 + w) // 8],
                            scalar1=4 * jn, scalar2=15,
                            op0=ALU.logical_shift_right, op1=ALU.bitwise_and)
                    ins.then_inc(*_ti(ZPP, 1))
                    _wge(e, ZPP, 3 * i + 1)
                    e.tensor_scalar(out=ZF[:, 0:w], in0=ZI[:, 0:w], scalar1=1,
                                    scalar2=-1.0, op0=ALU.add,
                                    op1=ALU.mult).then_inc(*_ti(ZPP, 1))
                    _wge(e, ZPP, 3 * i + 2)
                    e.tensor_mul(ZPN[:, 0:w], ZF[:, 0:w],
                                 SCS[:, 0:w]).then_inc(*_ti(ZPP, 1))
                    for T in range(4):
                        _wge(e, QWL[k % 2], 16 * (k // 2 + 1))
                        _wge(e, SBL[k % 2], 16 * (k // 2 + 1))
                        if T == 0:
                            _wge(e, MM1, 16 * i)
                        for j in range(8):
                            kc = 8 * T + j
                            if k + j > 0:
                                _wge(e, SCL, 8 * k + j)
                            e.tensor_scalar(
                                out=UI[:, 0:w], in0=QW[k % 2][:, 0:w],
                                scalar1=4 * j, scalar2=15,
                                op0=ALU.logical_shift_right,
                                op1=ALU.bitwise_and).then_inc(*_ti(UPK, 1))
                            _wge(e, UPK, 8 * k + j + 1)
                            e.tensor_mul(UB[:, kc, 0:w], UI[:, 0:w],
                                         SBT[k % 2][:, 0:w]).then_inc(*_ti(SCL, 1))
                        k += 1
                    if mi == 1:
                        for jp in range(8):
                            st = 8 * tp + jp
                            for tt in range(2):
                                _wge(e, MM1, 16 * i + 2 * jp + tt + 1)
                                _wge(e, GEVC, 16 * tp + 2 * jp + 2)
                                sti = 2 * st + tt
                                _wge(e, IST[sti % 2], 16 * (sti // 2))
                                e.tensor_mul(
                                    INST[sti % 2][0:mr, :],
                                    P1[jp % 2][tt][0:mr, :],
                                    SGT[0:mr, jp, tt * 512:(tt + 1) * 512]
                                ).then_inc(*_ti(UEVC, 1))

        @blk.scalar
        def _(e):
            for tp in range(NTP):
                n0, w, mr = _tp_cols(tp)
                i = 2 * tp
                for jp in range(8):
                    for tt in range(2):
                        _wge(e, MM1, 16 * i + 2 * jp + tt + 1)
                        e.activation(
                            out=SGT[0:mr, jp, tt * 512:(tt + 1) * 512],
                            in_=P1[jp % 2][tt][0:mr, :],
                            func=AF.Silu).then_inc(*_ti(GEVC, 1))

        @blk.tensor
        def _(e):
            _wge(e, GS16, 3)
            _wge(e, XLF, 16)
            for T in range(4):
                if T >= 1:
                    _wge(e, SXE, T)
                for tt in range(2):
                    for j in range(8):
                        kc = 8 * T + j
                        ins = e.matmul(PSX[:, tt * 512:(tt + 1) * 512],
                                       G16[:],
                                       XNT1[:, kc, tt * 512:(tt + 1) * 512],
                                       start=(j == 0), stop=(j == 7))
                ins.then_inc(*_ti(SXM, 1))
            for tp in range(NTP):
                n0, w, mr = _tp_cols(tp)
                for mi in range(2):
                    i = 2 * tp + mi
                    _wge(e, SCL, 32 * (i + 1))
                    _wge(e, ZPP, 3 * i + 3)
                    if i == 0:
                        _wge(e, SXD, 16 * 4)
                    for jp in range(8):
                        for tt in range(2):
                            if jp >= 2:
                                pi_, pjp = i, jp - 2
                            else:
                                pi_, pjp = i - 1, jp + 6
                            if pi_ >= 0:
                                psem = GEVC if pi_ % 2 == 0 else UEVC
                                _wge(e, psem, 16 * (pi_ // 2) + 2 * pjp + tt + 1)
                            for kc in range(NCH):
                                e.matmul(
                                    P1[jp % 2][tt][0:mr, :],
                                    UB[:, kc, jp:w:8],
                                    XNT1[:, kc, tt * 512:(tt + 1) * 512],
                                    start=(kc == 0), stop=False)
                            e.matmul(
                                P1[jp % 2][tt][0:mr, :],
                                ZPN[:, jp:w:8],
                                SXA[:, tt * 512:(tt + 1) * 512],
                                start=False, stop=True).then_inc(*_ti(MM1, 1))

        @blk.gpsimd
        def _(e):
            for T in range(4):
                _wge(e, SXE, T + 1)
                e.dma_start(out=SXA[8 * T:8 * T + 8, :],
                            in_=ESX[:]).then_inc(*_ti(SXD, 16))
            nst = 0
            for tp in range(NTP):
                n0, w, mr = _tp_cols(tp)
                for jp in range(8):
                    st = 8 * tp + jp
                    for tt in range(2):
                        sti = 2 * st + tt
                        _wge(e, UEVC, 16 * tp + 2 * jp + tt + 1)
                        e.dma_start(
                            out=interh[st, 0:mr, tt * 512:(tt + 1) * 512],
                            in_=INST[sti % 2][0:mr, :]
                        ).then_inc(*_ti(IST[sti % 2], 16))
                        nst += 1
            _wge(e, IST[0], 16 * (nst // 2))
            _wge(e, IST[1], 16 * (nst // 2))

    ILD = _vs(nc, f"{tag}_ILD")
    QDL = _vs(nc, f"{tag}_QDL")
    SDL = _vs(nc, f"{tag}_SDL")
    ZPD2 = _vs(nc, f"{tag}_ZPD2")
    UPK2 = _vs(nc, f"{tag}_UPK2")
    SCL2 = _vs(nc, f"{tag}_SCL2")
    MM2 = _vs(nc, f"{tag}_MM2")
    ADC = _vs(nc, f"{tag}_ADC")
    SX2M = _vs(nc, f"{tag}_SX2M")
    SX2E = _vs(nc, f"{tag}_SX2E")
    SX2D = _vs(nc, f"{tag}_SX2D", swdge=True)
    FEV = _vs(nc, f"{tag}_FEV")
    FSL = _vs(nc, f"{tag}_FSL")
    GST2 = [_vs(nc, f"{tag}_GST2{i}", swdge=True) for i in range(2)]

    ACC = sc.sb(f"{tag}_acc", [128, NCH, TC], BF16)
    ITL = sc.sb(f"{tag}_itl", [128, 8, TC], BF16)
    QWD = sc.sb(f"{tag}_qwd", [128, D], I32)
    SD = sc.sb(f"{tag}_sd", [128, D], F16)
    UID = sc.sb(f"{tag}_uid", [128, D], I32)
    SXD2 = sc.sb(f"{tag}_sxd2", [96, TC], BF16)
    ESX2 = sc.sb(f"{tag}_esx2", [8, TC], BF16)
    ZQD = sc.sb(f"{tag}_zqd", [86, 512], I32)
    ZPND = sc.sb(f"{tag}_zpnd", [86, D], BF16)
    GS2 = [sc.sb(f"{tag}_gs2{i}", [128, 512], F16)
           for i in range(2)]
    GS3 = [sc.sb(f"{tag}_gs3{i}", [128, 512], F16)
           for i in range(2)]

    with nc.sbuf_tensor(f"{tag}_zid", [86, D], I32) as ZID, \
         nc.sbuf_tensor(f"{tag}_zfd", [86, D], F32) as ZFD, \
         nc.sbuf_tensor(f"{tag}_scd", [86, D], F16) as SCD:
        @blk.sync
        def _(e):
            _wge(e, IST[0], 16 * 88)
            _wge(e, IST[1], 16 * 88)
            e.dma_start(out=ZQD[:], in_=dz).then_inc(*_ti(ZPD2, 16))
            e.dma_start(out=SCD[:], in_=ds).then_inc(*_ti(ZPD2, 16))

        @blk.vector
        def _(e):
            _wge(e, ZPD2, 32)
            for jn in range(8):
                ins = e.tensor_scalar(
                    out=ZID[:, jn:D:8], in0=ZQD[:],
                    scalar1=4 * jn, scalar2=15,
                    op0=ALU.logical_shift_right, op1=ALU.bitwise_and)
            ins.then_inc(*_ti(ZPD2, 1))
            _wge(e, ZPD2, 33)
            e.tensor_scalar(out=ZFD[:], in0=ZID[:], scalar1=1, scalar2=-1.0,
                            op0=ALU.add, op1=ALU.mult).then_inc(*_ti(ZPD2, 1))
            _wge(e, ZPD2, 34)
            e.tensor_mul(ZPND[:], ZFD[:], SCD[:]).then_inc(*_ti(ZPD2, 1))

    UBD = sc.sb(f"{tag}_ubd", [128, 8, D], BF16)
    PD = [sc.ps(f"{tag}_pd{i}", [128, 512], F32)
          for i in range(4)]
    PSX2 = sc.ps(f"{tag}_psx2", [8, TC], F32)

    @blk.sync
    def _(e):
        _wge(e, ZPD2, 35)
        for tpp in range(NTP):
            mr2 = 128 if tpp < 10 else 96
            glo = 8 * tpp
            if tpp >= 1:
                _wge(e, MM2, 64 * tpp)
                _wge(e, SX2M, tpp)
            e.dma_start(out=ITL[0:mr2, :, :],
                        in_=interh[glo:glo + 8, 0:mr2, :].transpose([1, 0, 2])
                        ).then_inc(*_ti(ILD, 16))
            e.dma_start(out=QWD[0:mr2, :],
                        in_=dq[128 * tpp:128 * tpp + mr2, :]).then_inc(*_ti(QDL, 16))
            ng = 8 if tpp < 10 else 6
            src = ds[8 * tpp:8 * tpp + ng, :]
            src = src[:, None, :].partition_broadcast(16)
            src = src.transpose([1, 0, 2, 3])[:, :, 0, :]
            e.dma_start(out=SD[0:16 * ng, :],
                        in_=src).then_inc(*_ti(SDL, 16))

    @blk.vector
    def _(e):
        for tpp in range(NTP):
            mr2 = 128 if tpp < 10 else 96
            _wge(e, QDL, 16 * (tpp + 1))
            _wge(e, SDL, 16 * (tpp + 1))
            for j in range(8):
                if 8 * tpp + j > 0:
                    _wge(e, SCL2, 8 * tpp + j)
                e.tensor_scalar(out=UID[0:mr2, :], in0=QWD[0:mr2, :],
                                scalar1=4 * j, scalar2=15,
                                op0=ALU.logical_shift_right,
                                op1=ALU.bitwise_and).then_inc(*_ti(UPK2, 1))
                _wge(e, UPK2, 8 * tpp + j + 1)
                e.tensor_mul(UBD[0:mr2, j, :], UID[0:mr2, :],
                             SD[0:mr2, :]).then_inc(*_ti(SCL2, 1))
            _wge(e, SX2M, tpp + 1)
            if tpp >= 1:
                _wge(e, SX2D, 16 * tpp)
            e.tensor_copy(out=ESX2[:], in_=PSX2[:]).then_inc(*_ti(SX2E, 1))
            for oc in range(NCH):
                for tt in range(2):
                    idx = 64 * tpp + 2 * oc + tt
                    _wge(e, MM2, idx + 1)
                    if tpp == 0:
                        e.tensor_copy(out=ACC[:, oc, tt * 512:(tt + 1) * 512],
                                      in_=PD[idx % 4][:]).then_inc(*_ti(ADC, 1))
                    else:
                        e.tensor_add(ACC[:, oc, tt * 512:(tt + 1) * 512],
                                     ACC[:, oc, tt * 512:(tt + 1) * 512],
                                     PD[idx % 4][:]).then_inc(*_ti(ADC, 1))
        for oc in range(NCH):
            for tt in range(2):
                idx = 64 * NTP + 2 * oc + tt
                _wge(e, MM2, idx + 1)
                e.tensor_add(ACC[:, oc, tt * 512:(tt + 1) * 512],
                             ACC[:, oc, tt * 512:(tt + 1) * 512],
                             PD[idx % 4][:]).then_inc(*_ti(ADC, 1))

    @blk.tensor
    def _(e):
        for tpp in range(NTP):
            mr2 = 128 if tpp < 10 else 96
            ng = 8 if tpp < 10 else 6
            _wge(e, ILD, 16 * (tpp + 1))
            if tpp >= 1:
                _wge(e, SX2E, tpp)
            for tt in range(2):
                for j in range(8):
                    ins = e.matmul(PSX2[0:ng, tt * 512:(tt + 1) * 512],
                                   G16[0:mr2, 0:ng],
                                   ITL[0:mr2, j, tt * 512:(tt + 1) * 512],
                                   start=(j == 0), stop=(j == 7))
            ins.then_inc(*_ti(SX2M, 1))
            _wge(e, SCL2, 8 * (tpp + 1))
            for oc in range(NCH):
                for tt in range(2):
                    idx = 64 * tpp + 2 * oc + tt
                    if idx >= 4:
                        _wge(e, ADC, idx - 3)
                    for j in range(8):
                        ins = e.matmul(PD[idx % 4][:],
                                       UBD[0:mr2, j, oc * 128:(oc + 1) * 128],
                                       ITL[0:mr2, j, tt * 512:(tt + 1) * 512],
                                       start=(j == 0), stop=(j == 7))
                    ins.then_inc(*_ti(MM2, 1))
        _wge(e, SX2D, 16 * NTP)
        _wge(e, ZPD2, 35)
        for oc in range(NCH):
            for tt in range(2):
                idx = 64 * NTP + 2 * oc + tt
                _wge(e, ADC, idx - 3)
                e.matmul(PD[idx % 4][:],
                         ZPND[:, oc * 128:(oc + 1) * 128],
                         SXD2[0:86, tt * 512:(tt + 1) * 512],
                         start=True, stop=True).then_inc(*_ti(MM2, 1))

    @blk.gpsimd
    def _(e):
        for tpp in range(NTP):
            _wge(e, SX2E, tpp + 1)
            ng = 8 if tpp < 10 else 6
            e.dma_start(out=SXD2[8 * tpp:8 * tpp + ng, :],
                        in_=ESX2[0:ng, :]).then_inc(*_ti(SX2D, 16))
        nst = 0
        for oc in range(NCH):
            for tt in range(2):
                kk = 2 * oc + tt
                _wge(e, FEV, kk + 1)
                e.dma_start(out=g1_hbm[oc, :, tt * 512:(tt + 1) * 512],
                            in_=GS3[kk % 2][:]).then_inc(*_ti(GST2[kk % 2], 16))
                nst += 1
        _wge(e, GST2[0], 16 * (nst // 2))
        _wge(e, GST2[1], 16 * (nst // 2))

    @blk.scalar
    def _(e):
        for oc in range(NCH):
            for tt in range(2):
                kk = 2 * oc + tt
                _wge(e, ADC, 64 * (NTP + 1))
                if kk >= 2:
                    _wge(e, FEV, kk - 1)
                e.copy(out=GS2[kk % 2][:],
                       in_=ACC[:, oc, tt * 512:(tt + 1) * 512]).then_inc(*_ti(FSL, 1))
                _wge(e, FSL, kk + 1)
                if kk >= 2:
                    _wge(e, GST2[kk % 2], 16 * (kk // 2))
                e.activation(out=GS3[kk % 2][:], in_=GS2[kk % 2][:],
                             func=AF.Silu).then_inc(*_ti(FEV, 1))
    sc.close()



def build_ffn1(tag="f1"):
    nc = bass.Bass("TRN2", target_bir_lowering=False, debug=False)
    xnT = nc.dram_tensor("xnT", [NCH, 128, TC], BF16, kind="ExternalInput").ap()
    gq = nc.dram_tensor("gq", [512, INTER], I32, kind="ExternalInput").ap()
    gs = nc.dram_tensor("gs", [32, INTER], F16, kind="ExternalInput").ap()
    gz = nc.dram_tensor("gz", [32, 1376], I32, kind="ExternalInput").ap()
    uq = nc.dram_tensor("uq", [512, INTER], I32, kind="ExternalInput").ap()
    us = nc.dram_tensor("us", [32, INTER], F16, kind="ExternalInput").ap()
    uz = nc.dram_tensor("uz", [32, 1376], I32, kind="ExternalInput").ap()
    dq = nc.dram_tensor("dq", [1376, D], I32, kind="ExternalInput").ap()
    ds = nc.dram_tensor("ds", [86, D], F16, kind="ExternalInput").ap()
    dz = nc.dram_tensor("dz", [86, 512], I32, kind="ExternalInput").ap()
    g1 = nc.dram_tensor("g1", [NCH, 128, TC], F16, kind="ExternalOutput").ap()
    with nc.Block() as blk:
        emit_ffn1(nc, blk, xnT, gq, gs, gz, uq, us, uz, dq, ds, dz, g1, tag)
    return nc


def emit_qkvkv(nc, blk, xnTf_hbm, w_hbm, b_hbm, kf_hbm, vf_hbm, tag):
    sc = _Scope(nc)
    IDS = _vs(nc, f"{tag}_IDS")
    idt = _ident(nc, blk, IDS, sc, tag)

    BS_ = _vs(nc, f"{tag}_BS")
    LW = [_vs(nc, f"{tag}_LW{i}") for i in range(2)]
    CVW = _vs(nc, f"{tag}_CVW")
    XLD = _vs(nc, f"{tag}_XLD")
    MM = _vs(nc, f"{tag}_MM")
    EVB = _vs(nc, f"{tag}_EVB")
    SQK = [_vs(nc, f"{tag}_SQK{i}", swdge=True) for i in range(2)]
    VEV = _vs(nc, f"{tag}_VEV")
    TRV = _vs(nc, f"{tag}_TRV")
    EV2 = _vs(nc, f"{tag}_EV2")
    VS = [_vs(nc, f"{tag}_VS{i}", swdge=True) for i in range(2)]

    BIAS = sc.sb(f"{tag}_bias", [128, 64], F32)
    WF = [sc.sb(f"{tag}_wf{i}", [128, 512], F32) for i in range(2)]
    WGB = [sc.sb(f"{tag}_wgb{i}", [128, NCH, 512], BF16) for i in range(2)]
    TAT = 2048
    XR = sc.sb(f"{tag}_xr", [128, NCH, TAT], BF16)
    STQ = [sc.sb(f"{tag}_stq{i}", [128, 512], BF16) for i in range(2)]
    VSTG = [sc.sb(f"{tag}_vstg{i}", [128, 512], BF16) for i in range(2)]
    VST2 = [sc.sb(f"{tag}_vst2{i}", [128, 4, 128], BF16) for i in range(2)]
    PS = [sc.ps(f"{tag}_ps{i}", [128, 512], F32) for i in range(8)]
    PTV = [PS[6][:].bitcast(BF16), PS[7][:].bitcast(BF16)]

    groups = [list(range(32 + g * 4, 32 + g * 4 + 4)) for g in range(8)]
    vstart = [64 + 3 * g for g in range(10)] + [94]
    groups += [list(range(st, min(st + 3, 96))) for st in vstart]
    NG = len(groups)

    def is_v(goc):
        return goc >= 64

    NTC = (B * S) // TAT
    NTL = TAT // 512
    NSPL = 4

    iters = [(tc, gi, tql) for tc in range(NTC) for gi in range(NG)
             for tql in range(NTL)]
    kev_after = []
    vev_after = []
    ke, ve = 0, 0
    for tc, gi, tql in iters:
        if is_v(groups[gi][0]):
            ve += len(groups[gi])
        else:
            ke += len(groups[gi])
        kev_after.append(ke)
        vev_after.append(ve)

    @blk.sync
    def _(e):
        for c in range(64):
            e.dma_start(out=BIAS[:, c:c + 1],
                        in_=b_hbm[(32 + c) * 128:(33 + c) * 128][:, None]
                        ).then_inc(*_ti(BS_, 16))
        kwl = 0
        for tc in range(NTC):
            for gi, g in enumerate(groups):
                c0, c1 = g[0] * 128, (g[-1] + 1) * 128
                for ic in range(NCH):
                    if kwl >= 2:
                        _wge(e, CVW, kwl - 1)
                    e.dma_start(out=WF[kwl % 2][:, 0:(c1 - c0)],
                                in_=w_hbm[ic * 128:(ic + 1) * 128, c0:c1]
                                ).then_inc(*_ti(LW[kwl % 2], 16))
                    kwl += 1

    @blk.gpsimd
    def _(e):
        def loadx(tc):
            if tc >= 1:
                _wge(e, MM, NCH * NG * NTL * tc)
            t0 = tc * TAT
            for sp in range(NSPL):
                nh = NCH // NSPL
                e.dma_start(out=XR[:, sp * nh:(sp + 1) * nh, :],
                            in_=xnTf_hbm[sp * nh:(sp + 1) * nh, :, t0:t0 + TAT]
                            .transpose([1, 0, 2])).then_inc(*_ti(XLD, 16))

        sq = 0
        ev2 = 0
        for q, (tc, gi, tql) in enumerate(iters):
            if gi == 0 and tql == 0:
                loadx(tc)
            g = groups[gi]
            t0 = tc * TAT + tql * 512
            b = t0 // S
            pos = t0 % S
            for j, goc in enumerate(g):
                if not is_v(goc):
                    h = goc - 32
                    _wge(e, EVB, sq + 1)
                    e.dma_start(out=kf_hbm[b, h, :, pos:pos + 512],
                                in_=STQ[sq % 2][:]
                                ).then_inc(*_ti(SQK[sq % 2], 16))
                    sq += 1
                else:
                    h = goc - 64
                    _wge(e, EV2, ev2 + 1)
                    dst = vf_hbm[b, h, pos:pos + 512, :]
                    e.dma_start(out=dst.rearrange("(j p) d -> p j d", j=4),
                                in_=VST2[ev2 % 2][:]
                                ).then_inc(*_ti(VS[ev2 % 2], 16))
                    ev2 += 1
        _wge(e, SQK[0], 16 * ((sq + 1) // 2))
        _wge(e, SQK[1], 16 * (sq // 2))
        _wge(e, VS[0], 16 * ((ev2 + 1) // 2))
        _wge(e, VS[1], 16 * (ev2 // 2))

    @blk.scalar
    def _(e):
        kwl = 0
        for tc in range(NTC):
            for gi, g in enumerate(groups):
                gg = tc * NG + gi
                w = len(g) * 128
                for ic in range(NCH):
                    _wge(e, LW[kwl % 2], 16 * (kwl // 2 + 1))
                    if gg >= 2:
                        _wge(e, MM, NCH * NTL * (gg - 1))
                    e.copy(out=WGB[gg % 2][:, ic, 0:w],
                           in_=WF[kwl % 2][:, 0:w]).then_inc(*_ti(CVW, 1))
                    kwl += 1

    @blk.tensor
    def _(e):
        _wge(e, IDS, 2)
        trv = 0
        for q, (tc, gi, tql) in enumerate(iters):
            g = groups[gi]
            gg = tc * NG + gi
            if gi == 0 and tql == 0:
                _wge(e, XLD, 16 * NSPL * (tc + 1))
                if tc >= 1:
                    _wge(e, EV2, trv)
            if tql == 0:
                _wge(e, CVW, NCH * (gg + 1))
            if q >= 2:
                _wge(e, EVB, kev_after[q - 2])
                _wge(e, VEV, vev_after[q - 2])
            for ic in range(NCH):
                for j, goc in enumerate(g):
                    ins = e.matmul(
                        PS[j * 2 + q % 2][:],
                        WGB[gg % 2][:, ic, j * 128:(j + 1) * 128],
                        XR[:, ic, tql * 512:(tql + 1) * 512],
                        start=(ic == 0), stop=(ic == NCH - 1))
                ins.then_inc(*_ti(MM, 1))
            if is_v(g[0]):
                for idx in range(len(g)):
                    vi = trv
                    _wge(e, VEV, vi + 1)
                    if vi >= 2:
                        _wge(e, EV2, vi - 1)
                    for j4 in range(4):
                        ins = e.transpose(
                            PTV[vi % 2][:, j4 * 128:(j4 + 1) * 128],
                            VSTG[vi % 2][:, j4 * 128:(j4 + 1) * 128],
                            idt[:])
                    ins.then_inc(*_ti(TRV, 1))
                    trv += 1

    @blk.vector
    def _(e):
        _wge(e, BS_, 16 * 64)
        sq = 0
        vev = 0
        ev2 = [0]

        def emit_evac2():
            vi = ev2[0]
            _wge(e, TRV, vi + 1)
            _wge(e, VS[vi % 2], 16 * (vi // 2))
            e.tensor_copy(out=VST2[vi % 2][:],
                          in_=PTV[vi % 2][:, 0:512]).then_inc(*_ti(EV2, 1))
            ev2[0] += 1

        for q, (tc, gi, tql) in enumerate(iters):
            g = groups[gi]
            _wge(e, MM, NCH * (q + 1))
            for j, goc in enumerate(g):
                if not is_v(goc):
                    _wge(e, SQK[sq % 2], 16 * (sq // 2))
                    e.tensor_scalar(
                        out=STQ[sq % 2][:], in0=PS[j * 2 + q % 2][:],
                        scalar1=BIAS[:, goc - 32:goc - 31], scalar2=None,
                        op0=ALU.add).then_inc(*_ti(EVB, 1))
                    sq += 1
                else:
                    _wge(e, TRV, max(0, vev - 1))
                    e.tensor_scalar(
                        out=VSTG[vev % 2][:], in0=PS[j * 2 + q % 2][:],
                        scalar1=BIAS[:, goc - 32:goc - 31], scalar2=None,
                        op0=ALU.add).then_inc(*_ti(VEV, 1))
                    vev += 1
                    if vev >= 3:
                        emit_evac2()
            if gi == NG - 1 and tql == NTL - 1:
                while ev2[0] < vev:
                    emit_evac2()
    sc.close()


def build_fused(nph=10):
    nc = bass.Bass("TRN2", target_bir_lowering=False, debug=False)
    BS_ = B * S
    xfull = nc.dram_tensor("xfull", [BS_, D], F32, kind="ExternalInput").ap()
    xs = nc.dram_tensor("xs", [TC, D], F32, kind="ExternalInput").ap()
    qkvw = nc.dram_tensor("qkvw", [D, 3 * D], F32, kind="ExternalInput").ap()
    qkvb = nc.dram_tensor("qkvb", [3 * D], F32, kind="ExternalInput").ap()
    ow = nc.dram_tensor("ow", [D, D], F32, kind="ExternalInput").ap()
    ob = nc.dram_tensor("ob", [D], F32, kind="ExternalInput").ap()
    w0 = nc.dram_tensor("w0", [D, D], F32, kind="ExternalInput").ap()
    b0 = nc.dram_tensor("b0", [D], F32, kind="ExternalInput").ap()
    gw0 = nc.dram_tensor("gw0", [8, D], F32, kind="ExternalInput").ap()
    ics = nc.dram_tensor("ics", [8, D], F32, kind="ExternalInput").ap()
    ocs = nc.dram_tensor("ocs", [8, D], F32, kind="ExternalInput").ap()
    wqkv1 = nc.dram_tensor("wqkv1", [6144, D], F32, kind="ExternalInput").ap()
    wproj1 = nc.dram_tensor("wproj1", [D, D], F32, kind="ExternalInput").ap()
    gq = nc.dram_tensor("gq", [512, INTER], I32, kind="ExternalInput").ap()
    gs = nc.dram_tensor("gs", [32, INTER], F16, kind="ExternalInput").ap()
    gz = nc.dram_tensor("gz", [32, 1376], I32, kind="ExternalInput").ap()
    uq = nc.dram_tensor("uq", [512, INTER], I32, kind="ExternalInput").ap()
    us = nc.dram_tensor("us", [32, INTER], F16, kind="ExternalInput").ap()
    uz = nc.dram_tensor("uz", [32, 1376], I32, kind="ExternalInput").ap()
    dq = nc.dram_tensor("dq", [1376, D], I32, kind="ExternalInput").ap()
    ds = nc.dram_tensor("ds", [86, D], F16, kind="ExternalInput").ap()
    dz = nc.dram_tensor("dz", [86, 512], I32, kind="ExternalInput").ap()
    out = nc.dram_tensor("out", [TC, D], F32, kind="ExternalOutput").ap()

    xnTf = nc.dram_tensor("i_xnTf", [NCH, 128, BS_], BF16, kind="Internal").ap()
    xnT0 = nc.dram_tensor("i_xnT0", [NCH, 128, TC], BF16, kind="Internal").ap()
    qscr = nc.dram_tensor("i_q", [NCH, 128, TC], BF16, kind="Internal").ap()
    kf = nc.dram_tensor("i_kf", [B, 32, 128, S], BF16, kind="Internal").ap()
    vf = nc.dram_tensor("i_vf", [B, 32, S, 128], BF16, kind="Internal").ap()
    gscr = nc.dram_tensor("i_g", [NCH, 128, TC], BF16, kind="Internal").ap()
    ctxs = nc.dram_tensor("i_ctx", [NCH, 128, TC], BF16, kind="Internal").ap()
    x1s = nc.dram_tensor("i_x1", [TC, D], F32, kind="Internal").ap()
    xn1s = nc.dram_tensor("i_xn1", [NCH, 128, TC], BF16, kind="Internal").ap()
    s1s = nc.dram_tensor("i_s1", [NCH, 128, TC], BF16, kind="Internal").ap()
    g1s = nc.dram_tensor("i_g1", [NCH, 128, TC], F16, kind="Internal").ap()

    _gbar_setup(nc)
    with nc.Block() as blk:
        def phase(fn):
            fn()
            emit_barrier(nc, blk)
            nc._vsem_pool.retire_phase()

        if nph == -2:
            phases = [
                lambda: emit_ln(nc, blk, xs, xnT0, False, "p1"),
                lambda: emit_ln(nc, blk, xs, xnT0, False, "p2b"),
            ]
            for fn in phases:
                phase(fn)
            nph = 0
        if nph == -5:
            def tiny():
                TS5 = _vs(nc, "t5")
                TT5 = nc.alloc_sbuf_tensor("t5t", [128, 64], F32)
                @blk.vector
                def _(e):
                    e.memset(TT5[:], 1.0).then_inc(*_ti(TS5, 1))
                @blk.gpsimd
                def _(e):
                    _wge(e, TS5, 1)
                    e.dma_start(out=xnT0[0, :, 0:64].transpose([0, 1]),
                                in_=TT5[:].bitcast(BF16)[:, 0:64]
                                ).then_inc(*_ti(TS5, 16))
                    _wge(e, TS5, 17)
            phase(tiny)
            phase(lambda: emit_ln(nc, blk, xfull, xnTf, False, "p2",
                                  ntok=BS_))
            nph = 0
        if nph == -6:
            emit_ln(nc, blk, xs, xnT0, False, "p1")
            emit_barrier(nc, blk)
            emit_ln(nc, blk, xfull, xnTf, False, "p2", ntok=BS_)
            emit_barrier(nc, blk)
            nph = 0
        if nph == -7:
            phase(lambda: emit_ln(nc, blk, xfull, xnTf, False, "p2",
                                  ntok=BS_))
            phase(lambda: emit_qkvkv(nc, blk, xnTf, qkvw, qkvb, kf, vf, "p4"))
            nph = 0
        if nph == -8:
            phase(lambda: emit_qkvkv(nc, blk, xnTf, qkvw, qkvb, kf, vf, "p4"))
            nph = 0
        if nph == -9:
            phase(lambda: emit_final(nc, blk, x1s, s1s, g1s, out, "p10"))
            nph = 0
        if nph == -10:
            phase(lambda: emit_ffn1(nc, blk, xn1s, gq, gs, gz, uq, us, uz,
                                    dq, ds, dz, g1s, "p9"))
            phase(lambda: emit_final(nc, blk, x1s, s1s, g1s, out, "p10"))
            nph = 0
        if nph == -3:
            phase(lambda: emit_ln(nc, blk, xfull, xnTf, False, "p2",
                                  ntok=BS_))
            nph = 0
        phases = [
            lambda: emit_ln(nc, blk, xs, xnT0, False, "p1"),
            lambda: emit_ln(nc, blk, xfull, xnTf, False, "p2", ntok=BS_),
            lambda: emit_qkv0(nc, blk, xnT0, qkvw, qkvb, qscr, None, None,
                              "p3", q_only=True),
            lambda: emit_qkvkv(nc, blk, xnTf, qkvw, qkvb, kf, vf, "p4"),
            lambda: emit_ffn0(nc, blk, xnT0, gw0, ics, ocs, w0, b0, gscr,
                              "p5"),
            lambda: emit_attn0(nc, blk, qscr, kf, vf, ctxs, "p6"),
            lambda: emit_combine0(nc, blk, ctxs, ow, ob, gscr, xs, x1s,
                                  xn1s, "p7"),
            lambda: emit_l1attn(nc, blk, xn1s, wqkv1, wproj1, s1s, "p8"),
            lambda: emit_ffn1(nc, blk, xn1s, gq, gs, gz, uq, us, uz,
                              dq, ds, dz, g1s, "p9"),
            lambda: emit_final(nc, blk, x1s, s1s, g1s, out, "p10"),
        ]
        for fn in phases[:nph]:
            phase(fn)
        if nph < 10:
            ZS = nc.alloc_semaphore("zf_s")
            ZD = nc.alloc_semaphore("zf_d")
            ztile = nc.alloc_sbuf_tensor("zf_t", [128, D], F32)

            @blk.vector
            def _(e):
                e.memset(ztile[:], 0.0).then_inc(ZS, 1)

            @blk.gpsimd
            def _(e):
                e.wait_ge(ZS, 1)
                for tt in range(TC // 128):
                    e.dma_start(out=out[tt * 128:(tt + 1) * 128, :],
                                in_=ztile[:]).then_inc(ZD, 16)
                e.wait_ge(ZD, 16 * (TC // 128))
            nc.all_engine_barrier()
    return nc



_nc_cache = {}


def _get_nc(key, builder):
    if key not in _nc_cache:
        _nc_cache[key] = builder()
    return _nc_cache[key]


def _run(key, builder, in_maps):
    nc = _get_nc(key, builder)
    res = run_bass_kernel_spmd(nc, in_maps, core_ids=list(range(NCORES)))
    return res.results


def kernel(L0_attn_x, L0_attn_attn_qkvw, L0_attn_attn_qkvb, L0_attn_attn_ow,
           L0_attn_attn_ob, L0_ffn_weight, L0_ffn_bias, L0_ffn_gate_w,
           L0_ffn_in_channel_scale, L0_ffn_out_channel_scale,
           L1_attn_weight_qkv, L1_attn_weight_proj,
           L1_ffn_gate_qweight, L1_ffn_gate_scales, L1_ffn_gate_zeros,
           L1_ffn_up_qweight, L1_ffn_up_scales, L1_ffn_up_zeros,
           L1_ffn_down_qweight, L1_ffn_down_scales, L1_ffn_down_zeros):
    f32 = lambda a: np.ascontiguousarray(np.asarray(a), dtype=np.float32)
    i32 = lambda a: np.ascontiguousarray(np.asarray(a), dtype=np.int32)
    f16 = lambda a: np.ascontiguousarray(np.asarray(a), dtype=np.float16)

    x = f32(L0_attn_x)
    xfull = np.ascontiguousarray(x.reshape(B * S, D))
    xs = [np.ascontiguousarray(
        x[:, c * PC:(c + 1) * PC, :].reshape(TC, D)) for c in range(NCORES)]

    common = {
        "xfull": xfull,
        "qkvw": f32(L0_attn_attn_qkvw), "qkvb": f32(L0_attn_attn_qkvb),
        "ow": f32(L0_attn_attn_ow), "ob": f32(L0_attn_attn_ob),
        "w0": f32(L0_ffn_weight), "b0": f32(L0_ffn_bias),
        "gw0": f32(L0_ffn_gate_w), "ics": f32(L0_ffn_in_channel_scale),
        "ocs": f32(L0_ffn_out_channel_scale),
        "wqkv1": f32(L1_attn_weight_qkv), "wproj1": f32(L1_attn_weight_proj),
        "gq": i32(L1_ffn_gate_qweight), "gs": f16(L1_ffn_gate_scales),
        "gz": i32(L1_ffn_gate_zeros),
        "uq": i32(L1_ffn_up_qweight), "us": f16(L1_ffn_up_scales),
        "uz": i32(L1_ffn_up_zeros),
        "dq": i32(L1_ffn_down_qweight), "ds": f16(L1_ffn_down_scales),
        "dz": i32(L1_ffn_down_zeros),
    }
    in_maps = [dict(common, xs=xs[c]) for c in range(NCORES)]

    nc = _get_nc("fused", build_fused)
    res = run_bass_kernel_spmd(nc, in_maps, core_ids=list(range(NCORES)))
    out = np.empty((B, S, D), np.float32)
    for c in range(NCORES):
        out[:, c * PC:(c + 1) * PC, :] = np.asarray(
            res.results[c]["out"]).reshape(B, PC, D)
    return out


# version 08 (final)
# speedup vs baseline: 1.2910x; 1.0106x over previous
import math
import numpy as np
import ml_dtypes
import concourse.bass as bass
import concourse.mybir as mybir
from concourse.bass_utils import run_bass_kernel_spmd

F32 = mybir.dt.float32
BF16 = mybir.dt.bfloat16
F16 = mybir.dt.float16
I32 = mybir.dt.int32
AF = mybir.ActivationFunctionType
ALU = mybir.AluOpType
AX = mybir.AxisListType

NCORES = 8
B = 4
S = 2048
D = 4096
PC = S // NCORES
TC = B * PC
NCH = D // 128
EPS = 1e-5
ISQ = 1.0 / math.sqrt(128.0)

bf16 = ml_dtypes.bfloat16


from contextlib import ExitStack


class _Scope:

    def __init__(self, nc):
        self.nc = nc
        self.st = ExitStack()

    def sb(self, name, shape, dt):
        return self.st.enter_context(self.nc.sbuf_tensor(name, shape, dt))

    def ps(self, name, shape, dt=F32):
        return self.st.enter_context(self.nc.psum_tensor(name, shape, dt))

    def close(self):
        self.st.close()


class VSem:

    __slots__ = ("h", "base", "emitted")

    def __init__(self, h):
        self.h = h
        self.base = 0
        self.emitted = 0


class VSemPool:
    def __init__(self, nc):
        self.nc = nc
        self.free = []
        self.inuse = []

    def get(self, name, swdge=False):
        if swdge:
            return VSem(self.nc.alloc_semaphore(name))
        if self.free:
            v = self.free.pop()
        else:
            v = VSem(self.nc.alloc_semaphore(name))
        self.inuse.append(v)
        return v

    def retire_phase(self):
        for v in self.inuse:
            v.base += v.emitted
            v.emitted = 0
        self.free.extend(self.inuse)
        self.inuse = []


def _vs(nc, name, swdge=False):
    if not hasattr(nc, "_vsem_pool"):
        nc._vsem_pool = VSemPool(nc)
    return nc._vsem_pool.get(name, swdge)


def _ti(S, n):
    S.emitted += n
    return (S.h, n)


def _wge(e, S, v):
    e.wait_ge(S.h, S.base + v)


def _gbar_setup(nc):
    if not hasattr(nc, "_gbar"):
        nc._gbar = [nc.alloc_semaphore(f"GBAR{i}") for i in range(5)]
        nc._gbar_n = 0
        nc._gbar_dum = nc.alloc_sbuf_tensor("gbar_dum", [1, 8], F32)
        nc._gbar_dmb = nc.alloc_sbuf_tensor("gbar_dmb", [1, 8], BF16)


def emit_barrier(nc, blk):
    if not hasattr(nc, "_gbar"):
        nc._gbar = nc.alloc_semaphore("GBAR")
        nc._gbar_n = 0
        nc._gbar_dum = nc.alloc_sbuf_tensor("gbar_dum", [1, 8], F32)
        nc._gbar_dmb = nc.alloc_sbuf_tensor("gbar_dmb", [1, 8], BF16)
    _gbar_setup(nc)
    first = nc._gbar_n == 0
    nc._gbar_n += 1
    n = nc._gbar_n
    GV, GA, GP, GT, GS_ = nc._gbar
    DUM = nc._gbar_dum

    def wait_all(e):
        for g_ in nc._gbar:
            e.wait_ge(g_, n)

    @blk.vector
    def _(e):
        if first:
            e.memset(nc._gbar_dmb[:], 1.0)
        e.memset(DUM[:, 0:2], 0.0).then_inc(GV, 1)
        wait_all(e)

    @blk.scalar
    def _(e):
        e.memzero(DUM[:, 2:4]).then_inc(GA, 1)
        wait_all(e)

    @blk.gpsimd
    def _(e):
        e.memset(DUM[:, 4:6], 0.0).then_inc(GP, 1)
        wait_all(e)

    with nc.psum_tensor(f"gbar_dps{n}", [1, 8], F32) as dps:
        @blk.tensor
        def _(e):
            e.wait_ge(GV, n)
            e.wait_ge(GA, n)
            e.wait_ge(GP, n)
            e.matmul(dps[0:1, 0:1], nc._gbar_dmb[0:1, 0:1],
                     nc._gbar_dmb[0:1, 0:1], start=True,
                     stop=True).then_inc(GT, 1)
            wait_all(e)

    @blk.sync
    def _(e):
        e.sem_inc(GS_, 1)
        wait_all(e)



def _ident(nc, blk, sem, sc, tag=""):
    iot = sc.sb(f"{tag}identiota", [128, 128], I32)
    idt = sc.sb(f"{tag}identity", [128, 128], BF16)

    @blk.gpsimd
    def _(g):
        g.iota(iot[:], pattern=[[1, 128]], base=0,
               channel_multiplier=-1).then_inc(*_ti(sem, 1))
        _wge(g, sem, 1)
        g.tensor_scalar(out=idt[:], in0=iot[:], scalar1=0, scalar2=None,
                        op0=ALU.is_equal).then_inc(*_ti(sem, 1))

    return idt


def _load_bias_pp(nc, blk, sem, hbm_vec, n_chunks, name):
    t = nc.alloc_sbuf_tensor(name, [128, n_chunks], F32)

    @blk.sync
    def _(e):
        for c in range(n_chunks):
            src = hbm_vec[c * 128:(c + 1) * 128]
            e.dma_start(out=t[:, c:c + 1], in_=src[:, None]).then_inc(sem, 16)

    return t


def emit_ln(nc, blk, x_hbm, xnT_hbm, pi_order, tag, ntok=TC):
    sc = _Scope(nc)
    LS = [_vs(nc, f"{tag}_LS{i}") for i in range(2)]
    VA = _vs(nc, f"{tag}_VA")
    SA = _vs(nc, f"{tag}_SA")
    TS = _vs(nc, f"{tag}_TS")
    ES = _vs(nc, f"{tag}_ES")
    SS = [_vs(nc, f"{tag}_SS{i}", swdge=True) for i in range(2)]
    IDS = _vs(nc, f"{tag}_IDS")
    VC = _vs(nc, f"{tag}_VC")
    AC = _vs(nc, f"{tag}_AC")

    idt = _ident(nc, blk, IDS, sc, tag)

    XT = [sc.sb(f"{tag}_xt{i}", [128, D], F32) for i in range(2)]
    SQ = sc.sb(f"{tag}_sq", [128, D], F32)
    XN = [sc.sb(f"{tag}_xn{i}", [128, D], BF16) for i in range(2)]
    SUM = sc.sb(f"{tag}_sum", [128, 1], F32)
    SSUM = sc.sb(f"{tag}_ssum", [128, 1], F32)
    M = sc.sb(f"{tag}_m", [128, 1], F32)
    MS = sc.sb(f"{tag}_ms", [128, 1], F32)
    VAR = sc.sb(f"{tag}_var", [128, 1], F32)
    VAR2 = sc.sb(f"{tag}_var2", [128, 1], F32)
    LV = sc.sb(f"{tag}_lv", [128, 1], F32)
    RSTD = sc.sb(f"{tag}_rstd", [128, 1], F32)
    EPST = sc.sb(f"{tag}_epst", [128, 1], F32)
    PT = [sc.ps(f"{tag}_pt{i}", [128, 1024], BF16) for i in range(4)]
    STG = [sc.sb(f"{tag}_stg{i}", [128, 1024], BF16) for i in range(2)]

    NT = ntok // 128
    NG = 4

    @blk.sync
    def _(e):
        for tt in range(NT):
            if tt >= 2:
                _wge(e, VA, 3 * (tt - 2) + 3)
            e.dma_start(out=XT[tt % 2][:],
                        in_=x_hbm[tt * 128:(tt + 1) * 128, :]).then_inc(*_ti(LS[tt % 2], 16))

    @blk.gpsimd
    def _(e):
        total = NT * NG * 16
        for tt in range(NT):
            for g in range(NG):
                k = tt * NG + g
                _wge(e, ES, k + 1)
                dst = xnT_hbm[g * 8:(g + 1) * 8, :,
                              tt * 128:(tt + 1) * 128].transpose([1, 0, 2])
                e.dma_start(
                    out=dst,
                    in_=STG[k % 2][:].rearrange("p (c t) -> p c t", c=8),
                ).then_inc(*_ti(SS[k % 2], 16))
        _wge(e, SS[0], total // 2)
        _wge(e, SS[1], total // 2)

    @blk.scalar
    def _(e):
        for tt in range(NT):
            _wge(e, LS[tt % 2], 16 * (tt // 2 + 1))
            e.activation(out=SQ[:], in_=XT[tt % 2][:], func=AF.Square,
                         accum_out=SSUM[:]).then_inc(*_ti(SA, 1))
            _wge(e, VA, 3 * tt + 2)
            e.activation(out=LV[:], in_=VAR2[:], func=AF.Ln,
                         bias=EPST[:]).then_inc(*_ti(AC, 1))
            _wge(e, AC, tt + 1)
            e.activation(out=RSTD[:], in_=LV[:], func=AF.Exp,
                         scale=-0.5).then_inc(*_ti(SA, 1))
            if tt > 0:
                for g in range(NG):
                    k = (tt - 1) * NG + g
                    _wge(e, TS, k + 1)
                    if k >= 2:
                        _wge(e, SS[k % 2], 16 * (k // 2))
                    e.copy(out=STG[k % 2][:], in_=PT[k % 4][:]).then_inc(*_ti(ES, 1))
        for g in range(NG):
            k = (NT - 1) * NG + g
            _wge(e, TS, k + 1)
            _wge(e, SS[k % 2], 16 * (k // 2))
            e.copy(out=STG[k % 2][:], in_=PT[k % 4][:]).then_inc(*_ti(ES, 1))

    @blk.vector
    def _(e):
        e.memset(EPST[:], float(EPS))
        for tt in range(NT):
            _wge(e, LS[tt % 2], 16 * (tt // 2 + 1))
            e.reduce_sum(out=SUM[:], in_=XT[tt % 2][:], axis=AX.X).then_inc(*_ti(VA, 1))
            _wge(e, VA, 3 * tt + 1)
            _wge(e, SA, 2 * tt + 1)
            e.tensor_scalar(out=M[:], in0=SUM[:], scalar1=1.0 / D, scalar2=None,
                            op0=ALU.mult)
            e.tensor_scalar(out=MS[:], in0=SSUM[:], scalar1=1.0 / D, scalar2=None,
                            op0=ALU.mult).then_inc(*_ti(VC, 1))
            _wge(e, VC, 3 * tt + 1)
            e.tensor_tensor(out=VAR[:], in0=M[:], in1=M[:],
                            op=ALU.mult).then_inc(*_ti(VC, 1))
            _wge(e, VC, 3 * tt + 2)
            e.tensor_sub(VAR2[:], MS[:], VAR[:]).then_inc(*_ti(VA, 1))
            e.sem_inc(*_ti(VC, 1))
            _wge(e, SA, 2 * tt + 2)
            if tt >= 2:
                _wge(e, TS, NG * (tt - 1))
            e.tensor_scalar(out=XN[tt % 2][:], in0=XT[tt % 2][:], scalar1=M[:],
                            scalar2=RSTD[:], op0=ALU.subtract,
                            op1=ALU.mult).then_inc(*_ti(VA, 1))

    @blk.tensor
    def _(e):
        _wge(e, IDS, 2)
        for tt in range(NT):
            _wge(e, VA, 3 * tt + 3)
            for g in range(NG):
                k = tt * NG + g
                if k >= 4:
                    _wge(e, ES, k - 3)
                for cc in range(8):
                    c = g * 8 + cc
                    if pi_order:
                        Tb, j = c // 8, c % 8
                        src = XN[tt % 2][:].rearrange("p (T m j) -> p T m j",
                                                      T=4, m=128, j=8)[:, Tb, :, j]
                    else:
                        src = XN[tt % 2][:, c * 128:(c + 1) * 128]
                    ins = e.transpose(PT[k % 4][:, cc * 128:(cc + 1) * 128],
                                      src, idt[:])
                ins.then_inc(*_ti(TS, 1))
    sc.close()



def build_ln(pi_order, tag="ln"):
    nc = bass.Bass("TRN2", target_bir_lowering=False, debug=False)
    x = nc.dram_tensor("x", [TC, D], F32, kind="ExternalInput").ap()
    xnT = nc.dram_tensor("xnT", [NCH, 128, TC], BF16, kind="ExternalOutput").ap()
    with nc.Block() as blk:
        emit_ln(nc, blk, x, xnT, pi_order, tag)
    return nc


def emit_qkv0(nc, blk, xnT_hbm, w_hbm, b_hbm, q_hbm, k_hbm, vtm_hbm, tag,
              q_only=False):
    sc = _Scope(nc)
    XL = _vs(nc, f"{tag}_XL")
    BS = _vs(nc, f"{tag}_BS")
    IDS = _vs(nc, f"{tag}_IDS")
    LW = [_vs(nc, f"{tag}_LW{i}") for i in range(2)]
    CV = _vs(nc, f"{tag}_CV")
    MM = _vs(nc, f"{tag}_MM")
    MME = _vs(nc, f"{tag}_MME")
    EVB = _vs(nc, f"{tag}_EVB")
    SQK = [_vs(nc, f"{tag}_SQK{i}", swdge=True) for i in range(2)]
    VEV = _vs(nc, f"{tag}_VEV")
    TRV = _vs(nc, f"{tag}_TRV")
    EV2 = _vs(nc, f"{tag}_EV2")
    VS = [_vs(nc, f"{tag}_VS{i}", swdge=True) for i in range(2)]

    idt = _ident(nc, blk, IDS, sc, tag)

    XNT = sc.sb(f"{tag}_xnt", [128, NCH, TC], BF16)
    BIAS = sc.sb(f"{tag}_bias", [128, 96], F32)
    WF = [sc.sb(f"{tag}_wf{i}", [128, 512], F32) for i in range(2)]
    WB = [sc.sb(f"{tag}_wb{i}", [128, 512], BF16) for i in range(2)]
    STQ = [sc.sb(f"{tag}_stq{i}", [128, 512], BF16) for i in range(2)]
    VSTG = [sc.sb(f"{tag}_vstg{i}", [128, 512], BF16) for i in range(2)]
    VST2 = [sc.sb(f"{tag}_vst2{i}", [128, 4, 128], BF16)
            for i in range(2)]
    PS = [sc.ps(f"{tag}_ps{i}", [128, 512], F32) for i in range(8)]
    PTV = [PS[6][:].bitcast(BF16), PS[7][:].bitcast(BF16)]

    if q_only:
        groups = [list(range(g * 4, g * 4 + 4)) for g in range(8)]
    else:
        groups = [list(range(g * 4, g * 4 + 4)) for g in range(16)]
        vstart = [64 + 3 * g for g in range(10)] + [94]
        groups += [list(range(s, min(s + 3, 96))) for s in vstart]

    def is_v(goc):
        return goc >= 64

    nbank = [len(g) * 2 for g in groups]
    kctr = 0

    @blk.sync
    def _(e):
        e.dma_start(out=XNT[:], in_=xnT_hbm.transpose([1, 0, 2])).then_inc(*_ti(XL, 16))
        for c in range(96):
            e.dma_start(out=BIAS[:, c:c + 1],
                        in_=b_hbm[c * 128:(c + 1) * 128][:, None]).then_inc(*_ti(BS, 16))
        k = 0
        for gi, g in enumerate(groups):
            c0, c1 = g[0] * 128, (g[-1] + 1) * 128
            for ic in range(NCH):
                if k >= 2:
                    _wge(e, CV, k - 1)
                e.dma_start(out=WF[k % 2][:, 0:(c1 - c0)],
                            in_=w_hbm[ic * 128:(ic + 1) * 128, c0:c1]
                            ).then_inc(*_ti(LW[k % 2], 16))
                k += 1

    @blk.scalar
    def _(e):
        k = 0
        ev2 = 0
        for gi, g in enumerate(groups):
            w = len(g) * 128
            for ic in range(NCH):
                _wge(e, LW[k % 2], 16 * (k // 2 + 1))
                if k >= 2:
                    _wge(e, MM, k - 1)
                e.copy(out=WB[k % 2][:, 0:w], in_=WF[k % 2][:, 0:w]).then_inc(*_ti(CV, 1))
                k += 1
            if is_v(g[0]):
                for idx in range(len(g) * 2):
                    _wge(e, TRV, ev2 + 1)
                    vi = ev2
                    _wge(e, VS[vi % 2], 16 * (vi // 2))
                    e.copy(out=VST2[vi % 2][:],
                           in_=PTV[vi % 2][:, 0:512]).then_inc(*_ti(EV2, 1))
                    ev2 += 1

    @blk.tensor
    def _(e):
        _wge(e, XL, 16)
        _wge(e, IDS, 2)
        k = 0
        qk_done = 0
        v_done = 0
        trv = 0
        for gi, g in enumerate(groups):
            _wge(e, EVB, qk_done)
            _wge(e, VEV, v_done)
            for ic in range(NCH):
                _wge(e, CV, k + 1)
                for j, goc in enumerate(g):
                    for tt in range(2):
                        ins = e.matmul(
                            PS[j * 2 + tt][:],
                            WB[k % 2][:, j * 128:(j + 1) * 128],
                            XNT[:, ic, tt * 512:(tt + 1) * 512],
                            start=(ic == 0), stop=(ic == NCH - 1))
                ins.then_inc(*_ti(MM, 1))
                k += 1
            for goc in g:
                if is_v(goc):
                    v_done += 2
                else:
                    qk_done += 2
            if is_v(g[0]):
                for idx in range(len(g) * 2):
                    vi = trv
                    _wge(e, VEV, vi + 1)
                    if vi >= 2:
                        _wge(e, EV2, vi - 1)
                    for j4 in range(4):
                        ins = e.transpose(
                            PTV[vi % 2][:, j4 * 128:(j4 + 1) * 128],
                            VSTG[vi % 2][:, j4 * 128:(j4 + 1) * 128], idt[:])
                    ins.then_inc(*_ti(TRV, 1))
                    trv += 1

    @blk.vector
    def _(e):
        _wge(e, BS, 16 * 96)
        sq = 0
        vev = 0
        for gi, g in enumerate(groups):
            _wge(e, MM, NCH * (gi + 1))
            for j, goc in enumerate(g):
                for tt in range(2):
                    if not is_v(goc):
                        _wge(e, SQK[sq % 2], 16 * (sq // 2))
                        if goc < 32:
                            e.tensor_scalar(
                                out=STQ[sq % 2][:], in0=PS[j * 2 + tt][:],
                                scalar1=BIAS[:, goc:goc + 1], scalar2=float(ISQ),
                                op0=ALU.add, op1=ALU.mult).then_inc(*_ti(EVB, 1))
                        else:
                            e.tensor_scalar(
                                out=STQ[sq % 2][:], in0=PS[j * 2 + tt][:],
                                scalar1=BIAS[:, goc:goc + 1], scalar2=None,
                                op0=ALU.add).then_inc(*_ti(EVB, 1))
                        sq += 1
                    else:
                        _wge(e, TRV, max(0, vev - 1))
                        e.tensor_scalar(
                            out=VSTG[vev % 2][:], in0=PS[j * 2 + tt][:],
                            scalar1=BIAS[:, goc:goc + 1], scalar2=None,
                            op0=ALU.add).then_inc(*_ti(VEV, 1))
                        vev += 1

    @blk.gpsimd
    def _(e):
        sq = 0
        ev2 = 0
        for gi, g in enumerate(groups):
            for j, goc in enumerate(g):
                for tt in range(2):
                    if not is_v(goc):
                        _wge(e, EVB, sq + 1)
                        dst = (q_hbm if goc < 32 else k_hbm)
                        c = goc if goc < 32 else goc - 32
                        e.dma_start(
                            out=dst[c, :, tt * 512:(tt + 1) * 512],
                            in_=STQ[sq % 2][:]).then_inc(*_ti(SQK[sq % 2], 16))
                        sq += 1
                    else:
                        _wge(e, EV2, ev2 + 1)
                        c = goc - 64
                        dst = vtm_hbm[tt * 512:(tt + 1) * 512,
                                      c * 128:(c + 1) * 128]
                        e.dma_start(
                            out=dst.rearrange("(j p) d -> p j d", j=4),
                            in_=VST2[ev2 % 2][:]).then_inc(*_ti(VS[ev2 % 2], 16))
                        ev2 += 1
        _wge(e, SQK[0], 16 * ((sq + 1) // 2))
        _wge(e, SQK[1], 16 * (sq // 2))
        _wge(e, VS[0], 16 * ((ev2 + 1) // 2))
        _wge(e, VS[1], 16 * (ev2 // 2))
    sc.close()



def build_qkv0(tag="qkv0"):
    nc = bass.Bass("TRN2", target_bir_lowering=False, debug=False)
    xnT = nc.dram_tensor("xnT", [NCH, 128, TC], BF16, kind="ExternalInput").ap()
    w = nc.dram_tensor("w", [D, 3 * D], F32, kind="ExternalInput").ap()
    b = nc.dram_tensor("b", [3 * D], F32, kind="ExternalInput").ap()
    q = nc.dram_tensor("q", [NCH, 128, TC], BF16, kind="ExternalOutput").ap()
    k = nc.dram_tensor("k", [NCH, 128, TC], BF16, kind="ExternalOutput").ap()
    vtm = nc.dram_tensor("vtm", [TC, D], BF16, kind="ExternalOutput").ap()
    with nc.Block() as blk:
        emit_qkv0(nc, blk, xnT, w, b, q, k, vtm, tag)
    return nc


def emit_ffn0(nc, blk, xnT_hbm, gw_hbm, ics_hbm, ocs_hbm, w_hbm, bias_hbm,
              g_hbm, tag):
    sc = _Scope(nc)
    XL = _vs(nc, f"{tag}_XL")
    GL = _vs(nc, f"{tag}_GL")
    BS = _vs(nc, f"{tag}_BS")
    IDS = _vs(nc, f"{tag}_IDS")
    CVS = _vs(nc, f"{tag}_CVS")
    RTR = _vs(nc, f"{tag}_RTR")
    VRT = _vs(nc, f"{tag}_VRT")
    ART = _vs(nc, f"{tag}_ART")
    USM = _vs(nc, f"{tag}_USM")
    UEV = _vs(nc, f"{tag}_UEV")
    LW2 = [_vs(nc, f"{tag}_LW2{i}") for i in range(2)]
    SGN = _vs(nc, f"{tag}_SGN")
    TRW = _vs(nc, f"{tag}_TRW")
    PWEV = _vs(nc, f"{tag}_PWEV")
    MM2 = _vs(nc, f"{tag}_MM2")
    POEV = _vs(nc, f"{tag}_POEV")
    YEV = _vs(nc, f"{tag}_YEV")
    SIL = _vs(nc, f"{tag}_SIL")
    ST2 = [_vs(nc, f"{tag}_ST2{i}", swdge=True) for i in range(2)]

    idt = _ident(nc, blk, IDS, sc, tag)
    iotf = sc.sb(f"{tag}_iotf", [128, 128], I32)
    idf = sc.sb(f"{tag}_idf", [128, 128], F32)

    @blk.gpsimd
    def _(g):
        g.iota(iotf[:], pattern=[[1, 128]], base=0,
               channel_multiplier=-1).then_inc(*_ti(IDS, 1))
        _wge(g, IDS, 3)
        g.tensor_scalar(out=idf[:], in0=iotf[:], scalar1=0, scalar2=None,
                        op0=ALU.is_equal).then_inc(*_ti(IDS, 1))

    OCSB = sc.sb(f"{tag}_ocsb", [8, D], BF16)
    GWT = sc.sb(f"{tag}_gwt", [128, NCH, 8], BF16)
    BIAS = sc.sb(f"{tag}_bias", [128, NCH], F32)
    RLS = sc.sb(f"{tag}_rls", [8, TC], F32)
    NMX = sc.sb(f"{tag}_nmx", [128, 1], F32)
    ZT = sc.sb(f"{tag}_zt", [128, 1], F32)
    RZ = sc.sb(f"{tag}_rz", [128, 1], F32)
    EXPT = sc.sb(f"{tag}_expt", [128, 8], F32)
    RTM = sc.sb(f"{tag}_rtm", [128, 8], BF16)
    RTT = sc.sb(f"{tag}_rtt", [8, TC], BF16)
    U = sc.sb(f"{tag}_u", [128, NCH, TC], BF16)

    NT = TC // 128

    with (
        nc.psum_tensor(f"{tag}_rl", [8, TC], F32) as RL,
        nc.psum_tensor(f"{tag}_pr", [128, NCH * 8], BF16) as PR,
        nc.psum_tensor(f"{tag}_psm", [128, 8], F32) as PSM,
        nc.psum_tensor(f"{tag}_prb", [8, TC], BF16) as PRB,
        nc.psum_tensor(f"{tag}_pu", [128, 512], F32) as PU,
        nc.sbuf_tensor(f"{tag}_xnt", [128, NCH, TC], BF16) as XNT_h,
        nc.sbuf_tensor(f"{tag}_scf", [8, D], F32) as SCF_h,
        nc.sbuf_tensor(f"{tag}_gwb", [8, D], BF16) as GWB_h,
        nc.sbuf_tensor(f"{tag}_icsb", [8, D], BF16) as ICSB_h,
    ):
        XNT, SCF, GWB, ICSB = XNT_h, SCF_h, GWB_h, ICSB_h

        @blk.sync
        def _(e):
            e.dma_start(out=XNT[:], in_=xnT_hbm.transpose([1, 0, 2])).then_inc(*_ti(XL, 16))
            e.dma_start(out=SCF[:], in_=gw_hbm).then_inc(*_ti(GL, 16))
            _wge(e, CVS, 1)
            e.dma_start(out=SCF[:], in_=ics_hbm).then_inc(*_ti(GL, 16))
            _wge(e, CVS, 2)
            e.dma_start(out=SCF[:], in_=ocs_hbm).then_inc(*_ti(GL, 16))
            for c in range(NCH):
                e.dma_start(out=BIAS[:, c:c + 1],
                            in_=bias_hbm[c * 128:(c + 1) * 128][:, None]
                            ).then_inc(*_ti(BS, 16))

        @blk.scalar
        def _(e):
            _wge(e, GL, 16)
            e.copy(out=GWB[:], in_=SCF[:]).then_inc(*_ti(CVS, 1))
            _wge(e, GL, 32)
            e.copy(out=ICSB[:], in_=SCF[:]).then_inc(*_ti(CVS, 1))
            _wge(e, GL, 48)
            e.copy(out=OCSB[:], in_=SCF[:]).then_inc(*_ti(CVS, 1))
            for tb in range(NT):
                _wge(e, VRT, 3 * tb + 1)
                e.activation(out=EXPT[:], in_=PSM[:], func=AF.Exp,
                             bias=NMX[:], accum_out=ZT[:]).then_inc(*_ti(ART, 1))

        @blk.tensor
        def _(e):
            _wge(e, IDS, 4)
            _wge(e, CVS, 1)
            for c in range(NCH):
                ins = e.transpose(PR[:, c * 8:(c + 1) * 8],
                                  GWB[:, c * 128:(c + 1) * 128], idt[0:8, 0:8])
            ins.then_inc(*_ti(RTR, 1))
            _wge(e, RTR, 2)
            _wge(e, XL, 16)
            for c in range(NCH):
                for hh in range(2):
                    ins = e.matmul(RL[:, hh * 512:(hh + 1) * 512],
                                   GWT[:, c, :],
                                   XNT[:, c, hh * 512:(hh + 1) * 512],
                                   start=(c == 0), stop=(c == NCH - 1))
            ins.then_inc(*_ti(RTR, 1))
            for tb in range(NT):
                _wge(e, RTR, 4 + 2 * tb)
                if tb >= 1:
                    _wge(e, ART, tb)
                e.transpose(PSM[:], RLS[:, tb * 128:(tb + 1) * 128],
                            idf[0:8, 0:8]).then_inc(*_ti(RTR, 1))
                _wge(e, VRT, 3 * tb + 3)
                e.transpose(PRB[:, tb * 128:(tb + 1) * 128], RTM[:],
                            idt[:]).then_inc(*_ti(RTR, 1))
            _wge(e, VRT, 3 * NT + 1)
            for oc in range(NCH):
                for tt in range(2):
                    k = 2 * oc + tt
                    if k >= 1:
                        _wge(e, UEV, k)
                    e.matmul(PU[:], ICSB[:, oc * 128:(oc + 1) * 128],
                             RTT[:, tt * 512:(tt + 1) * 512],
                             start=True, stop=True).then_inc(*_ti(USM, 1))

        @blk.vector
        def _(e):
            _wge(e, RTR, 1)
            e.tensor_copy(out=GWT[:], in_=PR[:].rearrange(
                "p (c e) -> p c e", e=8)).then_inc(*_ti(RTR, 1))
            _wge(e, RTR, 3)
            e.tensor_copy(out=RLS[:], in_=RL[:]).then_inc(*_ti(RTR, 1))
            for tb in range(NT):
                _wge(e, RTR, 5 + 2 * tb)
                if tb >= 1:
                    _wge(e, ART, tb)
                e.reduce_max(out=NMX[:], in_=PSM[:], axis=AX.X,
                             negate=True).then_inc(*_ti(VRT, 1))
                _wge(e, ART, tb + 1)
                e.reciprocal(out=RZ[:], in_=ZT[:]).then_inc(*_ti(VRT, 1))
                _wge(e, VRT, 3 * tb + 2)
                if tb >= 1:
                    _wge(e, RTR, 4 + 2 * tb)
                e.tensor_scalar(out=RTM[:], in0=EXPT[:], scalar1=RZ[:],
                                scalar2=None, op0=ALU.mult).then_inc(*_ti(VRT, 1))
            _wge(e, RTR, 4 + 2 * NT)
            e.tensor_copy(out=RTT[:], in_=PRB[:]).then_inc(*_ti(VRT, 1))
            for oc in range(NCH):
                for tt in range(2):
                    k = 2 * oc + tt
                    _wge(e, USM, k + 1)
                    e.tensor_mul(U[:, oc, tt * 512:(tt + 1) * 512],
                                 XNT[:, oc, tt * 512:(tt + 1) * 512],
                                 PU[:]).then_inc(*_ti(UEV, 1))

    WF2 = [sc.sb(f"{tag}_wf2{i}", [128, D], F32) for i in range(2)]
    SW = sc.sb(f"{tag}_sw", [128, D], BF16)
    TWB = [sc.sb(f"{tag}_twb{i}", [128, NCH, 128], BF16)
           for i in range(2)]
    OSS = [sc.sb(f"{tag}_oss{i}", [128, 512], F32) for i in range(2)]
    YT = [sc.sb(f"{tag}_yt{i}", [128, 512], F32) for i in range(2)]
    GST = [sc.sb(f"{tag}_gst{i}", [128, 512], BF16) for i in range(2)]

    with (
        nc.psum_tensor(f"{tag}_pw0", [128, 1024], BF16) as PW0,
        nc.psum_tensor(f"{tag}_pw1", [128, 1024], BF16) as PW1,
        nc.psum_tensor(f"{tag}_py0", [128, 512], F32) as PY0,
        nc.psum_tensor(f"{tag}_py1", [128, 512], F32) as PY1,
        nc.psum_tensor(f"{tag}_py2", [128, 512], F32) as PY2,
        nc.psum_tensor(f"{tag}_py3", [128, 512], F32) as PY3,
        nc.psum_tensor(f"{tag}_po0", [128, 512], F32) as PO0,
        nc.psum_tensor(f"{tag}_po1", [128, 512], F32) as PO1,
    ):
        PW = [PW0, PW1]
        PY = [PY0, PY1, PY2, PY3]
        PO = [PO0, PO1]

        @blk.sync
        def _(e):
            _wge(e, UEV, 64)
            for ob in range(NCH):
                if ob >= 2:
                    _wge(e, SGN, ob - 1)
                for sp in range(4):
                    e.dma_start(out=WF2[ob % 2][:, sp * 1024:(sp + 1) * 1024],
                                in_=w_hbm[ob * 128:(ob + 1) * 128,
                                          sp * 1024:(sp + 1) * 1024]
                                ).then_inc(*_ti(LW2[ob % 2], 16))

        @blk.scalar
        def _(e):
            _wge(e, BS, 16 * NCH)
            for ob in range(NCH):
                _wge(e, LW2[ob % 2], 64 * (ob // 2 + 1))
                if ob >= 1:
                    _wge(e, TRW, 4 * ob)
                e.sign(out=SW[:], in_=WF2[ob % 2][:]).then_inc(*_ti(SGN, 1))
                for tt in range(2):
                    if ob >= 1:
                        kk = 2 * (ob - 1) + tt
                        _wge(e, YEV, kk + 1)
                        _wge(e, ST2[kk % 2], 16 * (kk // 2))
                        e.activation(out=GST[kk % 2][:], in_=YT[kk % 2][:],
                                     func=AF.Silu,
                                     bias=BIAS[:, (ob - 1):ob]).then_inc(*_ti(SIL, 1))
            for tt in range(2):
                kk = 2 * (NCH - 1) + tt
                _wge(e, YEV, kk + 1)
                _wge(e, ST2[kk % 2], 16 * (kk // 2))
                e.activation(out=GST[kk % 2][:], in_=YT[kk % 2][:],
                             func=AF.Silu,
                             bias=BIAS[:, NCH - 1:NCH]).then_inc(*_ti(SIL, 1))

        @blk.tensor
        def _(e):
            for ob in range(NCH):
                _wge(e, SGN, ob + 1)
                for gg in range(4):
                    kg = 4 * ob + gg
                    if kg >= 2:
                        _wge(e, PWEV, kg - 1)
                    for cc in range(8):
                        hc = gg * 8 + cc
                        ins = e.transpose(
                            PW[kg % 2][:, cc * 128:(cc + 1) * 128],
                            SW[:, hc * 128:(hc + 1) * 128], idt[:])
                    ins.then_inc(*_ti(TRW, 1))
                for tt in range(2):
                    kk = 2 * ob + tt
                    if kk >= 2:
                        _wge(e, POEV, kk - 1)
                    e.matmul(PO[kk % 2][:], OCSB[:, ob * 128:(ob + 1) * 128],
                             RTT[:, tt * 512:(tt + 1) * 512],
                             start=True, stop=True).then_inc(*_ti(MM2, 1))
                _wge(e, PWEV, 4 * (ob + 1))
                for tt in range(2):
                    kk = 2 * ob + tt
                    if kk >= 4:
                        _wge(e, YEV, kk - 3)
                    for hc in range(NCH):
                        ins = e.matmul(PY[kk % 4][:],
                                       TWB[ob % 2][:, hc, :],
                                       U[:, hc, tt * 512:(tt + 1) * 512],
                                       start=(hc == 0), stop=(hc == NCH - 1))
                    ins.then_inc(*_ti(MM2, 1))

        @blk.vector
        def _(e):
            for ob in range(NCH):
                for gg in range(4):
                    kg = 4 * ob + gg
                    _wge(e, TRW, kg + 1)
                    if ob >= 2:
                        _wge(e, MM2, 4 * (ob - 1))
                    e.tensor_copy(
                        out=TWB[ob % 2][:, gg * 8:(gg + 1) * 8, :],
                        in_=PW[kg % 2][:].rearrange("p (c w) -> p c w", c=8)
                    ).then_inc(*_ti(PWEV, 1))
                for tt in range(2):
                    kk = 2 * ob + tt
                    _wge(e, MM2, 4 * ob + tt + 1)
                    e.tensor_copy(out=OSS[kk % 2][:],
                                  in_=PO[kk % 2][:]).then_inc(*_ti(POEV, 1))
                for tt in range(2):
                    kk = 2 * ob + tt
                    _wge(e, MM2, 4 * ob + 2 + tt + 1)
                    _wge(e, POEV, kk + 1)
                    if kk >= 2:
                        _wge(e, SIL, kk - 1)
                    e.tensor_mul(YT[kk % 2][:], OSS[kk % 2][:],
                                 PY[kk % 4][:]).then_inc(*_ti(YEV, 1))

        @blk.gpsimd
        def _(e):
            for ob in range(NCH):
                for tt in range(2):
                    kk = 2 * ob + tt
                    _wge(e, SIL, kk + 1)
                    e.dma_start(out=g_hbm[ob, :, tt * 512:(tt + 1) * 512],
                                in_=GST[kk % 2][:]).then_inc(*_ti(ST2[kk % 2], 16))
            _wge(e, ST2[0], 16 * NCH)
            _wge(e, ST2[1], 16 * NCH)
    sc.close()



def build_ffn0(tag="ffn0"):
    nc = bass.Bass("TRN2", target_bir_lowering=False, debug=False)
    xnT = nc.dram_tensor("xnT", [NCH, 128, TC], BF16, kind="ExternalInput").ap()
    gw = nc.dram_tensor("gw", [8, D], F32, kind="ExternalInput").ap()
    ics = nc.dram_tensor("ics", [8, D], F32, kind="ExternalInput").ap()
    ocs = nc.dram_tensor("ocs", [8, D], F32, kind="ExternalInput").ap()
    w = nc.dram_tensor("w", [D, D], F32, kind="ExternalInput").ap()
    bias = nc.dram_tensor("bias", [D], F32, kind="ExternalInput").ap()
    g = nc.dram_tensor("g", [NCH, 128, TC], BF16, kind="ExternalOutput").ap()
    with nc.Block() as blk:
        emit_ffn0(nc, blk, xnT, gw, ics, ocs, w, bias, g, tag)
    return nc


def emit_attn0(nc, blk, q_hbm, kf_hbm, vf_hbm, ctx_hbm, tag):
    sc = _Scope(nc)
    QL = _vs(nc, f"{tag}_QL")
    IDS = _vs(nc, f"{tag}_IDS")
    KL = [_vs(nc, f"{tag}_KL{i}") for i in range(4)]
    VL = [_vs(nc, f"{tag}_VL{i}") for i in range(4)]
    SCM = _vs(nc, f"{tag}_SCM")
    VN = _vs(nc, f"{tag}_VN")
    AE = _vs(nc, f"{tag}_AE")
    TRP = _vs(nc, f"{tag}_TRP")
    PEV = _vs(nc, f"{tag}_PEV")
    PVM = _vs(nc, f"{tag}_PVM")
    RC = _vs(nc, f"{tag}_RC")
    CEV = _vs(nc, f"{tag}_CEV")
    TCX = _vs(nc, f"{tag}_TCX")
    CTE = _vs(nc, f"{tag}_CTE")
    CST = [_vs(nc, f"{tag}_CST{i}", swdge=True) for i in range(2)]

    idt = _ident(nc, blk, IDS, sc, tag)

    QR = sc.sb(f"{tag}_qr", [128, NCH, TC], BF16)
    KT = [sc.sb(f"{tag}_kt{i}", [128, S], BF16) for i in range(4)]
    VT = [sc.sb(f"{tag}_vt{i}", [128, 16, 128], BF16)
          for i in range(4)]
    PB = [sc.sb(f"{tag}_pb{i}", [128, S], BF16) for i in range(2)]
    PTS = sc.sb(f"{tag}_pts", [128, 16, 128], BF16)
    NMX = sc.sb(f"{tag}_nmx", [128, 1], F32)
    Z = [sc.sb(f"{tag}_z{i}", [128, 1], F32) for i in range(2)]
    RZ = [sc.sb(f"{tag}_rz{i}", [128, 1], F32) for i in range(2)]
    CXS = [sc.sb(f"{tag}_cxs{i}", [128, 128], BF16) for i in range(2)]
    CTS = [sc.sb(f"{tag}_cts{i}", [128, 128], BF16) for i in range(2)]

    SC = sc.ps(f"{tag}_sc", [128, S], F32)
    PSB = [sc.ps(f"{tag}_psb{i}", [128, 1024], BF16)
           for i in range(2)]
    PCX = sc.ps(f"{tag}_pcx", [128, 128], F32)
    PCT = sc.ps(f"{tag}_pct", [128, 128], BF16)

    BH = [(b, h) for b in range(B) for h in range(32)]

    @blk.sync
    def _(e):
        e.dma_start(out=QR[:], in_=q_hbm.transpose([1, 0, 2])).then_inc(*_ti(QL, 16))
        for i, (b, h) in enumerate(BH):
            if i >= 4:
                _wge(e, SCM, 2 * (i - 3))
            for hf in range(2):
                e.dma_start(out=KT[i % 4][:, hf * 1024:(hf + 1) * 1024],
                            in_=kf_hbm[b, h][:, hf * 1024:(hf + 1) * 1024]
                            ).then_inc(*_ti(KL[i % 4], 16))
            if i >= 4:
                _wge(e, PVM, 2 * (i - 3))
            for hf in range(2):
                e.dma_start(out=VT[i % 4][:, hf * 8:(hf + 1) * 8, :],
                            in_=vf_hbm[b, h][hf * 1024:(hf + 1) * 1024, :]
                            .rearrange("(kt p) d -> p kt d", p=128)
                            ).then_inc(*_ti(VL[i % 4], 16))

    @blk.tensor
    def _(e):
        _wge(e, QL, 16)
        _wge(e, IDS, 2)
        for i, (b, h) in enumerate(BH):
            _wge(e, KL[i % 4], 32 * (i // 4 + 1))
            _wge(e, VL[i % 4], 32 * (i // 4 + 1))
            for qt in range(2):
                u = 2 * i + qt
                if u >= 1:
                    _wge(e, AE, u)
                qsl = QR[:, h, b * PC + qt * 128: b * PC + (qt + 1) * 128]
                for j in range(4):
                    ins = e.matmul(SC[:, j * 512:(j + 1) * 512], qsl,
                                   KT[i % 4][:, j * 512:(j + 1) * 512],
                                   start=True, stop=True)
                ins.then_inc(*_ti(SCM, 1))
                _wge(e, AE, u + 1)
                _wge(e, PEV, 2 * u)
                for g in range(2):
                    for cc in range(8):
                        kt_i = g * 8 + cc
                        ins = e.transpose(
                            PSB[g][:, cc * 128:(cc + 1) * 128],
                            PB[u % 2][:, kt_i * 128:(kt_i + 1) * 128], idt[:])
                    ins.then_inc(*_ti(TRP, 1))
                _wge(e, PEV, 2 * u + 2)
                if u >= 1:
                    _wge(e, CEV, u)
                for kt_i in range(16):
                    ins = e.matmul(PCX[:], PTS[:, kt_i, :], VT[i % 4][:, kt_i, :],
                                   start=(kt_i == 0), stop=(kt_i == 15))
                ins.then_inc(*_ti(PVM, 1))
                _wge(e, CEV, u + 1)
                if u >= 1:
                    _wge(e, CTE, u)
                e.transpose(PCT[:], CXS[u % 2][:], idt[:]).then_inc(*_ti(TCX, 1))

    @blk.vector
    def _(e):
        for i, (b, h) in enumerate(BH):
            for qt in range(2):
                u = 2 * i + qt
                _wge(e, SCM, u + 1)
                e.reduce_max(out=NMX[:], in_=SC[:], axis=AX.X,
                             negate=True).then_inc(*_ti(VN, 1))
                _wge(e, AE, u + 1)
                e.reciprocal(out=RZ[u % 2][:], in_=Z[u % 2][:]).then_inc(*_ti(RC, 1))
                _wge(e, TRP, 2 * u + 1)
                if u >= 1:
                    _wge(e, PVM, u)
                e.tensor_copy(out=PTS[:, 0:8, :],
                              in_=PSB[0][:].rearrange("p (c w) -> p c w", c=8)
                              ).then_inc(*_ti(PEV, 1))
                _wge(e, RC, u + 1)
                _wge(e, PVM, u + 1)
                if u >= 2:
                    _wge(e, TCX, u - 1)
                e.tensor_scalar(out=CXS[u % 2][:], in0=PCX[:],
                                scalar1=RZ[u % 2][:], scalar2=None,
                                op0=ALU.mult).then_inc(*_ti(CEV, 1))

    @blk.scalar
    def _(e):
        for i, (b, h) in enumerate(BH):
            for qt in range(2):
                u = 2 * i + qt
                _wge(e, VN, u + 1)
                if u >= 2:
                    _wge(e, TRP, 2 * (u - 1))
                    _wge(e, RC, u - 1)
                e.activation(out=PB[u % 2][:], in_=SC[:], func=AF.Exp,
                             bias=NMX[:], accum_out=Z[u % 2][:]).then_inc(*_ti(AE, 1))
                _wge(e, TRP, 2 * u + 2)
                if u >= 1:
                    _wge(e, PVM, u)
                e.copy(out=PTS[:, 8:16, :],
                       in_=PSB[1][:].rearrange("p (c w) -> p c w", c=8)
                       ).then_inc(*_ti(PEV, 1))
                _wge(e, TCX, u + 1)
                if u >= 2:
                    _wge(e, CST[u % 2], 16 * (u // 2))
                e.copy(out=CTS[u % 2][:], in_=PCT[:]).then_inc(*_ti(CTE, 1))

    @blk.gpsimd
    def _(e):
        for i, (b, h) in enumerate(BH):
            for qt in range(2):
                u = 2 * i + qt
                _wge(e, CTE, u + 1)
                e.dma_start(out=ctx_hbm[h, :, b * PC + qt * 128:
                                        b * PC + (qt + 1) * 128],
                            in_=CTS[u % 2][:]).then_inc(*_ti(CST[u % 2], 16))
        _wge(e, CST[0], 16 * 128)
        _wge(e, CST[1], 16 * 128)
    sc.close()



def build_attn0(tag="attn0"):
    nc = bass.Bass("TRN2", target_bir_lowering=False, debug=False)
    q = nc.dram_tensor("q", [NCH, 128, TC], BF16, kind="ExternalInput").ap()
    kf = nc.dram_tensor("kf", [B, 32, 128, S], BF16, kind="ExternalInput").ap()
    vf = nc.dram_tensor("vf", [B, 32, S, 128], BF16, kind="ExternalInput").ap()
    ctx = nc.dram_tensor("ctx", [NCH, 128, TC], BF16, kind="ExternalOutput").ap()
    with nc.Block() as blk:
        emit_attn0(nc, blk, q, kf, vf, ctx, tag)
    return nc


def emit_addln(nc, blk, tag, SG, SGR, sg_ready, x_hbm, x1_hbm, xnT_hbm, pi_order,
               idt, IDS, x1_dtype=F32, sg_all=None):
    sc = _Scope(nc)
    LXc = [_vs(nc, f"{tag}_LX{i}") for i in range(2)]
    TS1 = _vs(nc, f"{tag}_TS1")
    AD = _vs(nc, f"{tag}_AD")
    XS = _vs(nc, f"{tag}_XS")
    VA = _vs(nc, f"{tag}_VA")
    SA = _vs(nc, f"{tag}_SA")
    TS = _vs(nc, f"{tag}_TS")
    ES = _vs(nc, f"{tag}_ES")
    SSX = [_vs(nc, f"{tag}_SSX{i}", swdge=True) for i in range(2)]
    SSN = [_vs(nc, f"{tag}_SSN{i}", swdge=True) for i in range(2)]
    VC = _vs(nc, f"{tag}_VC")
    AC = _vs(nc, f"{tag}_AC")

    XTILE = [sc.sb(f"{tag}_xtile{i}", [128, D], F32)
             for i in range(2)]
    X1T = sc.sb(f"{tag}_x1t", [128, D], F32)
    X1STG = [sc.sb(f"{tag}_x1stg{i}", [128, D], x1_dtype)
             for i in range(2)]
    PX = [sc.ps(f"{tag}_px{i}", [128, 512], BF16) for i in range(2)]
    ln = xnT_hbm is not None
    if ln:
        SQ = sc.sb(f"{tag}_sq", [128, D], F32)
        XN = [sc.sb(f"{tag}_xn{i}", [128, D], BF16)
              for i in range(2)]
        SUM = sc.sb(f"{tag}_sum", [128, 1], F32)
        SSUM = sc.sb(f"{tag}_ssum", [128, 1], F32)
        M = sc.sb(f"{tag}_m", [128, 1], F32)
        MS = sc.sb(f"{tag}_ms", [128, 1], F32)
        VAR = sc.sb(f"{tag}_var", [128, 1], F32)
        VAR2 = sc.sb(f"{tag}_var2", [128, 1], F32)
        LV = sc.sb(f"{tag}_lv", [128, 1], F32)
        RSTD = sc.sb(f"{tag}_rstd", [128, 1], F32)
        EPST = sc.sb(f"{tag}_epst", [128, 1], F32)
        PT = [sc.ps(f"{tag}_pt{i}", [128, 1024], BF16)
              for i in range(4)]
        STG = [sc.sb(f"{tag}_stg{i}", [128, 1024], BF16)
               for i in range(2)]

    NT = TC // 128
    NG = 4

    @blk.sync
    def _(e):
        for tt in range(NT):
            if tt >= 2:
                _wge(e, AD, 8 * (tt - 1))
            e.dma_start(out=XTILE[tt % 2][:],
                        in_=x_hbm[tt * 128:(tt + 1) * 128, :]
                        ).then_inc(*_ti(LXc[tt % 2], 16))

    @blk.tensor
    def _(e):
        _wge(e, IDS, 2)
        if sg_all is not None:
            _wge(e, SGR, sg_all)
        for tt in range(NT):
            for gg in range(8):
                kq = 8 * tt + gg
                _wge(e, SGR, sg_ready(gg * 4 + 3))
                if kq >= 2:
                    _wge(e, AD, kq - 1)
                for cc in range(4):
                    oc = gg * 4 + cc
                    ins = e.transpose(PX[kq % 2][:, cc * 128:(cc + 1) * 128],
                                      SG[:, oc, tt * 128:(tt + 1) * 128], idt[:])
                ins.then_inc(*_ti(TS1, 1))
            if ln:
                _wge(e, VA, 2 * tt + 2)
                for g in range(NG):
                    k = tt * NG + g
                    if k >= 4:
                        _wge(e, ES, k - 3)
                    for cc in range(8):
                        c = g * 8 + cc
                        if pi_order:
                            Tb, j = c // 8, c % 8
                            src = XN[tt % 2][:].rearrange(
                                "p (T m j) -> p T m j",
                                T=4, m=128, j=8)[:, Tb, :, j]
                        else:
                            src = XN[tt % 2][:, c * 128:(c + 1) * 128]
                        ins = e.transpose(PT[k % 4][:, cc * 128:(cc + 1) * 128],
                                          src, idt[:])
                    ins.then_inc(*_ti(TS, 1))

    @blk.vector
    def _(e):
        if ln:
            e.memset(EPST[:], float(EPS))
        for tt in range(NT):
            _wge(e, LXc[tt % 2], 16 * (tt // 2 + 1))
            if tt >= 1:
                _wge(e, XS, tt)
            for gg in range(8):
                kq = 8 * tt + gg
                _wge(e, TS1, kq + 1)
                e.tensor_add(X1T[:, gg * 512:(gg + 1) * 512],
                             XTILE[tt % 2][:, gg * 512:(gg + 1) * 512],
                             PX[kq % 2][:]).then_inc(*_ti(AD, 1))
            if ln:
                _wge(e, SA, 2 * tt + 1)
                e.reduce_sum(out=SUM[:], in_=X1T[:], axis=AX.X).then_inc(*_ti(VC, 1))
                _wge(e, VC, 4 * tt + 1)
                e.tensor_scalar(out=M[:], in0=SUM[:], scalar1=1.0 / D,
                                scalar2=None, op0=ALU.mult)
                e.tensor_scalar(out=MS[:], in0=SSUM[:], scalar1=1.0 / D,
                                scalar2=None, op0=ALU.mult).then_inc(*_ti(VC, 1))
                _wge(e, VC, 4 * tt + 2)
                e.tensor_tensor(out=VAR[:], in0=M[:], in1=M[:],
                                op=ALU.mult).then_inc(*_ti(VC, 1))
                _wge(e, VC, 4 * tt + 3)
                e.tensor_sub(VAR2[:], MS[:], VAR[:]).then_inc(*_ti(VA, 1))
                e.sem_inc(*_ti(VC, 1))
                _wge(e, SA, 2 * tt + 2)
                if tt >= 2:
                    _wge(e, TS, NG * (tt - 1))
                e.tensor_scalar(out=XN[tt % 2][:], in0=X1T[:], scalar1=M[:],
                                scalar2=RSTD[:], op0=ALU.subtract,
                                op1=ALU.mult).then_inc(*_ti(VA, 1))

    @blk.scalar
    def _(e):
        for tt in range(NT):
            _wge(e, AD, 8 * (tt + 1))
            if ln:
                e.activation(out=SQ[:], in_=X1T[:], func=AF.Square,
                             accum_out=SSUM[:]).then_inc(*_ti(SA, 1))
                _wge(e, VA, 2 * tt + 1)
                e.activation(out=LV[:], in_=VAR2[:], func=AF.Ln,
                             bias=EPST[:]).then_inc(*_ti(AC, 1))
                _wge(e, AC, tt + 1)
                e.activation(out=RSTD[:], in_=LV[:], func=AF.Exp,
                             scale=-0.5).then_inc(*_ti(SA, 1))
            if tt >= 2:
                _wge(e, SSX[tt % 2], 16 * (tt // 2))
            if ln:
                _wge(e, VA, 2 * tt + 2)
            e.copy(out=X1STG[tt % 2][:], in_=X1T[:]).then_inc(*_ti(XS, 1))
            if ln and tt > 0:
                for g in range(NG):
                    k = (tt - 1) * NG + g
                    _wge(e, TS, k + 1)
                    if k >= 2:
                        _wge(e, SSN[k % 2], 16 * (k // 2))
                    e.copy(out=STG[k % 2][:], in_=PT[k % 4][:]).then_inc(*_ti(ES, 1))
        if ln:
            for g in range(NG):
                k = (NT - 1) * NG + g
                _wge(e, TS, k + 1)
                _wge(e, SSN[k % 2], 16 * (k // 2))
                e.copy(out=STG[k % 2][:], in_=PT[k % 4][:]).then_inc(*_ti(ES, 1))

    @blk.gpsimd
    def _(e):
        for tt in range(NT):
            _wge(e, XS, tt + 1)
            e.dma_start(out=x1_hbm[tt * 128:(tt + 1) * 128, :],
                        in_=X1STG[tt % 2][:]).then_inc(*_ti(SSX[tt % 2], 16))
            if ln:
                for g in range(NG):
                    k = tt * NG + g
                    _wge(e, ES, k + 1)
                    dst = xnT_hbm[g * 8:(g + 1) * 8, :,
                                  tt * 128:(tt + 1) * 128].transpose([1, 0, 2])
                    e.dma_start(
                        out=dst,
                        in_=STG[k % 2][:].rearrange("p (c t) -> p c t", c=8),
                    ).then_inc(*_ti(SSN[k % 2], 16))
        _wge(e, SSX[0], 16 * (NT // 2))
        _wge(e, SSX[1], 16 * (NT // 2))
        if ln:
            _wge(e, SSN[0], 16 * (NT * NG // 2))
            _wge(e, SSN[1], 16 * (NT * NG // 2))


    sc.close()



def emit_combine0(nc, blk, ctx_hbm, ow_hbm, ob_hbm, g_hbm, x_hbm, x1_hbm,
                  xnT_hbm, tag):
    sc = _Scope(nc)
    IDS = _vs(nc, f"{tag}_IDS")
    idt = _ident(nc, blk, IDS, sc, tag)

    CXL = _vs(nc, f"{tag}_CXL")
    OBL = _vs(nc, f"{tag}_OBL")
    GLD = [_vs(nc, f"{tag}_GLD{i}", swdge=True) for i in range(2)]
    LW = [_vs(nc, f"{tag}_LW{i}") for i in range(2)]
    CV = _vs(nc, f"{tag}_CV")
    MM = _vs(nc, f"{tag}_MM")
    SGR = _vs(nc, f"{tag}_SGR")

    SG = sc.sb(f"{tag}_sg", [128, NCH, TC], BF16)
    OB = sc.sb(f"{tag}_ob", [128, NCH], F32)
    GTL = [sc.sb(f"{tag}_gtl{i}", [128, TC], BF16)
           for i in range(2)]

    with (
        nc.sbuf_tensor(f"{tag}_ctx", [128, NCH, TC], BF16) as CTX,
        nc.sbuf_tensor(f"{tag}_wf0", [128, 512], F32) as WFa,
        nc.sbuf_tensor(f"{tag}_wf1", [128, 512], F32) as WFb,
        nc.sbuf_tensor(f"{tag}_wb0", [128, 512], BF16) as WBa,
        nc.sbuf_tensor(f"{tag}_wb1", [128, 512], BF16) as WBb,
        nc.psum_tensor(f"{tag}_ps0", [128, 512], F32) as Pa,
        nc.psum_tensor(f"{tag}_ps1", [128, 512], F32) as Pb,
        nc.psum_tensor(f"{tag}_ps2", [128, 512], F32) as Pc,
        nc.psum_tensor(f"{tag}_ps3", [128, 512], F32) as Pd,
        nc.psum_tensor(f"{tag}_ps4", [128, 512], F32) as Pe,
        nc.psum_tensor(f"{tag}_ps5", [128, 512], F32) as Pf,
    ):
        WF = [WFa, WFb]
        WB = [WBa, WBb]
        PS = [Pa, Pb, Pc, Pd, Pe, Pf]
        NGRP = NCH // 3 + (1 if NCH % 3 else 0)
        groups = [list(range(3 * i, min(3 * i + 3, NCH))) for i in range(NGRP)]

        @blk.sync
        def _(e):
            e.dma_start(out=CTX[:], in_=ctx_hbm.transpose([1, 0, 2])
                        ).then_inc(*_ti(CXL, 16))
            for c in range(NCH):
                e.dma_start(out=OB[:, c:c + 1],
                            in_=ob_hbm[c * 128:(c + 1) * 128][:, None]
                            ).then_inc(*_ti(OBL, 16))
            k = 0
            for gi, g in enumerate(groups):
                c0, c1 = g[0] * 128, (g[-1] + 1) * 128
                for ic in range(NCH):
                    if k >= 2:
                        _wge(e, CV, k - 1)
                    e.dma_start(out=WF[k % 2][:, 0:(c1 - c0)],
                                in_=ow_hbm[ic * 128:(ic + 1) * 128, c0:c1]
                                ).then_inc(*_ti(LW[k % 2], 16))
                    k += 1

        @blk.gpsimd
        def _(e):
            for oc in range(NCH):
                if oc >= 2:
                    _wge(e, SGR, 2 * (oc - 1))
                e.dma_start(out=GTL[oc % 2][:], in_=g_hbm[oc]
                            ).then_inc(*_ti(GLD[oc % 2], 16))

        @blk.scalar
        def _(e):
            k = 0
            for gi, g in enumerate(groups):
                w = len(g) * 128
                for ic in range(NCH):
                    _wge(e, LW[k % 2], 16 * (k // 2 + 1))
                    if k >= 2:
                        _wge(e, MM, k - 1)
                    e.copy(out=WB[k % 2][:, 0:w],
                           in_=WF[k % 2][:, 0:w]).then_inc(*_ti(CV, 1))
                    k += 1

        @blk.tensor
        def _(e):
            _wge(e, CXL, 16)
            k = 0
            for gi, g in enumerate(groups):
                _wge(e, SGR, 2 * (3 * gi))
                for ic in range(NCH):
                    _wge(e, CV, k + 1)
                    for j, oc in enumerate(g):
                        for tt in range(2):
                            ins = e.matmul(
                                PS[j * 2 + tt][:],
                                WB[k % 2][:, j * 128:(j + 1) * 128],
                                CTX[:, ic, tt * 512:(tt + 1) * 512],
                                start=(ic == 0), stop=(ic == NCH - 1))
                    ins.then_inc(*_ti(MM, 1))
                    k += 1

        @blk.vector
        def _(e):
            _wge(e, OBL, 16 * NCH)
            for gi, g in enumerate(groups):
                _wge(e, MM, NCH * (gi + 1))
                for j, oc in enumerate(g):
                    _wge(e, GLD[oc % 2], 16 * (oc // 2 + 1))
                    for tt in range(2):
                        e.scalar_tensor_tensor(
                            out=SG[:, oc, tt * 512:(tt + 1) * 512],
                            in0=PS[j * 2 + tt][:],
                            scalar=OB[:, oc:oc + 1],
                            in1=GTL[oc % 2][:, tt * 512:(tt + 1) * 512],
                            op0=ALU.add, op1=ALU.mult).then_inc(*_ti(SGR, 1))

    emit_addln(nc, blk, f"{tag}b", SG, SGR, lambda oc: 2 * (oc + 1),
               x_hbm, x1_hbm, xnT_hbm, True, idt, IDS, sg_all=2 * NCH)
    sc.close()



def build_combine0(tag="cmb0"):
    nc = bass.Bass("TRN2", target_bir_lowering=False, debug=False)
    ctx = nc.dram_tensor("ctx", [NCH, 128, TC], BF16, kind="ExternalInput").ap()
    ow = nc.dram_tensor("ow", [D, D], F32, kind="ExternalInput").ap()
    ob = nc.dram_tensor("ob", [D], F32, kind="ExternalInput").ap()
    g = nc.dram_tensor("g", [NCH, 128, TC], BF16, kind="ExternalInput").ap()
    x = nc.dram_tensor("x", [TC, D], F32, kind="ExternalInput").ap()
    x1 = nc.dram_tensor("x1", [TC, D], F32, kind="ExternalOutput").ap()
    xnT = nc.dram_tensor("xnT", [NCH, 128, TC], BF16, kind="ExternalOutput").ap()
    with nc.Block() as blk:
        emit_combine0(nc, blk, ctx, ow, ob, g, x, x1, xnT, tag)
    return nc


def emit_final(nc, blk, x1_hbm, s1_hbm, g1_hbm, out_hbm, tag):
    sc = _Scope(nc)
    IDS = _vs(nc, f"{tag}_IDS")
    idt = _ident(nc, blk, IDS, sc, tag)
    S1L = [_vs(nc, f"{tag}_S1L{i}") for i in range(2)]
    G1L = [_vs(nc, f"{tag}_G1L{i}") for i in range(2)]
    SGR = _vs(nc, f"{tag}_SGR")
    SG = sc.sb(f"{tag}_sg", [128, NCH, TC], BF16)
    SC1 = [sc.sb(f"{tag}_sc1{i}", [128, TC], BF16)
           for i in range(2)]
    GC1 = [sc.sb(f"{tag}_gc1{i}", [128, TC], F16)
           for i in range(2)]

    @blk.sync
    def _(e):
        for oc in range(NCH):
            if oc >= 2:
                _wge(e, SGR, oc - 1)
            e.dma_start(out=SC1[oc % 2][:], in_=s1_hbm[oc]
                        ).then_inc(*_ti(S1L[oc % 2], 16))
            e.dma_start(out=GC1[oc % 2][:], in_=g1_hbm[oc]
                        ).then_inc(*_ti(G1L[oc % 2], 16))

    @blk.vector
    def _(e):
        for oc in range(NCH):
            _wge(e, S1L[oc % 2], 16 * (oc // 2 + 1))
            _wge(e, G1L[oc % 2], 16 * (oc // 2 + 1))
            e.tensor_mul(SG[:, oc, :], SC1[oc % 2][:],
                         GC1[oc % 2][:]).then_inc(*_ti(SGR, 1))

    emit_addln(nc, blk, f"{tag}b", SG, SGR, lambda oc: oc + 1,
               x1_hbm, out_hbm, None, False, idt, IDS)
    sc.close()



def build_final(tag="fin"):
    nc = bass.Bass("TRN2", target_bir_lowering=False, debug=False)
    x1 = nc.dram_tensor("x1", [TC, D], F32, kind="ExternalInput").ap()
    s1 = nc.dram_tensor("s1", [NCH, 128, TC], BF16, kind="ExternalInput").ap()
    g1 = nc.dram_tensor("g1", [NCH, 128, TC], F16, kind="ExternalInput").ap()
    out = nc.dram_tensor("out", [TC, D], F32, kind="ExternalOutput").ap()
    with nc.Block() as blk:
        emit_final(nc, blk, x1, s1, g1, out, tag)
    return nc


def emit_densewT(nc, blk, tag, w_hbm, NOB, rhs_get, store_dst, permute,
                 gate=None, TW=None, QST=None, PWQ=None, PQ=None, idt=None,
                 IDS=None, n_in_blk=4, WF=None, WB=None):
    sc = _Scope(nc)
    LW = [_vs(nc, f"{tag}_LW{i}") for i in range(2)]
    CV = _vs(nc, f"{tag}_CV")
    TR = _vs(nc, f"{tag}_TR")
    TE = _vs(nc, f"{tag}_TE")
    MM = _vs(nc, f"{tag}_MM")
    PE = _vs(nc, f"{tag}_PE")
    ST = [_vs(nc, f"{tag}_ST{i}", swdge=True) for i in range(2)]

    if WF is None:
        WF = [sc.sb(f"{tag}_wf{i}", [128, 1024], F32)
              for i in range(2)]
        WB = [sc.sb(f"{tag}_wb{i}", [128, 1024], BF16)
              for i in range(2)]

    @blk.sync
    def _(e):
        if gate is not None:
            _wge(e, gate[0], gate[1])
        k = 0
        for ob in range(NOB):
            for Tb in range(n_in_blk):
                if k >= 2:
                    _wge(e, CV, k - 1)
                e.dma_start(out=WF[k % 2][:],
                            in_=w_hbm[ob * 128:(ob + 1) * 128,
                                      Tb * 1024:(Tb + 1) * 1024]
                            ).then_inc(*_ti(LW[k % 2], 16))
                k += 1

    @blk.scalar
    def _(e):
        k = 0
        for ob in range(NOB):
            for Tb in range(n_in_blk):
                _wge(e, LW[k % 2], 16 * (k // 2 + 1))
                if k >= 2:
                    _wge(e, TR, 8 * (k - 1))
                if permute:
                    e.copy(out=WB[k % 2][:].rearrange("p (j m) -> p j m", j=8),
                           in_=WF[k % 2][:].rearrange("p (m j) -> p j m", j=8)
                           ).then_inc(*_ti(CV, 1))
                else:
                    e.copy(out=WB[k % 2][:], in_=WF[k % 2][:]).then_inc(*_ti(CV, 1))
                k += 1

    @blk.tensor
    def _(e):
        _wge(e, IDS, 2)
        k = 0
        for ob in range(NOB):
            for Tb in range(n_in_blk):
                _wge(e, CV, k + 1)
                if k >= 2:
                    _wge(e, TE, 4 * (k - 1))
                for cc in range(8):
                    ins = e.transpose(
                        PWQ[k % 2][:, cc * 128:(cc + 1) * 128],
                        WB[k % 2][:, cc * 128:(cc + 1) * 128], idt[:])
                ins.then_inc(*_ti(TR, 8))
                k += 1
            _wge(e, TE, 4 * n_in_blk * (ob + 1))
            for tt in range(2):
                kk = 2 * ob + tt
                if kk >= 2:
                    _wge(e, PE, kk - 1)
                for hc in range(NCH):
                    ins = e.matmul(PQ[kk % 2][:],
                                   TW[ob % 2][:, hc, :],
                                   rhs_get(hc, tt),
                                   start=(hc == 0), stop=(hc == NCH - 1))
                ins.then_inc(*_ti(MM, 1))

    @blk.vector
    def _(e):
        for ob in range(NOB):
            for Tb in range(n_in_blk):
                k = ob * n_in_blk + Tb
                _wge(e, TR, 8 * (k + 1))
                if ob >= 1:
                    _wge(e, MM, 2 * ob)
                e.tensor_copy(
                    out=TW[ob % 2][:, Tb * 8:(Tb + 1) * 8, :],
                    in_=PWQ[k % 2][:].rearrange("p (c w) -> p c w", c=8)
                ).then_inc(*_ti(TE, 4))
            for tt in range(2):
                kk = 2 * ob + tt
                _wge(e, MM, kk + 1)
                _wge(e, ST[kk % 2], 16 * (kk // 2))
                e.tensor_copy(out=QST[kk % 2][:],
                              in_=PQ[kk % 2][:]).then_inc(*_ti(PE, 1))

    @blk.gpsimd
    def _(e):
        for ob in range(NOB):
            for tt in range(2):
                kk = 2 * ob + tt
                _wge(e, PE, kk + 1)
                e.dma_start(out=store_dst(ob, tt),
                            in_=QST[kk % 2][:]).then_inc(*_ti(ST[kk % 2], 16))
        _wge(e, ST[0], 16 * NOB)
        _wge(e, ST[1], 16 * NOB)

    return ST
    sc.close()



def emit_l1attn(nc, blk, xnT_hbm, wqkv_hbm, wproj_hbm, s1_hbm, tag):
    sc = _Scope(nc)
    IDS = _vs(nc, f"{tag}_IDS")
    idt = _ident(nc, blk, IDS, sc, tag)

    AO = sc.sb(f"{tag}_ao", [128, NCH, TC], BF16)
    TW = [sc.sb(f"{tag}_tw{i}", [128, NCH, 128], BF16)
          for i in range(2)]
    QST = [sc.sb(f"{tag}_qst{i}", [128, 512], BF16)
           for i in range(2)]
    WFs = [sc.sb(f"{tag}_wfs{i}", [128, 1024], F32)
           for i in range(2)]
    WBs = [sc.sb(f"{tag}_wbs{i}", [128, 1024], BF16)
           for i in range(2)]
    PWQ = [sc.ps(f"{tag}_pwq{i}", [128, 1024], BF16)
           for i in range(2)]
    PQ = [sc.ps(f"{tag}_pq{i}", [128, 512], F32) for i in range(2)]
    PSC = [sc.ps(f"{tag}_psc{i}", [128, 64, 16], F32)
           for i in range(2)]
    PW3 = PWQ[0][0:16, :]

    qkv1h = nc.dram_tensor(f"{tag}_qkv1h", [48, 128, TC], BF16,
                           kind="Internal").ap()
    w2h = nc.dram_tensor(f"{tag}_w2h", [16, 64, 128], BF16, kind="Internal").ap()

    pairs = [(s, t) for s in range(4) for t in range(s + 1)]
    col = {(s, t): s * (s + 1) // 2 + t for (s, t) in pairs}
    AVSCH = []
    for s in range(4):
        for t in range(s + 1):
            for hf in range(2):
                AVSCH.append((s, t, hf))
    for s in range(4):
        for hf in range(2):
            AVSCH.append((s, None, hf))
    MRD = []
    _av = 0
    for (s, t, hf) in AVSCH:
        if t is None or t == 0:
            _av += 1
            MRD.append(_av)
        else:
            _av += 1
            MRD.append(_av)
            _av += 1
    AV_TOTAL = _av

    with nc.sbuf_tensor(f"{tag}_xnt1", [128, NCH, TC], BF16) as XNT1:
        XL1 = _vs(nc, f"{tag}_XL1")

        @blk.sync
        def _(e):
            e.dma_start(out=XNT1[:], in_=xnT_hbm.transpose([1, 0, 2])
                        ).then_inc(*_ti(XL1, 16))

        @blk.tensor
        def _(e):
            _wge(e, XL1, 16)

        AST = emit_densewT(
            nc, blk, f"{tag}A", wqkv_hbm, 48,
            lambda hc, tt: XNT1[:, hc, tt * 512:(tt + 1) * 512],
            lambda ob, tt: qkv1h[ob, :, tt * 512:(tt + 1) * 512],
            True, gate=None, TW=TW, QST=QST, PWQ=PWQ, PQ=PQ, idt=idt, IDS=IDS,
            WF=WFs, WB=WBs)

    ESB = _vs(nc, f"{tag}_ESB")
    KVL = _vs(nc, f"{tag}_KVL")
    QSL = _vs(nc, f"{tag}_QSL")
    PRD = _vs(nc, f"{tag}_PRD")
    SMM = _vs(nc, f"{tag}_SMM")
    SFX = _vs(nc, f"{tag}_SFX")
    SFA = _vs(nc, f"{tag}_SFA")
    TW2 = _vs(nc, f"{tag}_TW2")
    W3E = _vs(nc, f"{tag}_W3E")
    W3S = _vs(nc, f"{tag}_W3S", swdge=True)
    WBL = [_vs(nc, f"{tag}_WBL{i}") for i in range(2)]
    AVV = _vs(nc, f"{tag}_AVV")
    SAD = _vs(nc, f"{tag}_SAD")

    with nc.sbuf_tensor(f"{tag}_sca0", [128, 64, 16], F32) as SCAa, \
         nc.sbuf_tensor(f"{tag}_sca1", [128, 64, 16], F32) as SCAb, \
         nc.sbuf_tensor(f"{tag}_qs", [128, 8, 4, 256], BF16) as QS, \
         nc.sbuf_tensor(f"{tag}_ktv", [128, 8, 4, 256], BF16) as KTV, \
         nc.sbuf_tensor(f"{tag}_pr0", [128, 8192], BF16) as PRa, \
         nc.sbuf_tensor(f"{tag}_pr1", [128, 8192], BF16) as PRb, \
         nc.sbuf_tensor(f"{tag}_est", [128, 31], BF16) as EST, \
         nc.sbuf_tensor(f"{tag}_w2", [128, 64, 16], BF16) as W2, \
         nc.sbuf_tensor(f"{tag}_nmx1", [128, 64], F32) as NMX1, \
         nc.sbuf_tensor(f"{tag}_scw", [128, 64, 4], F32) as SCW, \
         nc.sbuf_tensor(f"{tag}_z1", [128, 64], F32) as Z1, \
         nc.sbuf_tensor(f"{tag}_rz1", [128, 64], F32) as RZ1, \
         nc.sbuf_tensor(f"{tag}_w3s", [16, 8, 128], BF16) as W3STG, \
         nc.sbuf_tensor(f"{tag}_wbc0", [128, 4096], BF16) as WBCa, \
         nc.sbuf_tensor(f"{tag}_wbc1", [128, 4096], BF16) as WBCb, \
         nc.sbuf_tensor(f"{tag}_tmp", [128, 4096], BF16) as TMP:
        PR = [PRa, PRb]
        WBC = [WBCa, WBCb]
        SCACC = [SCAa, SCAb]
        SCA = SCACC[1]

        def kb(t):
            return KTV[:, :, t:t + 1, :].broadcast_to([128, 8, 4, 256])

        vb = kb

        def ao_view(s):
            return AO[:].rearrange("p (kv sl) t -> p kv sl t", kv=8)[
                :, :, :, s * 256:(s + 1) * 256]

        @blk.sync
        def _(e):
            _wge(e, AST[0], 16 * 48)
            _wge(e, AST[1], 16 * 48)
            for t in range(4):
                src = qkv1h.rearrange("(kv sl) d t -> kv sl d t", kv=8)[
                    :, 4, :, t * 256:(t + 1) * 256].transpose([1, 0, 2])
                e.dma_start(out=KTV[:, :, t, :], in_=src).then_inc(*_ti(KVL, 16))
            for s in range(4):
                if s >= 1:
                    _wge(e, PRD, s * (s + 1) // 2)
                for sl in range(4):
                    src = qkv1h.rearrange("(kv sl) d t -> kv sl d t", kv=8)[
                        :, sl, :, s * 256:(s + 1) * 256].transpose([1, 0, 2])
                    e.dma_start(out=QS[:, :, sl, :], in_=src).then_inc(*_ti(QSL, 16))
            _wge(e, SMM, 10)
            for t in range(4):
                src = qkv1h.rearrange("(kv sl) d t -> kv sl d t", kv=8)[
                    :, 5, :, t * 256:(t + 1) * 256].transpose([1, 0, 2])
                e.dma_start(out=KTV[:, :, t, :], in_=src).then_inc(*_ti(KVL, 16))
            _wge(e, W3S, 16 * 8)
            for li, (s, t, hf) in enumerate(AVSCH):
                st = col[(s, t)] if t is not None else 12 + s
                if li >= 2:
                    _wge(e, AVV, MRD[li - 2])
                src = w2h[st, hf * 32:(hf + 1) * 32].flatten()[None, :]
                src = src.partition_broadcast(128)[:, 0, :]
                e.dma_start(out=WBC[li % 2][:], in_=src).then_inc(*_ti(WBL[li % 2], 16))

        @blk.vector
        def _(e):
            e.memset(EST[:], 0.0).then_inc(*_ti(ESB, 1))
            _wge(e, ESB, 1)
            e.memset(EST[:, 15:16], float(ISQ)).then_inc(*_ti(ESB, 1))
            _wge(e, KVL, 16 * 4)
            def acc_round(pi):
                _wge(e, SMM, pi + 1)
                if pi == 0:
                    e.tensor_copy(out=SCACC[0][:],
                                  in_=PSC[0][:]).then_inc(*_ti(SAD, 1))
                else:
                    _wge(e, SAD, pi)
                    e.tensor_add(SCACC[pi % 2][:], SCACC[(pi - 1) % 2][:],
                                 PSC[pi % 2][:]).then_inc(*_ti(SAD, 1))

            for pi, (s, t) in enumerate(pairs):
                _wge(e, QSL, 16 * 4 * (s + 1))
                if pi >= 2:
                    _wge(e, SMM, pi - 1)
                e.tensor_mul(PR[pi % 2][:].rearrange(
                    "p (kv sl t) -> p kv sl t", kv=8, sl=4),
                    QS[:], kb(t)).then_inc(*_ti(PRD, 1))
                if pi >= 1:
                    acc_round(pi - 1)
            acc_round(9)
            _wge(e, SAD, 10)
            e.memset(W2[:], 0.0).then_inc(*_ti(SFX, 1))
            _wge(e, SFX, 1)
            e.memset(W2[:, :, 0:1], 1.0)
            e.memset(W2[:, :, 12:13], 1.0).then_inc(*_ti(SFX, 1))
            for si in range(1, 4):
                c0 = si * (si + 1) // 2
                base = 5 * (si - 1) + 2
                if si >= 2:
                    _wge(e, SFA, si - 1)
                e.reduce_max(out=NMX1[:], in_=SCA[:, :, c0:c0 + si + 1],
                             axis=AX.X, negate=True).then_inc(*_ti(SFX, 1))
                _wge(e, SFX, base + 1)
                e.tensor_add(SCW[:, :, 0:si + 1], SCA[:, :, c0:c0 + si + 1],
                             NMX1[:, :, None].broadcast_to([128, 64, si + 1])
                             ).then_inc(*_ti(SFX, 1))
                _wge(e, SFA, si)
                e.reduce_sum(out=Z1[:], in_=W2[:, :, c0:c0 + si + 1],
                             axis=AX.X).then_inc(*_ti(SFX, 1))
                _wge(e, SFX, base + 3)
                e.reciprocal(out=RZ1[:], in_=Z1[:]).then_inc(*_ti(SFX, 1))
                _wge(e, SFX, base + 4)
                e.tensor_copy(out=W2[:, :, 12 + si:13 + si],
                              in_=RZ1[:, :, None]).then_inc(*_ti(SFX, 1))
            _wge(e, KVL, 16 * 8)
            av = 0
            for li, (s, t, hf) in enumerate(AVSCH):
                kv0 = hf * 4
                _wge(e, WBL[li % 2], 16 * (li // 2 + 1))
                wv = WBC[li % 2][:].rearrange(
                    "p (kv sl t) -> p kv sl t", kv=4, sl=4)
                aov = ao_view(s)[:, kv0:kv0 + 4, :, :]
                if t is None:
                    _wge(e, AVV, av)
                    e.tensor_mul(aov, aov, wv).then_inc(*_ti(AVV, 1))
                    av += 1
                elif t == 0:
                    e.tensor_mul(aov, vb(t)[:, kv0:kv0 + 4, :, :],
                                 wv).then_inc(*_ti(AVV, 1))
                    av += 1
                else:
                    tv = TMP[:].rearrange("p (kv sl t) -> p kv sl t",
                                          kv=4, sl=4)
                    _wge(e, AVV, av)
                    e.tensor_mul(tv, vb(t)[:, kv0:kv0 + 4, :, :],
                                 wv).then_inc(*_ti(AVV, 1))
                    av += 1
                    _wge(e, AVV, av)
                    e.tensor_add(aov, aov, tv).then_inc(*_ti(AVV, 1))
                    av += 1

        @blk.tensor
        def _(e):
            _wge(e, ESB, 2)
            for pi, (s, t) in enumerate(pairs):
                _wge(e, PRD, pi + 1)
                if pi >= 2:
                    _wge(e, SAD, pi - 1)
                cl = col[(s, t)]
                for bb in range(64):
                    ins = e.matmul(PSC[pi % 2][:, bb, :],
                                   PR[pi % 2][:, bb * 128:(bb + 1) * 128],
                                   EST[:, 15 - cl:31 - cl],
                                   start=True, stop=True)
                ins.then_inc(*_ti(SMM, 1))
            _wge(e, SFX, 17)
            _wge(e, SFA, 3)
            for grp in range(8):
                if grp >= 1:
                    _wge(e, W3E, grp)
                for j in range(8):
                    bb = grp * 8 + j
                    ins = e.transpose(PW3[:, j * 128:(j + 1) * 128],
                                      W2[:, bb, :], idt[:])
                ins.then_inc(*_ti(TW2, 1))

        @blk.scalar
        def _(e):
            for si in range(1, 4):
                c0 = si * (si + 1) // 2
                _wge(e, SFX, 5 * (si - 1) + 4)
                e.activation(out=W2[:, :, c0:c0 + si + 1],
                             in_=SCW[:, :, 0:si + 1],
                             func=AF.Exp).then_inc(*_ti(SFA, 1))
            for grp in range(8):
                _wge(e, TW2, grp + 1)
                if grp >= 1:
                    _wge(e, W3S, 16 * grp)
                e.copy(out=W3STG[:],
                       in_=PW3[:].rearrange("p (c w) -> p c w", c=8)
                       ).then_inc(*_ti(W3E, 1))

        @blk.gpsimd
        def _(e):
            for grp in range(8):
                _wge(e, W3E, grp + 1)
                e.dma_start(out=w2h[:, grp * 8:(grp + 1) * 8, :],
                            in_=W3STG[:]).then_inc(*_ti(W3S, 16))

    emit_densewT(
        nc, blk, f"{tag}C", wproj_hbm, NCH,
        lambda hc, tt: AO[:, hc, tt * 512:(tt + 1) * 512],
        lambda ob, tt: s1_hbm[ob, :, tt * 512:(tt + 1) * 512],
        False, gate=(AVV, AV_TOTAL), TW=TW, QST=QST, PWQ=PWQ, PQ=PQ, idt=idt,
        IDS=IDS, WF=WFs, WB=WBs)
    sc.close()



def build_l1attn(tag="l1a"):
    nc = bass.Bass("TRN2", target_bir_lowering=False, debug=False)
    xnT = nc.dram_tensor("xnT", [NCH, 128, TC], BF16, kind="ExternalInput").ap()
    wqkv = nc.dram_tensor("wqkv", [6144, D], F32, kind="ExternalInput").ap()
    wproj = nc.dram_tensor("wproj", [D, D], F32, kind="ExternalInput").ap()
    s1 = nc.dram_tensor("s1", [NCH, 128, TC], BF16, kind="ExternalOutput").ap()
    with nc.Block() as blk:
        emit_l1attn(nc, blk, xnT, wqkv, wproj, s1, tag)
    return nc


INTER = 11008
NTP = 11


def _tp_cols(tp):
    n0 = tp * 1024
    w = min(1024, INTER - n0)
    return n0, w, w // 8


def emit_ffn1(nc, blk, xnT_hbm, gq, gs, gz, uq, us, uz, dq, ds, dz, g1_hbm, tag):
    sc = _Scope(nc)
    IDS = _vs(nc, f"{tag}_IDS")
    idt = _ident(nc, blk, IDS, sc, tag)
    interh = nc.dram_tensor(f"{tag}_interh", [88, 128, TC], BF16,
                            kind="Internal").ap()

    G16 = sc.sb(f"{tag}_g16", [128, 8], BF16)
    GI = sc.sb(f"{tag}_gi", [128, 8], I32)
    GI2 = sc.sb(f"{tag}_gi2", [128, 8], I32)
    GI3 = sc.sb(f"{tag}_gi3", [128, 8], I32)

    GS16 = _vs(nc, f"{tag}_GS16")

    @blk.gpsimd
    def _(g):
        g.iota(GI[:], pattern=[[-16, 8]], base=0,
               channel_multiplier=1).then_inc(*_ti(IDS, 1))

    @blk.vector
    def _(v):
        _wge(v, IDS, 3)
        v.tensor_scalar(out=GI2[:], in0=GI[:], scalar1=0, scalar2=None,
                        op0=ALU.is_ge).then_inc(*_ti(GS16, 1))
        _wge(v, GS16, 1)
        v.tensor_scalar(out=GI3[:], in0=GI[:], scalar1=16, scalar2=None,
                        op0=ALU.is_lt).then_inc(*_ti(GS16, 1))
        _wge(v, GS16, 2)
        v.tensor_tensor(out=G16[:], in0=GI2[:], in1=GI3[:],
                        op=ALU.mult).then_inc(*_ti(GS16, 1))

    XLF = _vs(nc, f"{tag}_XLF")
    ZQS = _vs(nc, f"{tag}_ZQS")
    SXM = _vs(nc, f"{tag}_SXM")
    SXE = _vs(nc, f"{tag}_SXE")
    SXD = _vs(nc, f"{tag}_SXD", swdge=True)
    SCLD = _vs(nc, f"{tag}_SCLD")
    QWL = [_vs(nc, f"{tag}_QWL{i}") for i in range(4)]
    SBL = [_vs(nc, f"{tag}_SBL{i}") for i in range(4)]
    ZPP = _vs(nc, f"{tag}_ZPP")
    UPK = _vs(nc, f"{tag}_UPK")
    SCL = _vs(nc, f"{tag}_SCL")
    MM1 = _vs(nc, f"{tag}_MM1")
    GEVC = _vs(nc, f"{tag}_GEVC")
    UEVC = _vs(nc, f"{tag}_UEVC")
    IST = [_vs(nc, f"{tag}_IST{i}", swdge=True) for i in range(2)]

    from contextlib import ExitStack
    _st = ExitStack()
    with _st:
        XNT1 = _st.enter_context(
            nc.sbuf_tensor(f"{tag}_xnt1", [128, NCH, TC], BF16))
        UB = _st.enter_context(
            nc.sbuf_tensor(f"{tag}_ub", [128, NCH, 1024], BF16))
        SGT = _st.enter_context(nc.sbuf_tensor(f"{tag}_sgt", [128, 8, TC], BF16))
        SXA = _st.enter_context(nc.sbuf_tensor(f"{tag}_sxa", [32, TC], BF16))
        ESX = _st.enter_context(nc.sbuf_tensor(f"{tag}_esx", [8, TC], BF16))
        ZQG = _st.enter_context(nc.sbuf_tensor(f"{tag}_zqg", [32, 1376], I32))
        ZQU = _st.enter_context(nc.sbuf_tensor(f"{tag}_zqu", [32, 1376], I32))
        QWa = _st.enter_context(nc.sbuf_tensor(f"{tag}_qw0", [128, 1024], I32))
        QWb = _st.enter_context(nc.sbuf_tensor(f"{tag}_qw1", [128, 1024], I32))
        QWc = _st.enter_context(nc.sbuf_tensor(f"{tag}_qw2", [128, 1024], I32))
        QWd = _st.enter_context(nc.sbuf_tensor(f"{tag}_qw3", [128, 1024], I32))
        SBa = _st.enter_context(nc.sbuf_tensor(f"{tag}_sbt0", [128, 1024], F16))
        SBb = _st.enter_context(nc.sbuf_tensor(f"{tag}_sbt1", [128, 1024], F16))
        SBc = _st.enter_context(nc.sbuf_tensor(f"{tag}_sbt2", [128, 1024], F16))
        SBd = _st.enter_context(nc.sbuf_tensor(f"{tag}_sbt3", [128, 1024], F16))
        UIa = _st.enter_context(nc.sbuf_tensor(f"{tag}_ui0", [128, 1024], I32))
        UIb = _st.enter_context(nc.sbuf_tensor(f"{tag}_ui1", [128, 1024], I32))
        SCS = _st.enter_context(nc.sbuf_tensor(f"{tag}_scs", [32, 1024], F16))
        ZI = _st.enter_context(nc.sbuf_tensor(f"{tag}_zi", [32, 1024], I32))
        ZF = _st.enter_context(nc.sbuf_tensor(f"{tag}_zf", [32, 1024], F32))
        ZPN = _st.enter_context(nc.sbuf_tensor(f"{tag}_zpn", [32, 1024], BF16))
        ISa = _st.enter_context(nc.sbuf_tensor(f"{tag}_inst0", [128, 512], BF16))
        ISb = _st.enter_context(nc.sbuf_tensor(f"{tag}_inst1", [128, 512], BF16))
        P1a = _st.enter_context(nc.psum_tensor(f"{tag}_p1a", [128, 512], F32))
        P1b = _st.enter_context(nc.psum_tensor(f"{tag}_p1b", [128, 512], F32))
        P1c = _st.enter_context(nc.psum_tensor(f"{tag}_p1c", [128, 512], F32))
        P1d = _st.enter_context(nc.psum_tensor(f"{tag}_p1d", [128, 512], F32))
        PSX = _st.enter_context(nc.psum_tensor(f"{tag}_psx", [8, TC], F32))
        QW = [QWa, QWb, QWc, QWd]
        SBT = [SBa, SBb, SBc, SBd]
        UI = [UIa, UIb]
        INST = [ISa, ISb]
        P1 = [[P1a, P1b], [P1c, P1d]]
        ZQ = [ZQG, ZQU]
        mats = [(gq, gs, gz), (uq, us, uz)]

        @blk.sync
        def _(e):
            e.dma_start(out=XNT1[:], in_=xnT_hbm.transpose([1, 0, 2])
                        ).then_inc(*_ti(XLF, 16))
            e.dma_start(out=ZQG[:], in_=gz).then_inc(*_ti(ZQS, 16))
            e.dma_start(out=ZQU[:], in_=uz).then_inc(*_ti(ZQS, 16))
            k = 0
            for tp in range(NTP):
                n0, w, mr = _tp_cols(tp)
                for mi in range(2):
                    i = 2 * tp + mi
                    mq, ms, mz = mats[mi]
                    if i >= 1:
                        _wge(e, ZPP, 3 * i)
                    e.dma_start(out=SCS[:, 0:w],
                                in_=ms[:, n0:n0 + w]).then_inc(*_ti(SCLD, 16))
                    for T in range(4):
                        if k >= 4:
                            _wge(e, UPK, 8 * (k - 3))
                            _wge(e, SCL, 8 * (k - 3))
                        e.dma_start(out=QW[k % 4][:, 0:w],
                                    in_=mq[T * 128:(T + 1) * 128, n0:n0 + w]
                                    ).then_inc(*_ti(QWL[k % 4], 16))
                        src = ms[8 * T:8 * T + 8, n0:n0 + w]
                        src = src[:, None, :].partition_broadcast(16)
                        src = src.transpose([1, 0, 2, 3])[:, :, 0, :]
                        e.dma_start(out=SBT[k % 4][:, 0:w],
                                    in_=src).then_inc(*_ti(SBL[k % 4], 16))
                        k += 1

        @blk.vector
        def _(e):
            for T in range(4):
                _wge(e, SXM, T + 1)
                if T >= 1:
                    _wge(e, SXD, 16 * T)
                e.tensor_copy(out=ESX[:], in_=PSX[:]).then_inc(*_ti(SXE, 1))
            k = 0
            for tp in range(NTP):
                n0, w, mr = _tp_cols(tp)
                for mi in range(2):
                    i = 2 * tp + mi
                    _wge(e, ZQS, 32)
                    _wge(e, SCLD, 16 * (i + 1))
                    if i >= 1:
                        _wge(e, MM1, 16 * i)
                    for jn in range(8):
                        ins = e.tensor_scalar(
                            out=ZI[:, jn:w:8],
                            in0=ZQ[mi][:, n0 // 8:(n0 + w) // 8],
                            scalar1=4 * jn, scalar2=15,
                            op0=ALU.logical_shift_right, op1=ALU.bitwise_and)
                    ins.then_inc(*_ti(ZPP, 1))
                    _wge(e, ZPP, 3 * i + 1)
                    e.tensor_scalar(out=ZF[:, 0:w], in0=ZI[:, 0:w], scalar1=1,
                                    scalar2=-1.0, op0=ALU.add,
                                    op1=ALU.mult).then_inc(*_ti(ZPP, 1))
                    _wge(e, ZPP, 3 * i + 2)
                    e.tensor_mul(ZPN[:, 0:w], ZF[:, 0:w],
                                 SCS[:, 0:w]).then_inc(*_ti(ZPP, 1))
                    for T in range(4):
                        _wge(e, QWL[k % 4], 16 * (k // 4 + 1))
                        _wge(e, SBL[k % 4], 16 * (k // 4 + 1))
                        if T == 0:
                            _wge(e, MM1, 16 * i)
                        for j in range(8):
                            kc = 8 * T + j
                            if 8 * k + j > 0:
                                _wge(e, SCL, 8 * k + j)
                            e.tensor_scalar(
                                out=UI[0][:, 0:w], in0=QW[k % 4][:, 0:w],
                                scalar1=4 * j, scalar2=15,
                                op0=ALU.logical_shift_right,
                                op1=ALU.bitwise_and).then_inc(*_ti(UPK, 1))
                            _wge(e, UPK, 8 * k + j + 1)
                            e.tensor_mul(UB[:, kc, 0:w], UI[0][:, 0:w],
                                         SBT[k % 4][:, 0:w]).then_inc(*_ti(SCL, 1))
                        k += 1
                    if mi == 1:
                        for jp in range(8):
                            st = 8 * tp + jp
                            for tt in range(2):
                                _wge(e, MM1, 16 * i + 2 * jp + tt + 1)
                                _wge(e, GEVC, 16 * tp + 2 * jp + 2)
                                sti = 2 * st + tt
                                _wge(e, IST[sti % 2], 16 * (sti // 2))
                                e.tensor_mul(
                                    INST[sti % 2][0:mr, :],
                                    P1[jp % 2][tt][0:mr, :],
                                    SGT[0:mr, jp, tt * 512:(tt + 1) * 512]
                                ).then_inc(*_ti(UEVC, 1))

        @blk.scalar
        def _(e):
            for tp in range(NTP):
                n0, w, mr = _tp_cols(tp)
                i = 2 * tp
                for jp in range(8):
                    for tt in range(2):
                        _wge(e, MM1, 16 * i + 2 * jp + tt + 1)
                        e.activation(
                            out=SGT[0:mr, jp, tt * 512:(tt + 1) * 512],
                            in_=P1[jp % 2][tt][0:mr, :],
                            func=AF.Silu).then_inc(*_ti(GEVC, 1))
                for jp in range(8):
                    st = 8 * tp + jp
                    for tt in range(2):
                        sti = 2 * st + tt
                        _wge(e, UEVC, 16 * tp + 2 * jp + tt + 1)
                        e.dma_start(
                            out=interh[st, 0:mr, tt * 512:(tt + 1) * 512],
                            in_=INST[sti % 2][0:mr, :]
                        ).then_inc(*_ti(IST[sti % 2], 16))
            _wge(e, IST[0], 16 * 88)
            _wge(e, IST[1], 16 * 88)

        @blk.tensor
        def _(e):
            _wge(e, GS16, 3)
            _wge(e, XLF, 16)
            for T in range(4):
                if T >= 1:
                    _wge(e, SXE, T)
                for tt in range(2):
                    for j in range(8):
                        kc = 8 * T + j
                        ins = e.matmul(PSX[:, tt * 512:(tt + 1) * 512],
                                       G16[:],
                                       XNT1[:, kc, tt * 512:(tt + 1) * 512],
                                       start=(j == 0), stop=(j == 7))
                ins.then_inc(*_ti(SXM, 1))
            for tp in range(NTP):
                n0, w, mr = _tp_cols(tp)
                for mi in range(2):
                    i = 2 * tp + mi
                    _wge(e, SCL, 32 * (i + 1))
                    _wge(e, ZPP, 3 * i + 3)
                    if i == 0:
                        _wge(e, SXD, 16 * 4)
                    for jp in range(8):
                        for tt in range(2):
                            if jp >= 2:
                                pi_, pjp = i, jp - 2
                            else:
                                pi_, pjp = i - 1, jp + 6
                            if pi_ >= 0:
                                psem = GEVC if pi_ % 2 == 0 else UEVC
                                _wge(e, psem, 16 * (pi_ // 2) + 2 * pjp + tt + 1)
                            for kc in range(NCH):
                                e.matmul(
                                    P1[jp % 2][tt][0:mr, :],
                                    UB[:, kc, jp:w:8],
                                    XNT1[:, kc, tt * 512:(tt + 1) * 512],
                                    start=(kc == 0), stop=False)
                            e.matmul(
                                P1[jp % 2][tt][0:mr, :],
                                ZPN[:, jp:w:8],
                                SXA[:, tt * 512:(tt + 1) * 512],
                                start=False, stop=True).then_inc(*_ti(MM1, 1))

        @blk.gpsimd
        def _(e):
            for T in range(4):
                _wge(e, SXE, T + 1)
                e.dma_start(out=SXA[8 * T:8 * T + 8, :],
                            in_=ESX[:]).then_inc(*_ti(SXD, 16))

    ILD = [_vs(nc, f"{tag}_ILD{i}") for i in range(2)]
    QDL = [_vs(nc, f"{tag}_QDL{i}") for i in range(2)]
    SDL = _vs(nc, f"{tag}_SDL")
    ZPD2 = _vs(nc, f"{tag}_ZPD2")
    UPK2 = _vs(nc, f"{tag}_UPK2")
    SCL2 = _vs(nc, f"{tag}_SCL2")
    MM2 = _vs(nc, f"{tag}_MM2")
    ADC = _vs(nc, f"{tag}_ADC")
    SX2M = _vs(nc, f"{tag}_SX2M")
    SX2E = _vs(nc, f"{tag}_SX2E")
    SX2D = _vs(nc, f"{tag}_SX2D", swdge=True)
    FEV = _vs(nc, f"{tag}_FEV")
    FSL = _vs(nc, f"{tag}_FSL")
    GST2 = [_vs(nc, f"{tag}_GST2{i}", swdge=True) for i in range(2)]

    ACC = sc.sb(f"{tag}_acc", [128, NCH, TC], BF16)
    ITL = sc.sb(f"{tag}_itl", [128, 8, TC], BF16)
    QWD = sc.sb(f"{tag}_qwd", [128, D], I32)
    SD = sc.sb(f"{tag}_sd", [128, D], F16)
    UID = sc.sb(f"{tag}_uid", [128, D], I32)
    SXD2 = sc.sb(f"{tag}_sxd2", [96, TC], BF16)
    ESX2 = sc.sb(f"{tag}_esx2", [8, TC], BF16)
    ZQD = sc.sb(f"{tag}_zqd", [86, 512], I32)
    ZPND = sc.sb(f"{tag}_zpnd", [86, D], BF16)
    GS2 = [sc.sb(f"{tag}_gs2{i}", [128, 512], F16)
           for i in range(2)]
    GS3 = [sc.sb(f"{tag}_gs3{i}", [128, 512], F16)
           for i in range(2)]

    with nc.sbuf_tensor(f"{tag}_zid", [86, D], I32) as ZID, \
         nc.sbuf_tensor(f"{tag}_zfd", [86, D], F32) as ZFD, \
         nc.sbuf_tensor(f"{tag}_scd", [86, D], F16) as SCD:
        @blk.sync
        def _(e):
            _wge(e, IST[0], 16 * 88)
            _wge(e, IST[1], 16 * 88)
            e.dma_start(out=ZQD[:], in_=dz).then_inc(*_ti(ZPD2, 16))
            e.dma_start(out=SCD[:], in_=ds).then_inc(*_ti(ZPD2, 16))

        @blk.vector
        def _(e):
            _wge(e, ZPD2, 32)
            for jn in range(8):
                ins = e.tensor_scalar(
                    out=ZID[:, jn:D:8], in0=ZQD[:],
                    scalar1=4 * jn, scalar2=15,
                    op0=ALU.logical_shift_right, op1=ALU.bitwise_and)
            ins.then_inc(*_ti(ZPD2, 1))
            _wge(e, ZPD2, 33)
            e.tensor_scalar(out=ZFD[:], in0=ZID[:], scalar1=1, scalar2=-1.0,
                            op0=ALU.add, op1=ALU.mult).then_inc(*_ti(ZPD2, 1))
            _wge(e, ZPD2, 34)
            e.tensor_mul(ZPND[:], ZFD[:], SCD[:]).then_inc(*_ti(ZPD2, 1))

    UBD = sc.sb(f"{tag}_ubd", [128, 8, D], BF16)
    PD = [sc.ps(f"{tag}_pd{i}", [128, 512], F32)
          for i in range(4)]
    PSX2 = sc.ps(f"{tag}_psx2", [8, TC], F32)

    @blk.sync
    def _(e):
        _wge(e, ZPD2, 35)
        for tpp in range(NTP):
            mr2 = 128 if tpp < 10 else 96
            glo = 8 * tpp
            if tpp >= 1:
                _wge(e, MM2, 64 * (tpp - 1) + 32)
                _wge(e, SX2M, tpp)
            e.dma_start(out=ITL[0:mr2, :, 0:512],
                        in_=interh[glo:glo + 8, 0:mr2, 0:512]
                        .transpose([1, 0, 2])).then_inc(*_ti(ILD[0], 16))
            if tpp >= 1:
                _wge(e, MM2, 64 * tpp)
            e.dma_start(out=ITL[0:mr2, :, 512:1024],
                        in_=interh[glo:glo + 8, 0:mr2, 512:1024]
                        .transpose([1, 0, 2])).then_inc(*_ti(ILD[1], 16))
            for h in range(2):
                e.dma_start(out=QWD[0:mr2, h * 2048:(h + 1) * 2048],
                            in_=dq[128 * tpp:128 * tpp + mr2,
                                   h * 2048:(h + 1) * 2048]
                            ).then_inc(*_ti(QDL[h], 16))
            ng = 8 if tpp < 10 else 6
            src = ds[8 * tpp:8 * tpp + ng, :]
            src = src[:, None, :].partition_broadcast(16)
            src = src.transpose([1, 0, 2, 3])[:, :, 0, :]
            e.dma_start(out=SD[0:16 * ng, :],
                        in_=src).then_inc(*_ti(SDL, 16))

    @blk.vector
    def _(e):
        for tpp in range(NTP):
            mr2 = 128 if tpp < 10 else 96
            _wge(e, QDL[0], 16 * (tpp + 1))
            _wge(e, QDL[1], 16 * (tpp + 1))
            _wge(e, SDL, 16 * (tpp + 1))
            for j in range(8):
                if 8 * tpp + j > 0:
                    _wge(e, SCL2, 8 * tpp + j)
                e.tensor_scalar(out=UID[0:mr2, :], in0=QWD[0:mr2, :],
                                scalar1=4 * j, scalar2=15,
                                op0=ALU.logical_shift_right,
                                op1=ALU.bitwise_and).then_inc(*_ti(UPK2, 1))
                _wge(e, UPK2, 8 * tpp + j + 1)
                e.tensor_mul(UBD[0:mr2, j, :], UID[0:mr2, :],
                             SD[0:mr2, :]).then_inc(*_ti(SCL2, 1))
            for tt in range(2):
                for oc in range(NCH):
                    idx = 64 * tpp + 32 * tt + oc
                    _wge(e, MM2, idx + 1)
                    if tpp == 0:
                        e.tensor_copy(out=ACC[:, oc, tt * 512:(tt + 1) * 512],
                                      in_=PD[idx % 4][:]).then_inc(*_ti(ADC, 1))
                    else:
                        e.tensor_add(ACC[:, oc, tt * 512:(tt + 1) * 512],
                                     ACC[:, oc, tt * 512:(tt + 1) * 512],
                                     PD[idx % 4][:]).then_inc(*_ti(ADC, 1))
                if tt == 0:
                    _wge(e, SX2M, tpp + 1)
                    if tpp >= 1:
                        _wge(e, SX2D, 16 * tpp)
                    e.tensor_copy(out=ESX2[:],
                                  in_=PSX2[:]).then_inc(*_ti(SX2E, 1))
        for oc in range(NCH):
            for tt in range(2):
                idx = 64 * NTP + 2 * oc + tt
                _wge(e, MM2, idx + 1)
                e.tensor_add(ACC[:, oc, tt * 512:(tt + 1) * 512],
                             ACC[:, oc, tt * 512:(tt + 1) * 512],
                             PD[idx % 4][:]).then_inc(*_ti(ADC, 1))

    @blk.tensor
    def _(e):
        for tpp in range(NTP):
            mr2 = 128 if tpp < 10 else 96
            ng = 8 if tpp < 10 else 6

            def dgroup(oc, tt, idx):
                if idx >= 4:
                    _wge(e, ADC, idx - 3)
                for j in range(8):
                    ins = e.matmul(PD[idx % 4][:],
                                   UBD[0:mr2, j, oc * 128:(oc + 1) * 128],
                                   ITL[0:mr2, j, tt * 512:(tt + 1) * 512],
                                   start=(j == 0), stop=(j == 7))
                ins.then_inc(*_ti(MM2, 1))

            _wge(e, ILD[0], 16 * (tpp + 1))
            _wge(e, SCL2, 8 * (tpp + 1))
            for oc in range(NCH):
                dgroup(oc, 0, 64 * tpp + oc)
            _wge(e, ILD[1], 16 * (tpp + 1))
            if tpp >= 1:
                _wge(e, SX2E, tpp)
            for tt in range(2):
                for j in range(8):
                    ins = e.matmul(PSX2[0:ng, tt * 512:(tt + 1) * 512],
                                   G16[0:mr2, 0:ng],
                                   ITL[0:mr2, j, tt * 512:(tt + 1) * 512],
                                   start=(j == 0), stop=(j == 7))
            ins.then_inc(*_ti(SX2M, 1))
            for oc in range(NCH):
                dgroup(oc, 1, 64 * tpp + 32 + oc)
        _wge(e, SX2D, 16 * NTP)
        _wge(e, ZPD2, 35)
        for oc in range(NCH):
            for tt in range(2):
                idx = 64 * NTP + 2 * oc + tt
                _wge(e, ADC, idx - 3)
                e.matmul(PD[idx % 4][:],
                         ZPND[:, oc * 128:(oc + 1) * 128],
                         SXD2[0:86, tt * 512:(tt + 1) * 512],
                         start=True, stop=True).then_inc(*_ti(MM2, 1))

    @blk.gpsimd
    def _(e):
        for tpp in range(NTP):
            _wge(e, SX2E, tpp + 1)
            ng = 8 if tpp < 10 else 6
            e.dma_start(out=SXD2[8 * tpp:8 * tpp + ng, :],
                        in_=ESX2[0:ng, :]).then_inc(*_ti(SX2D, 16))
        nst = 0
        for oc in range(NCH):
            for tt in range(2):
                kk = 2 * oc + tt
                _wge(e, FEV, kk + 1)
                e.dma_start(out=g1_hbm[oc, :, tt * 512:(tt + 1) * 512],
                            in_=GS3[kk % 2][:]).then_inc(*_ti(GST2[kk % 2], 16))
                nst += 1
        _wge(e, GST2[0], 16 * (nst // 2))
        _wge(e, GST2[1], 16 * (nst // 2))

    @blk.scalar
    def _(e):
        for oc in range(NCH):
            for tt in range(2):
                kk = 2 * oc + tt
                _wge(e, ADC, 64 * (NTP + 1))
                if kk >= 2:
                    _wge(e, FEV, kk - 1)
                e.copy(out=GS2[kk % 2][:],
                       in_=ACC[:, oc, tt * 512:(tt + 1) * 512]).then_inc(*_ti(FSL, 1))
                _wge(e, FSL, kk + 1)
                if kk >= 2:
                    _wge(e, GST2[kk % 2], 16 * (kk // 2))
                e.activation(out=GS3[kk % 2][:], in_=GS2[kk % 2][:],
                             func=AF.Silu).then_inc(*_ti(FEV, 1))
    sc.close()



def build_ffn1(tag="f1"):
    nc = bass.Bass("TRN2", target_bir_lowering=False, debug=False)
    xnT = nc.dram_tensor("xnT", [NCH, 128, TC], BF16, kind="ExternalInput").ap()
    gq = nc.dram_tensor("gq", [512, INTER], I32, kind="ExternalInput").ap()
    gs = nc.dram_tensor("gs", [32, INTER], F16, kind="ExternalInput").ap()
    gz = nc.dram_tensor("gz", [32, 1376], I32, kind="ExternalInput").ap()
    uq = nc.dram_tensor("uq", [512, INTER], I32, kind="ExternalInput").ap()
    us = nc.dram_tensor("us", [32, INTER], F16, kind="ExternalInput").ap()
    uz = nc.dram_tensor("uz", [32, 1376], I32, kind="ExternalInput").ap()
    dq = nc.dram_tensor("dq", [1376, D], I32, kind="ExternalInput").ap()
    ds = nc.dram_tensor("ds", [86, D], F16, kind="ExternalInput").ap()
    dz = nc.dram_tensor("dz", [86, 512], I32, kind="ExternalInput").ap()
    g1 = nc.dram_tensor("g1", [NCH, 128, TC], F16, kind="ExternalOutput").ap()
    with nc.Block() as blk:
        emit_ffn1(nc, blk, xnT, gq, gs, gz, uq, us, uz, dq, ds, dz, g1, tag)
    return nc


def emit_qkvkv(nc, blk, xnTf_hbm, w_hbm, b_hbm, kf_hbm, vf_hbm, tag):
    sc = _Scope(nc)
    IDS = _vs(nc, f"{tag}_IDS")
    idt = _ident(nc, blk, IDS, sc, tag)

    BS_ = _vs(nc, f"{tag}_BS")
    LW = [_vs(nc, f"{tag}_LW{i}") for i in range(2)]
    CVW = _vs(nc, f"{tag}_CVW")
    XLD = _vs(nc, f"{tag}_XLD")
    MM = _vs(nc, f"{tag}_MM")
    EVB = _vs(nc, f"{tag}_EVB")
    SQK = [_vs(nc, f"{tag}_SQK{i}", swdge=True) for i in range(2)]
    VEV = _vs(nc, f"{tag}_VEV")
    TRV = _vs(nc, f"{tag}_TRV")
    EV2 = _vs(nc, f"{tag}_EV2")
    VS = [_vs(nc, f"{tag}_VS{i}", swdge=True) for i in range(2)]

    BIAS = sc.sb(f"{tag}_bias", [128, 64], F32)
    WF = [sc.sb(f"{tag}_wf{i}", [128, 512], F32) for i in range(2)]
    WGB = [sc.sb(f"{tag}_wgb{i}", [128, NCH, 512], BF16) for i in range(2)]
    TAT = 2048
    XR = sc.sb(f"{tag}_xr", [128, NCH, TAT], BF16)
    STQ = [sc.sb(f"{tag}_stq{i}", [128, 512], BF16) for i in range(2)]
    VSTG = [sc.sb(f"{tag}_vstg{i}", [128, 512], BF16) for i in range(2)]
    VST2 = [sc.sb(f"{tag}_vst2{i}", [128, 4, 128], BF16) for i in range(2)]
    PS = [sc.ps(f"{tag}_ps{i}", [128, 512], F32) for i in range(8)]
    PTV = [PS[6][:].bitcast(BF16), PS[7][:].bitcast(BF16)]

    groups = [list(range(32 + g * 4, 32 + g * 4 + 4)) for g in range(8)]
    vstart = [64 + 3 * g for g in range(10)] + [94]
    groups += [list(range(st, min(st + 3, 96))) for st in vstart]
    NG = len(groups)

    def is_v(goc):
        return goc >= 64

    NTC = (B * S) // TAT
    NTL = TAT // 512
    NSPL = 4

    iters = [(tc, gi, tql) for tc in range(NTC) for gi in range(NG)
             for tql in range(NTL)]
    kev_after = []
    vev_after = []
    ke, ve = 0, 0
    for tc, gi, tql in iters:
        if is_v(groups[gi][0]):
            ve += len(groups[gi])
        else:
            ke += len(groups[gi])
        kev_after.append(ke)
        vev_after.append(ve)

    @blk.sync
    def _(e):
        for c in range(64):
            e.dma_start(out=BIAS[:, c:c + 1],
                        in_=b_hbm[(32 + c) * 128:(33 + c) * 128][:, None]
                        ).then_inc(*_ti(BS_, 16))
        kwl = 0
        for tc in range(NTC):
            for gi, g in enumerate(groups):
                c0, c1 = g[0] * 128, (g[-1] + 1) * 128
                for ic in range(NCH):
                    if kwl >= 2:
                        _wge(e, CVW, kwl - 1)
                    e.dma_start(out=WF[kwl % 2][:, 0:(c1 - c0)],
                                in_=w_hbm[ic * 128:(ic + 1) * 128, c0:c1]
                                ).then_inc(*_ti(LW[kwl % 2], 16))
                    kwl += 1

    @blk.gpsimd
    def _(e):
        def loadx(tc):
            if tc >= 1:
                _wge(e, MM, NCH * NG * NTL * tc)
            t0 = tc * TAT
            for sp in range(NSPL):
                nh = NCH // NSPL
                e.dma_start(out=XR[:, sp * nh:(sp + 1) * nh, :],
                            in_=xnTf_hbm[sp * nh:(sp + 1) * nh, :, t0:t0 + TAT]
                            .transpose([1, 0, 2])).then_inc(*_ti(XLD, 16))

        sq = 0
        ev2 = 0
        for q, (tc, gi, tql) in enumerate(iters):
            if gi == 0 and tql == 0:
                loadx(tc)
            g = groups[gi]
            t0 = tc * TAT + tql * 512
            b = t0 // S
            pos = t0 % S
            for j, goc in enumerate(g):
                if not is_v(goc):
                    h = goc - 32
                    _wge(e, EVB, sq + 1)
                    e.dma_start(out=kf_hbm[b, h, :, pos:pos + 512],
                                in_=STQ[sq % 2][:]
                                ).then_inc(*_ti(SQK[sq % 2], 16))
                    sq += 1
                else:
                    h = goc - 64
                    _wge(e, EV2, ev2 + 1)
                    dst = vf_hbm[b, h, pos:pos + 512, :]
                    e.dma_start(out=dst.rearrange("(j p) d -> p j d", j=4),
                                in_=VST2[ev2 % 2][:]
                                ).then_inc(*_ti(VS[ev2 % 2], 16))
                    ev2 += 1
        _wge(e, SQK[0], 16 * ((sq + 1) // 2))
        _wge(e, SQK[1], 16 * (sq // 2))
        _wge(e, VS[0], 16 * ((ev2 + 1) // 2))
        _wge(e, VS[1], 16 * (ev2 // 2))

    @blk.scalar
    def _(e):
        kwl = 0
        for tc in range(NTC):
            for gi, g in enumerate(groups):
                gg = tc * NG + gi
                w = len(g) * 128
                for ic in range(NCH):
                    _wge(e, LW[kwl % 2], 16 * (kwl // 2 + 1))
                    if gg >= 2:
                        _wge(e, MM, NCH * NTL * (gg - 1))
                    e.copy(out=WGB[gg % 2][:, ic, 0:w],
                           in_=WF[kwl % 2][:, 0:w]).then_inc(*_ti(CVW, 1))
                    kwl += 1

    @blk.tensor
    def _(e):
        _wge(e, IDS, 2)
        trv = 0
        for q, (tc, gi, tql) in enumerate(iters):
            g = groups[gi]
            gg = tc * NG + gi
            if gi == 0 and tql == 0:
                _wge(e, XLD, 16 * NSPL * (tc + 1))
                if tc >= 1:
                    _wge(e, EV2, trv)
            if tql == 0:
                _wge(e, CVW, NCH * (gg + 1))
            if q >= 2:
                _wge(e, EVB, kev_after[q - 2])
                _wge(e, VEV, vev_after[q - 2])
            for ic in range(NCH):
                for j, goc in enumerate(g):
                    ins = e.matmul(
                        PS[j * 2 + q % 2][:],
                        WGB[gg % 2][:, ic, j * 128:(j + 1) * 128],
                        XR[:, ic, tql * 512:(tql + 1) * 512],
                        start=(ic == 0), stop=(ic == NCH - 1))
                ins.then_inc(*_ti(MM, 1))
            if is_v(g[0]):
                for idx in range(len(g)):
                    vi = trv
                    _wge(e, VEV, vi + 1)
                    if vi >= 2:
                        _wge(e, EV2, vi - 1)
                    for j4 in range(4):
                        ins = e.transpose(
                            PTV[vi % 2][:, j4 * 128:(j4 + 1) * 128],
                            VSTG[vi % 2][:, j4 * 128:(j4 + 1) * 128],
                            idt[:])
                    ins.then_inc(*_ti(TRV, 1))
                    trv += 1

    @blk.vector
    def _(e):
        _wge(e, BS_, 16 * 64)
        sq = 0
        vev = 0
        ev2 = [0]

        def emit_evac2():
            vi = ev2[0]
            _wge(e, TRV, vi + 1)
            _wge(e, VS[vi % 2], 16 * (vi // 2))
            e.tensor_copy(out=VST2[vi % 2][:],
                          in_=PTV[vi % 2][:, 0:512]).then_inc(*_ti(EV2, 1))
            ev2[0] += 1

        for q, (tc, gi, tql) in enumerate(iters):
            g = groups[gi]
            _wge(e, MM, NCH * (q + 1))
            for j, goc in enumerate(g):
                if not is_v(goc):
                    _wge(e, SQK[sq % 2], 16 * (sq // 2))
                    e.tensor_scalar(
                        out=STQ[sq % 2][:], in0=PS[j * 2 + q % 2][:],
                        scalar1=BIAS[:, goc - 32:goc - 31], scalar2=None,
                        op0=ALU.add).then_inc(*_ti(EVB, 1))
                    sq += 1
                else:
                    _wge(e, TRV, max(0, vev - 1))
                    e.tensor_scalar(
                        out=VSTG[vev % 2][:], in0=PS[j * 2 + q % 2][:],
                        scalar1=BIAS[:, goc - 32:goc - 31], scalar2=None,
                        op0=ALU.add).then_inc(*_ti(VEV, 1))
                    vev += 1
                    if vev >= 3:
                        emit_evac2()
            if gi == NG - 1 and tql == NTL - 1:
                while ev2[0] < vev:
                    emit_evac2()
    sc.close()


def build_fused(nph=10):
    nc = bass.Bass("TRN2", target_bir_lowering=False, debug=False)
    BS_ = B * S
    xfull = nc.dram_tensor("xfull", [BS_, D], F32, kind="ExternalInput").ap()
    xs = nc.dram_tensor("xs", [TC, D], F32, kind="ExternalInput").ap()
    qkvw = nc.dram_tensor("qkvw", [D, 3 * D], F32, kind="ExternalInput").ap()
    qkvb = nc.dram_tensor("qkvb", [3 * D], F32, kind="ExternalInput").ap()
    ow = nc.dram_tensor("ow", [D, D], F32, kind="ExternalInput").ap()
    ob = nc.dram_tensor("ob", [D], F32, kind="ExternalInput").ap()
    w0 = nc.dram_tensor("w0", [D, D], F32, kind="ExternalInput").ap()
    b0 = nc.dram_tensor("b0", [D], F32, kind="ExternalInput").ap()
    gw0 = nc.dram_tensor("gw0", [8, D], F32, kind="ExternalInput").ap()
    ics = nc.dram_tensor("ics", [8, D], F32, kind="ExternalInput").ap()
    ocs = nc.dram_tensor("ocs", [8, D], F32, kind="ExternalInput").ap()
    wqkv1 = nc.dram_tensor("wqkv1", [6144, D], F32, kind="ExternalInput").ap()
    wproj1 = nc.dram_tensor("wproj1", [D, D], F32, kind="ExternalInput").ap()
    gq = nc.dram_tensor("gq", [512, INTER], I32, kind="ExternalInput").ap()
    gs = nc.dram_tensor("gs", [32, INTER], F16, kind="ExternalInput").ap()
    gz = nc.dram_tensor("gz", [32, 1376], I32, kind="ExternalInput").ap()
    uq = nc.dram_tensor("uq", [512, INTER], I32, kind="ExternalInput").ap()
    us = nc.dram_tensor("us", [32, INTER], F16, kind="ExternalInput").ap()
    uz = nc.dram_tensor("uz", [32, 1376], I32, kind="ExternalInput").ap()
    dq = nc.dram_tensor("dq", [1376, D], I32, kind="ExternalInput").ap()
    ds = nc.dram_tensor("ds", [86, D], F16, kind="ExternalInput").ap()
    dz = nc.dram_tensor("dz", [86, 512], I32, kind="ExternalInput").ap()
    out = nc.dram_tensor("out", [TC, D], F32, kind="ExternalOutput").ap()

    xnTf = nc.dram_tensor("i_xnTf", [NCH, 128, BS_], BF16, kind="Internal").ap()
    xnT0 = nc.dram_tensor("i_xnT0", [NCH, 128, TC], BF16, kind="Internal").ap()
    qscr = nc.dram_tensor("i_q", [NCH, 128, TC], BF16, kind="Internal").ap()
    kf = nc.dram_tensor("i_kf", [B, 32, 128, S], BF16, kind="Internal").ap()
    vf = nc.dram_tensor("i_vf", [B, 32, S, 128], BF16, kind="Internal").ap()
    gscr = nc.dram_tensor("i_g", [NCH, 128, TC], BF16, kind="Internal").ap()
    ctxs = nc.dram_tensor("i_ctx", [NCH, 128, TC], BF16, kind="Internal").ap()
    x1s = nc.dram_tensor("i_x1", [TC, D], F32, kind="Internal").ap()
    xn1s = nc.dram_tensor("i_xn1", [NCH, 128, TC], BF16, kind="Internal").ap()
    s1s = nc.dram_tensor("i_s1", [NCH, 128, TC], BF16, kind="Internal").ap()
    g1s = nc.dram_tensor("i_g1", [NCH, 128, TC], F16, kind="Internal").ap()

    _gbar_setup(nc)
    with nc.Block() as blk:
        def phase(fn):
            fn()
            emit_barrier(nc, blk)
            nc._vsem_pool.retire_phase()

        if nph == -2:
            phases = [
                lambda: emit_ln(nc, blk, xs, xnT0, False, "p1"),
                lambda: emit_ln(nc, blk, xs, xnT0, False, "p2b"),
            ]
            for fn in phases:
                phase(fn)
            nph = 0
        if nph == -5:
            def tiny():
                TS5 = _vs(nc, "t5")
                TT5 = nc.alloc_sbuf_tensor("t5t", [128, 64], F32)
                @blk.vector
                def _(e):
                    e.memset(TT5[:], 1.0).then_inc(*_ti(TS5, 1))
                @blk.gpsimd
                def _(e):
                    _wge(e, TS5, 1)
                    e.dma_start(out=xnT0[0, :, 0:64].transpose([0, 1]),
                                in_=TT5[:].bitcast(BF16)[:, 0:64]
                                ).then_inc(*_ti(TS5, 16))
                    _wge(e, TS5, 17)
            phase(tiny)
            phase(lambda: emit_ln(nc, blk, xfull, xnTf, False, "p2",
                                  ntok=BS_))
            nph = 0
        if nph == -6:
            emit_ln(nc, blk, xs, xnT0, False, "p1")
            emit_barrier(nc, blk)
            emit_ln(nc, blk, xfull, xnTf, False, "p2", ntok=BS_)
            emit_barrier(nc, blk)
            nph = 0
        if nph == -7:
            phase(lambda: emit_ln(nc, blk, xfull, xnTf, False, "p2",
                                  ntok=BS_))
            phase(lambda: emit_qkvkv(nc, blk, xnTf, qkvw, qkvb, kf, vf, "p4"))
            nph = 0
        if nph == -8:
            phase(lambda: emit_qkvkv(nc, blk, xnTf, qkvw, qkvb, kf, vf, "p4"))
            nph = 0
        if nph == -9:
            phase(lambda: emit_final(nc, blk, x1s, s1s, g1s, out, "p10"))
            nph = 0
        if nph == -10:
            phase(lambda: emit_ffn1(nc, blk, xn1s, gq, gs, gz, uq, us, uz,
                                    dq, ds, dz, g1s, "p9"))
            phase(lambda: emit_final(nc, blk, x1s, s1s, g1s, out, "p10"))
            nph = 0
        if nph == -3:
            phase(lambda: emit_ln(nc, blk, xfull, xnTf, False, "p2",
                                  ntok=BS_))
            nph = 0
        phases = [
            lambda: emit_ln(nc, blk, xs, xnT0, False, "p1"),
            lambda: emit_ln(nc, blk, xfull, xnTf, False, "p2", ntok=BS_),
            lambda: emit_qkv0(nc, blk, xnT0, qkvw, qkvb, qscr, None, None,
                              "p3", q_only=True),
            lambda: emit_qkvkv(nc, blk, xnTf, qkvw, qkvb, kf, vf, "p4"),
            lambda: emit_ffn0(nc, blk, xnT0, gw0, ics, ocs, w0, b0, gscr,
                              "p5"),
            lambda: emit_attn0(nc, blk, qscr, kf, vf, ctxs, "p6"),
            lambda: emit_combine0(nc, blk, ctxs, ow, ob, gscr, xs, x1s,
                                  xn1s, "p7"),
            lambda: emit_l1attn(nc, blk, xn1s, wqkv1, wproj1, s1s, "p8"),
            lambda: emit_ffn1(nc, blk, xn1s, gq, gs, gz, uq, us, uz,
                              dq, ds, dz, g1s, "p9"),
            lambda: emit_final(nc, blk, x1s, s1s, g1s, out, "p10"),
        ]
        for fn in phases[:nph]:
            phase(fn)
        if nph < 10:
            ZS = nc.alloc_semaphore("zf_s")
            ZD = nc.alloc_semaphore("zf_d")
            ztile = nc.alloc_sbuf_tensor("zf_t", [128, D], F32)

            @blk.vector
            def _(e):
                e.memset(ztile[:], 0.0).then_inc(ZS, 1)

            @blk.gpsimd
            def _(e):
                e.wait_ge(ZS, 1)
                for tt in range(TC // 128):
                    e.dma_start(out=out[tt * 128:(tt + 1) * 128, :],
                                in_=ztile[:]).then_inc(ZD, 16)
                e.wait_ge(ZD, 16 * (TC // 128))
            nc.all_engine_barrier()
    return nc



_nc_cache = {}


def _get_nc(key, builder):
    if key not in _nc_cache:
        _nc_cache[key] = builder()
    return _nc_cache[key]


def _run(key, builder, in_maps):
    nc = _get_nc(key, builder)
    res = run_bass_kernel_spmd(nc, in_maps, core_ids=list(range(NCORES)))
    return res.results


def kernel(L0_attn_x, L0_attn_attn_qkvw, L0_attn_attn_qkvb, L0_attn_attn_ow,
           L0_attn_attn_ob, L0_ffn_weight, L0_ffn_bias, L0_ffn_gate_w,
           L0_ffn_in_channel_scale, L0_ffn_out_channel_scale,
           L1_attn_weight_qkv, L1_attn_weight_proj,
           L1_ffn_gate_qweight, L1_ffn_gate_scales, L1_ffn_gate_zeros,
           L1_ffn_up_qweight, L1_ffn_up_scales, L1_ffn_up_zeros,
           L1_ffn_down_qweight, L1_ffn_down_scales, L1_ffn_down_zeros):
    f32 = lambda a: np.ascontiguousarray(np.asarray(a), dtype=np.float32)
    i32 = lambda a: np.ascontiguousarray(np.asarray(a), dtype=np.int32)
    f16 = lambda a: np.ascontiguousarray(np.asarray(a), dtype=np.float16)

    x = f32(L0_attn_x)
    xfull = np.ascontiguousarray(x.reshape(B * S, D))
    xs = [np.ascontiguousarray(
        x[:, c * PC:(c + 1) * PC, :].reshape(TC, D)) for c in range(NCORES)]

    common = {
        "xfull": xfull,
        "qkvw": f32(L0_attn_attn_qkvw), "qkvb": f32(L0_attn_attn_qkvb),
        "ow": f32(L0_attn_attn_ow), "ob": f32(L0_attn_attn_ob),
        "w0": f32(L0_ffn_weight), "b0": f32(L0_ffn_bias),
        "gw0": f32(L0_ffn_gate_w), "ics": f32(L0_ffn_in_channel_scale),
        "ocs": f32(L0_ffn_out_channel_scale),
        "wqkv1": f32(L1_attn_weight_qkv), "wproj1": f32(L1_attn_weight_proj),
        "gq": i32(L1_ffn_gate_qweight), "gs": f16(L1_ffn_gate_scales),
        "gz": i32(L1_ffn_gate_zeros),
        "uq": i32(L1_ffn_up_qweight), "us": f16(L1_ffn_up_scales),
        "uz": i32(L1_ffn_up_zeros),
        "dq": i32(L1_ffn_down_qweight), "ds": f16(L1_ffn_down_scales),
        "dz": i32(L1_ffn_down_zeros),
    }
    in_maps = [dict(common, xs=xs[c]) for c in range(NCORES)]

    nc = _get_nc("fused", build_fused)
    res = run_bass_kernel_spmd(nc, in_maps, core_ids=list(range(NCORES)))
    out = np.empty((B, S, D), np.float32)
    for c in range(NCORES):
        out[:, c * PC:(c + 1) * PC, :] = np.asarray(
            res.results[c]["out"]).reshape(B, PC, D)
    return out
